# version 25
# baseline (speedup 1.0000x reference)
import math
from contextlib import ExitStack

import numpy as np
import ml_dtypes

import concourse.bass as bass
import concourse.mybir as mybir
from concourse.bass_utils import run_bass_kernel_spmd
from concourse.aot_env import allow_ici_all_to_all

F32 = mybir.dt.float32
BF16 = mybir.dt.bfloat16
AF = mybir.ActivationFunctionType
ALU = mybir.AluOpType
AX = mybir.AxisListType

NCORE = 8
S = 8192
D = 2048
T = 1024
KC = 16
NB = 64
FC = 64
ISD = 1.0 / math.sqrt(128.0)
NEG_CAUSAL = -1.0e6
NEG_PEN = -113152.0

ENGINES = ("sync", "gpsimd", "tensor", "vector", "scalar")
OWN_COMPUTE = {"tensor": "TT", "vector": "VV", "scalar": "SC"}
VSTRIDE = 129


class Builder:
    def __init__(self, debug_taps=()):
        self.nc = bass.Bass("TRN2", target_bir_lowering=False)
        self.ops = []
        self.cnt = {}
        self.pending = {}
        self.ctx = ExitStack()
        self.sems = {}
        self.debug_taps = set(debug_taps)
        self.slot_pv = [0, 0, 0]
        self.bank_exp = [0, 0]
        self.pv_free = [0, 0, 0, 0]
        self.g_mi = 0
        self.gidx = 0
        self.rb_free = {}

    def c(self, sem):
        return self.cnt.get(sem, 0)

    def add(self, eng, fn, waits=(), inc=None, n=1):
        w = list(self.pending.pop(eng, [])) + [(s, v) for (s, v) in waits if v > 0]
        self.ops.append((eng, fn, w, inc, n))
        if inc is not None:
            self.cnt[inc] = self.cnt.get(inc, 0) + n
        return self.cnt.get(inc, 0) if inc else None

    def dma(self, eng, fn, waits=(), sem="LD"):
        return self.add(eng, fn, waits, inc=sem, n=16)

    def barrier(self):
        snap = [(s, v) for (s, v) in self.cnt.items()
                if v > 0 and not s.startswith("CC")]
        for e in ENGINES:
            own = OWN_COMPUTE.get(e)
            self.pending[e] = [(s, v) for (s, v) in snap if s != own]

    def sb(self, name, shape, dt, stack=None, side=None):
        return (stack or self.ctx).enter_context(
            self.nc.sbuf_tensor(name, shape, dt, side=side))

    def tap(self, name, ap_src, shape, dt):
        if name not in self.debug_taps:
            return
        dst = self.nc.dram_tensor("dbg_" + name, shape, dt, kind="ExternalOutput").ap()
        waits = [(s, v) for (s, v) in self.cnt.items() if v > 0]
        self.dma("sync", lambda e: e.dma_start(out=dst, in_=ap_src), waits=waits)

    def emit_and_finalize(self):
        block = self.ctx.enter_context(self.nc.Block())
        sems, ops = self.sems, self.ops

        def engine_body(en):
            def f(eng):
                for (oen, fn, waits, inc, n) in ops:
                    if oen != en:
                        continue
                    for (s, v) in waits:
                        eng.wait_ge(sems[s], v)
                    inst = fn(eng)
                    if inc is not None:
                        inst.then_inc(sems[inc], n)
            return f

        block.sync(engine_body("sync"))
        block.gpsimd(engine_body("gpsimd"))
        block.tensor(engine_body("tensor"))
        block.vector(engine_body("vector"))
        block.scalar(engine_body("scalar"))

    def build(self):
        nc = self.nc
        P = self
        EI = lambda name, shape, dt=F32: nc.dram_tensor(name, shape, dt, kind="ExternalInput").ap()
        EO = lambda name, shape, dt=F32: nc.dram_tensor(name, shape, dt, kind="ExternalOutput").ap()
        IN = lambda name, shape, dt: nc.dram_tensor(name, shape, dt).ap()
        SH = lambda name, shape, dt: nc.dram_tensor(
            name, shape, dt, addr_space="Shared").ap()

        d_xT = EI("xT", [KC, 128, T])
        d_cossin = EI("cossin", [128, S])
        d_mwqkv = EI("mwqkv", [KC, 4, 128, 128])
        d_wot = EI("wot", [KC, KC, 128, 128])
        d_w2qkv = EI("w2qkv", [KC, 6, 128, 128])
        d_wo2s = EI("wo2s", [2, KC, 128, 128])
        d_a2s = EI("a2s", [2, 128, 16])
        d_afull = EI("afull", [KC, 128, 16])
        d_bmat = EI("bmat", [FC, 16, 128])
        d_lnw = EI("lnw", [128, 2 * KC])
        d_win = EI("winT", [FC, KC, 128, 128])
        d_wup = EI("wupT", [FC, KC, 128, 128])
        d_wdn = EI("wdnT", [KC, FC, 128, 128])
        d_consts = EI("consts", [128, 4])
        d_crow = EI("crow", [1, 256])
        d_ident = EI("ident", [128, 128])
        d_tril = EI("tril01", [128, 128])
        d_trilneg = EI("trilneg", [128, 128])
        d_out = EO("out", [KC, 128, T])

        d_ag1i = [IN("ag1_in%d" % i, [KC, 128, 512], BF16) for i in range(2)]
        d_ag1o = [SH("ag1_out%d" % i, [NCORE, KC, 128, 512], BF16) for i in range(2)]
        d_ag2i = [IN("ag2_in%d" % i, [KC, 128, 512], BF16) for i in range(2)]
        d_ag2o = [SH("ag2_out%d" % i, [NCORE, KC, 128, 512], BF16) for i in range(2)]
        d_a2a1i = [IN("a2a1_in%d" % i, [NCORE, 128, T], BF16) for i in range(2)]
        d_a2a1o = [IN("a2a1_out%d" % i, [NCORE, 128, T], BF16) for i in range(2)]
        d_a2a2i = [IN("a2a2_in%d" % i, [NCORE, 128, T], BF16) for i in range(2)]
        d_a2a2o = [IN("a2a2_out%d" % i, [NCORE, 128, T], BF16) for i in range(2)]
        d_wotai = IN("wota_in", [KC, 128, 16], F32)
        d_wotao = SH("wota_out", [KC, 128, 16], F32)
        d_x1sp = IN("x1_sp", [KC, 128, T], F32)
        d_hssp = IN("hs_sp", [KC, 128, T], F32)

        for s in ("LD", "WD", "TT", "VV", "SC", "CC1a", "CC1b", "CC1c", "CC1d",
                  "CC2a", "CC2b",
                  "CC3a", "CC3b", "CC3c", "CC3d", "CC4a", "CC4b", "CC5",
                  "Q0", "Q1", "Q2", "Q3", "Q4", "Q5", "Q6", "Q7", "Q8", "Q9",
                  "R0", "R1", "R2", "R3", "R4", "R5", "R6", "R7", "WI",
                  "U0", "U1", "U2", "U3",
                  "YA", "YB", "SG1", "SG2", "SG3", "SG4", "SP", "SP2", "WT", "XB"):
            self.sems[s] = self.ctx.enter_context(nc.semaphore(s))

        csb = self.sb("csb", [128, 4], F32)
        csb16 = self.sb("csb16", [128, 4], BF16)
        crow16 = self.sb("crow16", [1, 256], BF16)
        crow32 = self.sb("crow32", [1, 256], F32)
        id16 = self.sb("id16", [128, 128], BF16)
        tril_sb = self.sb("tril_sb", [128, 128], F32)
        trilneg_sb = self.sb("trilneg_sb", [128, 128], F32)
        lnw_sb = self.sb("lnw_sb", [128, 2 * KC], F32)
        tmp32 = self.sb("tmp32", [128, 1024], F32)
        wota16 = self.sb("wota16", [128, KC * 16], BF16)
        a16 = self.sb("a16", [128, KC * 16], BF16)
        modhA = self.sb("modhA", [16, T], BF16)
        m1_32 = self.sb("m1_32", [16, T], F32)
        grow4 = self.sb("grow4", [1, 512], BF16)
        gt32_4 = self.sb("gt32_4", [64, 512], F32)
        cmp4 = self.sb("cmp4", [64, 512], BF16)
        msk_sb = self.sb("msk_sb", [128, 128], F32)
        rc_sb = self.sb("rc_sb", [128, 4], F32)
        otm16 = self.sb("otm16", [128, 512], BF16)

        scstack = ExitStack()
        strow = self.sb("strow", [1, 4 * T], F32, scstack, side="right")
        sc16a = self.sb("sc16a", [128, T], BF16, scstack, side="right")
        sc16b = self.sb("sc16b", [128, T], BF16, scstack, side="right")
        sq16a = self.sb("sq16a", [128, T], BF16, scstack, side="right")
        sq16b = self.sb("sq16b", [128, T], BF16, scstack, side="right")
        mwstack = ExitStack()
        mw16 = self.sb("mw16", [128, KC * 4 * 128], BF16, mwstack, side="right")
        xstack1 = ExitStack()
        xbuf = self.sb("xbuf", [128, KC * T], F32, xstack1, side="right")

        scU = self.ctx.enter_context(nc.psum_tensor("scU", [128, 1024], F32))
        scV = self.ctx.enter_context(nc.psum_tensor("scV", [128, 1024], F32))
        psE = self.ctx.enter_context(nc.psum_tensor("psE", [128, 512], F32))
        psF = self.ctx.enter_context(nc.psum_tensor("psF", [128, 512], F32))
        psG = self.ctx.enter_context(nc.psum_tensor("psG", [128, 512], F32))
        psT = self.ctx.enter_context(nc.psum_tensor("psT", [128, 1024], BF16))

        class PsView:
            def __init__(self, t, off):
                self.t, self.off = t, off

            def __getitem__(self, idx):
                p, c = idx
                return self.t[p, slice(c.start + self.off, c.stop + self.off)]

        psA = PsView(scU, 0)
        psB = PsView(scU, 512)
        psC = PsView(scV, 0)
        psD = PsView(scV, 512)
        PS = [psA, psB, psC, psD, psE, psF]

        P.dma("sync", lambda e: e.dma_start(out=csb[:, :], in_=d_consts))
        P.dma("sync", lambda e: e.dma_start(out=crow32[:, :], in_=d_crow))
        P.dma("sync", lambda e: e.dma_start(out=tril_sb[:, :], in_=d_tril))
        P.dma("sync", lambda e: e.dma_start(out=trilneg_sb[:, :], in_=d_trilneg))
        P.dma("sync", lambda e: e.dma_start(out=lnw_sb[:, :], in_=d_lnw))
        P.dma("sync", lambda e: e.dma_start(
            out=xbuf[:, :].rearrange("p (kc t) -> p kc t", kc=KC),
            in_=d_xT.rearrange("kc p t -> p kc t")))
        P.dma("gpsimd", lambda e: e.dma_start(out=csb16[:, :], in_=d_consts), sem="WD")
        P.dma("gpsimd", lambda e: e.dma_start(out=crow16[:, :], in_=d_crow), sem="WD")
        P.dma("gpsimd", lambda e: e.dma_start(out=id16[:, :], in_=d_ident), sem="WD")
        P.dma("gpsimd", lambda e: e.dma_start(
            out=a16[:, :].rearrange("p (dc r) -> p dc r", dc=KC),
            in_=d_afull.rearrange("dc p r -> p dc r")), sem="WD")
        ld0 = P.c("LD")

        wostack = ExitStack()
        wo2sb = self.sb("wo2sb", [128, 2 * KC * 128], BF16, wostack)
        a2sb = self.sb("a2sb", [128, 2 * 16], BF16, wostack)
        wotp = self.sb("wotp", [128, KC * 16], F32, wostack)
        P.dma("gpsimd", lambda e: e.dma_start(
            out=wo2sb[:, :].rearrange("p (e dc c) -> p e dc c", e=2, dc=KC),
            in_=d_wo2s.rearrange("e dc p c -> p e dc c")), sem="WD")
        P.dma("gpsimd", lambda e: e.dma_start(
            out=a2sb[:, :].rearrange("p (e r) -> p e r", e=2),
            in_=d_a2s.rearrange("e p r -> p e r")), sem="WD")
        P.dma("gpsimd", lambda e: e.dma_start(
            out=mw16[:, :].rearrange("p (kc m c) -> p kc m c", kc=KC, m=4),
            in_=d_mwqkv.rearrange("kc m p c -> p kc m c")), sem="WD")
        wd0 = P.c("WD")
        wd_mw = wd0
        wd_wo2 = wd0
        v_prev = 0
        for dc in range(KC):
            for e2 in range(2):
                P.add("tensor", lambda e, dc=dc, e2=e2: e.matmul(
                    psG[0:128, 0:16],
                    wo2sb[:, (e2 * KC + dc) * 128:(e2 * KC + dc + 1) * 128],
                    a2sb[:, e2 * 16:(e2 + 1) * 16],
                    start=(e2 == 0), stop=(e2 == 1)),
                    ([("WD", wd_wo2), ("VV", v_prev)] if e2 == 0 else []),
                    inc="TT" if e2 == 1 else None)
            tmark = P.c("TT")
            P.add("vector", lambda e, dc=dc: e.tensor_copy(
                wotp[:, dc * 16:(dc + 1) * 16], psG[:, 0:16]), [("TT", tmark)], inc="VV")
            v_prev = P.c("VV")
        P.dma("sync", lambda e: e.dma_start(
            out=d_wotai.rearrange("dc p r -> p dc r"),
            in_=wotp[:, :].rearrange("p (dc r) -> p dc r", dc=KC)),
            waits=[("VV", v_prev)], sem="WI")
        ld_wotai = P.c("WI")
        wostack.close()

        def emit_stats(src, *, with_mean, eps_col, prereq):
            sc16 = (sc16a, sc16b)
            sq16 = (sq16a, sq16b)
            marks = []
            for kc in range(KC):
                w = list(prereq) if kc < 2 else [("TT", marks[kc - 2])]
                cm = P.add("vector", lambda e, kc=kc: e.tensor_copy(
                    sc16[kc % 2][:, :], src[:, kc * T:(kc + 1) * T]), w, inc="VV")
                sm = P.add("vector", lambda e, kc=kc: e.tensor_tensor(
                    sq16[kc % 2][:, :], sc16[kc % 2][:, :], sc16[kc % 2][:, :], ALU.mult), [], inc="VV")
                for h in range(2):
                    if with_mean:
                        P.add("tensor", lambda e, kc=kc, h=h: e.matmul(
                            psA[(32 * h):(32 * h + 1), 0:512], csb16[:, 0:1],
                            sc16[kc % 2][:, h * 512:(h + 1) * 512],
                            start=(kc == 0), stop=(kc == KC - 1)),
                            [("VV", cm)] if h == 0 else [])
                    P.add("tensor", lambda e, kc=kc, h=h: e.matmul(
                        psB[(32 * h):(32 * h + 1), 0:512], csb16[:, 0:1],
                        sq16[kc % 2][:, h * 512:(h + 1) * 512],
                        start=(kc == 0), stop=(kc == KC - 1)),
                        [("VV", sm)] if h == 0 else [],
                        inc="TT" if h == 1 else None)
                marks.append(P.c("TT"))
            t_sums = P.c("TT")
            if with_mean:
                P.add("vector", lambda e: e.tensor_copy(strow[0:1, 0:512], psA[0:1, 0:512]),
                      [("TT", t_sums)], inc="VV")
                P.add("vector", lambda e: e.tensor_copy(strow[0:1, 512:1024], psA[32:33, 0:512]), [], inc="VV")
            P.add("vector", lambda e: e.tensor_copy(strow[0:1, T:T + 512], psB[0:1, 0:512]),
                  [("TT", t_sums)], inc="VV")
            P.add("vector", lambda e: e.tensor_copy(strow[0:1, T + 512:2 * T], psB[32:33, 0:512]), [], inc="VV")
            if with_mean:
                P.add("vector", lambda e: e.tensor_tensor(
                    strow[0:1, 2 * T:3 * T], strow[0:1, 0:T], strow[0:1, 0:T], ALU.mult), [], inc="VV")
                P.add("vector", lambda e: e.tensor_tensor(
                    strow[0:1, T:2 * T], strow[0:1, T:2 * T], strow[0:1, 2 * T:3 * T], ALU.subtract), [], inc="VV")
            v_var = P.c("VV")
            P.add("scalar", lambda e: e.activation(
                strow[0:1, 2 * T:3 * T], strow[0:1, T:2 * T], AF.Ln,
                bias=csb[0:1, eps_col:eps_col + 1]),
                [("VV", v_var)], inc="SC")
            P.add("scalar", lambda e: e.activation(
                strow[0:1, 3 * T:4 * T], strow[0:1, 2 * T:3 * T], AF.Exp, scale=-0.5), [], inc="SC")
            s_rstd = P.c("SC")
            if with_mean:
                P.add("vector", lambda e: e.tensor_tensor(
                    strow[0:1, 2 * T:3 * T], strow[0:1, 0:T], strow[0:1, 3 * T:4 * T], ALU.mult),
                    [("SC", s_rstd)], inc="VV")
            return P.c("VV"), s_rstd

        def emit_bcasts(*, with_mean, s_rstd, v_a):
            P.add("tensor", lambda e: e.matmul(psE[:, 0:512], crow32[0:1, 0:128],
                                               strow[0:1, 3 * T:3 * T + 512], start=True, stop=True),
                  [("SC", s_rstd)])
            P.add("tensor", lambda e: e.matmul(psF[:, 0:512], crow32[0:1, 0:128],
                                               strow[0:1, 3 * T + 512:4 * T], start=True, stop=True),
                  [], inc="TT")
            if with_mean:
                P.add("tensor", lambda e: e.matmul(psC[:, 0:512], crow32[0:1, 0:128],
                                                   strow[0:1, 2 * T:2 * T + 512], start=True, stop=True),
                      [("VV", v_a)])
                P.add("tensor", lambda e: e.matmul(psD[:, 0:512], crow32[0:1, 0:128],
                                                   strow[0:1, 2 * T + 512:3 * T], start=True, stop=True),
                      [], inc="TT")
            return P.c("TT")

        v0, s0 = emit_stats(xbuf, with_mean=True, eps_col=2, prereq=[("LD", ld0), ("WD", wd0)])
        t_bc0 = emit_bcasts(with_mean=True, s_rstd=s0, v_a=v0)
        for h in range(2):
            rb = psE if h == 0 else psF
            ab = psC if h == 0 else psD
            for kc in range(KC):
                dst = (sc16a, sq16a)[kc % 2]
                qs = "R%d" % (2 * h + kc % 2)
                w = [("TT", t_bc0)]
                if h == 1 and kc < 2:
                    pq = "R%d" % (kc % 2)
                    w.append((pq, P.c(pq)))
                elif kc >= 2:
                    w.append((qs, P.c(qs)))
                P.add("vector", lambda e, kc=kc, h=h, rb=rb: e.tensor_tensor(
                    tmp32[:, 0:512], xbuf[:, kc * T + h * 512: kc * T + (h + 1) * 512],
                    rb[:, 0:512], ALU.mult), w, inc="VV")
                P.add("vector", lambda e, dst=dst, ab=ab: e.tensor_tensor(
                    dst[:, 0:512], tmp32[:, 0:512], ab[:, 0:512], ALU.subtract),
                    [], inc="VV")
                vm = P.c("VV")
                P.dma("sync", lambda e, kc=kc, h=h, dst=dst: e.dma_start(
                    out=d_ag1i[h][kc], in_=dst[:, 0:512]), waits=[("VV", vm)], sem=qs)
            ra, rb_ = "R%d" % (2 * h), "R%d" % (2 * h + 1)
            P.add("gpsimd", lambda e, h=h: e.collective_compute(
                "AllGather", ALU.bypass, replica_groups=[list(range(NCORE))],
                ins=[d_ag1i[h].opt()], outs=[d_ag1o[h].opt()]),
                [(ra, P.c(ra)), (rb_, P.c(rb_))], inc="CC1" + "ab"[h])
        P.add("gpsimd", lambda e: e.collective_compute(
            "AllReduce", ALU.add, replica_groups=[list(range(NCORE))],
            ins=[d_wotai.opt()], outs=[d_wotao.opt()]), [("WI", ld_wotai)], inc="CC5")
        xstack1.close()

        P.barrier()
        mstackA = ExitStack()
        aQ0 = self.sb("aQ0", [128, S], BF16, mstackA)
        aQ1 = self.sb("aQ1", [128, S], BF16, mstackA)
        aK = self.sb("aK", [128, S], BF16, mstackA)
        aV = self.sb("aV", [128, NB * VSTRIDE + 64], BF16, mstackA)
        means16 = self.sb("means16", [128, 64], BF16, mstackA)
        m3stack = ExitStack()
        lnst = [self.sb("lnst%d" % i, [128, KC * 512], BF16, m3stack) for i in range(2)]
        cosb = self.sb("cosb", [64, S], BF16, m3stack)
        sinb = self.sb("sinb", [64, S], BF16, m3stack)
        rtmp = self.sb("rtmp", [64, 4 * 512], BF16, m3stack)
        rblo = [self.sb("rblo%d" % i, [64, 512], BF16, m3stack) for i in range(2)]
        rbhi = [self.sb("rbhi%d" % i, [64, 512], BF16, m3stack) for i in range(2)]
        vtmp = self.sb("vtmp", [128, 512], BF16, m3stack)
        P.add("vector", lambda e: e.memset(
            aV[:, :NB * VSTRIDE].rearrange("p (n c) -> p n c", c=VSTRIDE)[:, :, 128:129], 1.0),
            [], inc="VV")

        v_tr_prev = [0]
        t_tr_prev = [0]

        def emit_qkv_produce(nm, d_src, cc_sem, wsb, wd_w, nroles, on_evac, strbufs):
            ev = {}
            ld_marks = {}
            tt_roles = {}
            tc_list = list(range(0, 16, 2)) + list(range(1, 16, 2))
            for idx, tc in enumerate(tc_list):
                rk, hf = tc // 2, tc % 2
                slot = idx % 2
                qj = "Q%d" % (2 + slot)
                if idx < 2:
                    w = [(cc_sem[hf], 1)]
                else:
                    w = [("TT", tt_roles.get((tc_list[idx - 2], nroles - 1), 0))]
                    if idx in (8, 9):
                        w.append((cc_sem[1], 1))
                P.dma("sync", lambda e, rk=rk, hf=hf, slot=slot: e.dma_start(
                    out=strbufs[slot][:, :].rearrange("p (kc t) -> p kc t", kc=KC),
                    in_=d_src[hf][rk].rearrange("kc p t -> p kc t")),
                    waits=w, sem=qj)
                ld_marks[tc] = (qj, P.c(qj))
                for m in range(nroles):
                    si = idx * nroles + m
                    bank = PS[si % 6]
                    prev = si - 6
                    wv = []
                    if prev >= 0:
                        wv.append(ev[(tc_list[prev // nroles], prev % nroles)])
                    for kc in range(KC):
                        if kc == 0:
                            w_mm = [ld_marks[tc], ("WD", wd_w)] + wv
                        else:
                            w_mm = []
                        P.add("tensor", lambda e, m=m, kc=kc, slot=slot, bank=bank: e.matmul(
                            bank[:, 0:512],
                            wsb[:, (kc * nroles + m) * 128:(kc * nroles + m + 1) * 128],
                            strbufs[slot][:, kc * 512:(kc + 1) * 512],
                            start=(kc == 0), stop=(kc == KC - 1)),
                            w_mm, inc="TT" if kc == KC - 1 else None)
                    tt_roles[(tc, m)] = P.c("TT")
                    ev[(tc, m)] = on_evac(tc, m, bank, tt_roles[(tc, m)])
            return ev

        def mx_evac(tc, m, bank, tmark):
            cols = slice(tc * 512, (tc + 1) * 512)
            if m < 3:
                dst = (aQ0, aQ1, aK)[m]
                cs = slice(tc * 512, (tc + 1) * 512)
                rl, rh = rblo[m % 2], rbhi[m % 2]
                P.add("scalar", lambda e, bank=bank, rl=rl: e.activation(
                    rl[:, :], bank[0:64, 0:512], AF.Copy),
                    [("TT", tmark), ld_cs, ("VV", self.rb_free.get(m % 2, 0))], inc="SC")
                P.add("scalar", lambda e, bank=bank, rh=rh: e.activation(
                    rh[:, :], bank[64:128, 0:512], AF.Copy), [], inc="SC")
                scm = P.c("SC")
                P.add("vector", lambda e, rl=rl, cs=cs: e.tensor_tensor(
                    rtmp[:, 0:512], rl[:, :], cosb[:, cs], ALU.mult),
                    [("SC", scm)], inc="VV")
                P.add("vector", lambda e, rh=rh, cs=cs: e.tensor_tensor(
                    rtmp[:, 512:1024], rh[:, :], sinb[:, cs], ALU.mult), [], inc="VV")
                P.add("vector", lambda e, rl=rl, cs=cs: e.tensor_tensor(
                    rtmp[:, 1024:1536], rl[:, :], sinb[:, cs], ALU.mult), [], inc="VV")
                P.add("vector", lambda e, rh=rh, cs=cs: e.tensor_tensor(
                    rtmp[:, 1536:2048], rh[:, :], cosb[:, cs], ALU.mult), [], inc="VV")
                self.rb_free[m % 2] = P.c("VV")
                P.add("vector", lambda e, dst=dst, cols=cols: e.tensor_tensor(
                    dst[0:64, cols], rtmp[:, 0:512], rtmp[:, 512:1024], ALU.subtract), [], inc="VV")
                P.add("vector", lambda e, dst=dst, cols=cols: e.tensor_tensor(
                    dst[64:128, cols], rtmp[:, 1024:1536], rtmp[:, 1536:2048], ALU.add), [], inc="VV")
                return ("VV", P.c("VV"))
            P.add("vector", lambda e, bank=bank: e.tensor_copy(vtmp[:, :], bank[:, 0:512]),
                  [("TT", tmark), ("TT", t_tr_prev[0])], inc="VV")
            vm = P.c("VV")
            for j in range(4):
                P.add("tensor", lambda e, j=j: e.transpose(
                    psT[:, j * 128:(j + 1) * 128], vtmp[:, j * 128:(j + 1) * 128], id16[:, :]),
                    [("VV", vm), ("VV", v_tr_prev[0])] if j == 0 else [],
                    inc="TT" if j == 3 else None)
            ttm = P.c("TT")
            t_tr_prev[0] = ttm
            for j in range(4):
                bidx = 4 * tc + j
                P.add("vector", lambda e, j=j, bidx=bidx: e.tensor_copy(
                    aV[:, bidx * VSTRIDE:bidx * VSTRIDE + 128], psT[:, j * 128:(j + 1) * 128]),
                    [("TT", ttm)] if j == 0 else [], inc="VV" if j == 3 else None)
            v_tr_prev[0] = P.c("VV")
            return ("VV", v_tr_prev[0])

        P.dma("gpsimd", lambda e: e.dma_start(out=cosb[:, :], in_=d_cossin[0:64, :]), sem="WD")
        P.dma("gpsimd", lambda e: e.dma_start(out=sinb[:, :], in_=d_cossin[64:128, :]), sem="WD")
        ld_cs = ("WD", P.c("WD"))
        ev_mx = emit_qkv_produce("mx", d_ag1o, ("CC1a", "CC1b"), mw16, wd_mw, 4, mx_evac, lnst)

        def _means_op(e):
            with nc.allow_low_precision(reason="gating block sums feed comparisons only"):
                return e.reduce_sum(
                    means16[:, :].rearrange("p (n o) -> p n o", o=1),
                    aK[:, :].rearrange("p (n w) -> p n w", w=128), axis=AX.X)
        P.add("vector", _means_op, [ev_mx[(15, 2)]], inc="VV")
        P.tap("q0", aQ0[:, 0:1024], [128, 1024], BF16)
        P.tap("k", aK[:, 0:1024], [128, 1024], BF16)
        m3stack.close()
        mwstack.close()

        SCB2 = [scU, scV]

        def pvloc(i):
            return (psE if i < 2 else psF), (0 if i % 2 == 0 else 256)

        def emit_pen_precompute(Qs, means, penbuf):
            m_s3 = 0
            m_s4 = 0
            m_s6 = 0
            m_s8 = 0
            for tb in range(14):
                t0 = 8 + 4 * tb
                w1 = [("VV", m_s3), ("VV", m_s8)]
                w2 = [("VV", m_s4)]
                if tb == 0:
                    w1 += [("VV", self.g_mi)]
                    w2 += [("VV", self.pv_free[0]), ("VV", self.pv_free[1])]
                for i in range(4):
                    P.add("tensor", lambda e, t=t0 + i, i=i: e.matmul(
                        psG[0:1, i * 128:(i + 1) * 128], means[:, t:t + 1],
                        Qs[:, t * 128:(t + 1) * 128], start=True, stop=True),
                        w1 if i == 0 else [], inc=None)
                for i in range(4):
                    P.add("tensor", lambda e, t=t0 + i, i=i: e.matmul(
                        psE[0:64, i * 128:(i + 1) * 128], means[:, 0:64],
                        Qs[:, t * 128:(t + 1) * 128], start=True, stop=True),
                        w2 if i == 0 else [],
                        inc="TT" if i == 3 else None)
                m_s12 = P.c("TT")
                P.add("vector", lambda e: e.tensor_copy(grow4[0:1, :], psG[0:1, 0:512]),
                      [("TT", m_s12)], inc="VV")
                m_s3 = P.c("VV")
                P.add("vector", lambda e: e.tensor_copy(gt32_4[:, :], psE[0:64, 0:512]),
                      [], inc="VV")
                m_s4 = P.c("VV")
                w5 = [("VV", m_s4), ("VV", m_s6)]
                if tb == 0:
                    w5 += [("VV", self.pv_free[2]), ("VV", self.pv_free[3])]
                for i in range(4):
                    P.add("tensor", lambda e, i=i: e.matmul(
                        psF[0:64, i * 128:(i + 1) * 128], crow16[0:1, 0:64],
                        grow4[0:1, i * 128:(i + 1) * 128], start=True, stop=True),
                        w5 if i == 0 else [],
                        inc="TT" if i == 3 else None)
                m_s5 = P.c("TT")
                P.add("vector", lambda e: e.tensor_tensor(
                    cmp4[:, :], gt32_4[:, :], psF[0:64, 0:512], ALU.is_ge),
                    [("TT", m_s5)], inc="VV")
                m_s6 = P.c("VV")
                for i in range(4):
                    P.add("tensor", lambda e, t=t0 + i, i=i: e.matmul(
                        psG[0:1, i * 128:(i + 1) * 128], csb16[0:t + 1, 1:2],
                        cmp4[0:t + 1, i * 128:(i + 1) * 128], start=True, stop=True),
                        [("VV", m_s6)] if i == 0 else [],
                        inc="TT" if i == 3 else None)
                m_s7 = P.c("TT")
                P.add("vector", lambda e, t0=t0: e.tensor_scalar(
                    penbuf[0:1, (t0 - 8) * 128:(t0 - 4) * 128], psG[0:1, 0:512],
                    8.5, NEG_PEN, ALU.is_gt, ALU.mult), [("TT", m_s7)], inc="VV")
                m_s8 = P.c("VV")
            m_mi = 0
            for tb in range(14):
                for i in range(4):
                    t = 8 + 4 * tb + i
                    P.add("tensor", lambda e, t=t: e.matmul(
                        psG[:, 256:384], crow16[0:1, 0:128],
                        penbuf[0:1, (t - 8) * 128:(t - 7) * 128],
                        start=True, stop=True),
                        [("VV", m_s8 if (tb == 0 and i == 0) else 0), ("VV", m_mi)],
                        inc="TT")
                    m_bc = P.c("TT")
                    P.add("vector", lambda e, t=t: e.tensor_tensor(
                        mskbuf[:, (t - 8) * 128:(t - 7) * 128], tril_sb[:, :],
                        psG[:, 256:384], ALU.mult), [("TT", m_bc)], inc="VV")
                    m_mi = P.c("VV")
            self.pv_free[0] = max(self.pv_free[0], m_s4)
            self.pv_free[1] = max(self.pv_free[1], m_s4)
            self.pv_free[2] = max(self.pv_free[2], m_s6)
            self.pv_free[3] = max(self.pv_free[3], m_s6)
            self.g_mi = m_mi
            return P.c("VV")

        def emit_attention(Qs, Ks, Vtm, outb, ptb, *, mixer, means, penbuf=None):
            if mixer:
                v_pen = emit_pen_precompute(Qs, means, penbuf)
            pending = []
            norm_q = []

            def emit_norm_step():
                if not norm_q:
                    return
                i, t, vm = norm_q.pop(0)
                P.add("tensor", lambda e, i=i: e.transpose(
                    psT[:, i * 128:(i + 1) * 128], otm16[:, i * 128:(i + 1) * 128], id16[:, :]),
                    [("VV", vm), ("VV", v_tr_prev[0])], inc="TT")
                tm2 = P.c("TT")
                P.add("vector", lambda e, i=i, t=t: e.tensor_copy(
                    outb[:, t * 128:(t + 1) * 128], psT[:, i * 128:(i + 1) * 128]),
                    [("TT", tm2)], inc="VV")
                v_tr_prev[0] = P.c("VV")

            def emit_pv(ctx):
                slot, grp, rr, smark = ctx
                for j, (i, b, pj) in enumerate(grp):
                    t = 4 * rr + i
                    last = (b == t) if not (mixer and rr < 2) else (b == 7)
                    pvb, off = pvloc(i)
                    wv = [("SC", smark)] if j == 0 else []
                    if b == 0:
                        wv.append(("VV", self.pv_free[i]))
                        if i % 2 == 0:
                            wv.append(("VV", self.pv_free[i + 1]))
                    P.add("tensor", lambda e, b=b, pj=pj, slot=slot, last=last,
                          i=i, pvb=pvb, off=off: e.matmul(
                        pvb[:, off:off + 129],
                        ptb[:, slot * 1024 + pj * 128: slot * 1024 + (pj + 1) * 128],
                        Vtm[:, b * VSTRIDE:b * VSTRIDE + 129],
                        start=(b == 0 and i % 2 == 0), stop=last),
                        wv, inc="TT" if j == len(grp) - 1 else None)
                self.slot_pv[slot] = P.c("TT")

            def push_pv(ctx):
                pending.append(ctx)
                if len(pending) > 2:
                    emit_pv(pending.pop(0))

            def flush():
                while pending:
                    emit_pv(pending.pop(0))

            for r in range(16):
                n_common = 8 if (mixer and r < 2) else 4 * r
                qcols = slice(512 * r, 512 * r + 512)
                for pb in range(n_common // 2):
                    b0 = 2 * pb
                    g = self.gidx; self.gidx += 1
                    sc = SCB2[g % 2]
                    slot = g % 3
                    P.add("tensor", lambda e, b0=b0, sc=sc, qcols=qcols: e.matmul(
                        sc[:, 0:512], Ks[:, b0 * 128:(b0 + 1) * 128], Qs[:, qcols],
                        start=True, stop=True),
                        [("SC", self.bank_exp[g % 2])], inc=None)
                    P.add("tensor", lambda e, b0=b0, sc=sc, qcols=qcols: e.matmul(
                        sc[:, 512:1024], Ks[:, (b0 + 1) * 128:(b0 + 2) * 128], Qs[:, qcols],
                        start=True, stop=True), [], inc="TT")
                    tmark = P.c("TT")
                    emit_norm_step()
                    P.add("scalar", lambda e, sc=sc, slot=slot: e.activation(
                        ptb[:, slot * 1024:(slot + 1) * 1024], sc[:, 0:1024], AF.Exp, scale=ISD),
                        [("TT", tmark), ("TT", self.slot_pv[slot])], inc="SC")
                    self.bank_exp[g % 2] = P.c("SC")
                    push_pv((slot,
                             [(i, b0, i) for i in (0, 2, 1, 3)]
                             + [(i, b0 + 1, 4 + i) for i in (0, 2, 1, 3)],
                             r, P.c("SC")))
                if not (mixer and r < 2):
                    pieces = [(i, 4 * r + b) for i in range(4) for b in range(i + 1)]
                    groups = [pieces[0:8], pieces[8:10]]
                    rag_slots = []
                    for grp in groups:
                        g = self.gidx; self.gidx += 1
                        sc = SCB2[g % 2]
                        slot = g % 3
                        rag_slots.append(slot)
                        for j, (i, b) in enumerate(grp):
                            t = 4 * r + i
                            wv = [("SC", self.bank_exp[g % 2])] if j == 0 else []
                            P.add("tensor", lambda e, b=b, j=j, t=t, sc=sc: e.matmul(
                                sc[:, j * 128:(j + 1) * 128],
                                Ks[:, b * 128:(b + 1) * 128], Qs[:, t * 128:(t + 1) * 128],
                                start=True, stop=True),
                                wv, inc="TT" if j == len(grp) - 1 else None)
                        gmark = P.c("TT")
                        vmask = 0
                        for j, (i, b) in enumerate(grp):
                            t = 4 * r + i
                            if b != t:
                                continue
                            if not mixer:
                                P.add("vector", lambda e, j=j, sc=sc: e.tensor_tensor(
                                    sc[:, j * 128:(j + 1) * 128], sc[:, j * 128:(j + 1) * 128],
                                    trilneg_sb[:, :], ALU.add), [("TT", gmark)], inc="VV")
                                vmask = P.c("VV")
                            else:
                                P.add("vector", lambda e, j=j, t=t, sc=sc: e.tensor_tensor(
                                    sc[:, j * 128:(j + 1) * 128], sc[:, j * 128:(j + 1) * 128],
                                    mskbuf[:, (t - 8) * 128:(t - 7) * 128], ALU.add),
                                    [("TT", gmark), ("VV", v_pen)], inc="VV")
                                vmask = P.c("VV")
                        wexp = [("TT", gmark), ("TT", self.slot_pv[slot])]
                        if vmask:
                            wexp.append(("VV", vmask))
                        P.add("scalar", lambda e, sc=sc, slot=slot, n=len(grp): e.activation(
                            ptb[:, slot * 1024: slot * 1024 + n * 128], sc[:, 0:n * 128],
                            AF.Exp, scale=ISD), wexp, inc="SC")
                        self.bank_exp[g % 2] = P.c("SC")
                        push_pv((slot, [(i, b, j) for j, (i, b) in enumerate(grp)],
                                 r, P.c("SC")))
                flush()
                while norm_q:
                    emit_norm_step()
                if not (mixer and r < 2):
                    tA = self.slot_pv[rag_slots[0]]
                    tB = self.slot_pv[rag_slots[1]]
                else:
                    tA = tB = P.c("TT")
                for lo, hi, tm in ((0, 2, tA), (2, 4, tB)):
                    rms = []
                    for i in range(lo, hi):
                        pvb, off = pvloc(i)
                        P.add("vector", lambda e, i=i, pvb=pvb, off=off: e.reciprocal(
                            rc_sb[:, i:i + 1], pvb[:, off + 128:off + 129]),
                            [("TT", tm)], inc="VV")
                        rms.append(P.c("VV"))
                    for i in range(lo, hi):
                        t = 4 * r + i
                        pvb, off = pvloc(i)
                        P.add("vector", lambda e, i=i, pvb=pvb, off=off: e.tensor_scalar(
                            otm16[:, i * 128:(i + 1) * 128], pvb[:, off:off + 128],
                            rc_sb[:, i:i + 1], None, ALU.mult),
                            [("VV", rms[1])], inc="VV")
                        self.pv_free[i] = P.c("VV")
                        norm_q.append((i, t, P.c("VV")))
            while norm_q:
                emit_norm_step()

        P.barrier()
        mstackB = ExitStack()
        pt_sb = self.sb("pt_sb", [128, 3 * 1024], BF16, mstackB)
        outh0 = self.sb("outh0", [128, S], BF16, mstackB)
        outh1 = self.sb("outh1", [128, S], BF16, mstackB)
        penbuf = self.sb("penbuf", [1, 56 * 128], BF16, mstackB)
        mskbuf = self.sb("mskbuf", [128, 56 * 128], F32, mstackB)
        def stage_a2a(outb, d_in, d_out, cc, sg):
            v_att = P.c("VV")
            for r in range(NCORE):
                P.dma("sync", lambda e, r=r, outb=outb, d_in=d_in: e.dma_start(
                    out=d_in[r], in_=outb[:, r * T:(r + 1) * T]),
                    waits=[("VV", v_att)] if r == 0 else [], sem=sg)
            ldm = P.c(sg)

            def _op(e, d_in=d_in, d_out=d_out):
                with allow_ici_all_to_all():
                    return e.collective_compute(
                        "AllToAll", ALU.bypass, replica_groups=[list(range(NCORE))],
                        ins=[d_in.opt()], outs=[d_out.opt()])
            P.add("gpsimd", _op, [(sg, ldm)], inc=cc)

        emit_attention(aQ0, aK, aV, outh0, pt_sb, mixer=True, means=means16, penbuf=penbuf)
        stage_a2a(outh0, d_a2a1i[0], d_a2a1o[0], "CC2a", "SG1")
        emit_attention(aQ1, aK, aV, outh1, pt_sb, mixer=True, means=means16, penbuf=penbuf)
        stage_a2a(outh1, d_a2a1i[1], d_a2a1o[1], "CC2b", "SG2")
        mstackB.close()
        mstackA.close()

        P.barrier()
        wstack = ExitStack()
        att1 = self.sb("att1", [128, KC * T], BF16, wstack)
        wosb = [self.sb("wosb%d" % i, [128, KC * 128], BF16, wstack) for i in range(4)]
        xstack2 = ExitStack()
        xb2 = self.sb("xb2", [128, KC * T], F32, xstack2, side="right")
        P.dma("sync", lambda e: e.dma_start(
            out=xb2[:, :].rearrange("p (kc t) -> p kc t", kc=KC),
            in_=d_xT.rearrange("kc p t -> p kc t")), sem="XB")
        xb_ld = P.c("XB")
        for r in range(NCORE):
            P.dma("sync", lambda e, r=r: e.dma_start(
                out=att1[:, (2 * r) * T:(2 * r + 1) * T], in_=d_a2a1o[0][r]),
                waits=[("CC2a", 1)] if r == 0 else [], sem="YA")
        ld_att1A = ("YA", P.c("YA"))
        for r in range(NCORE):
            P.dma("sync", lambda e, r=r: e.dma_start(
                out=att1[:, (2 * r + 1) * T:(2 * r + 2) * T], in_=d_a2a1o[1][r]),
                waits=[("CC2b", 1)] if r == 0 else [], sem="YB")
        ld_att1B = ("YB", P.c("YB"))
        wo_tm = {}
        wo_ev = {}
        cast_mk = {}
        sq_mk = {}
        stat_tm = {}
        wo_seq = [(h, ec) for h in range(2) for ec in range(KC)]

        def emit_stat_mms(s2):
            h2, ec2 = wo_seq[s2]
            cb = (sc16a, sc16b)[(s2 % 4) // 2]
            qb = (sq16a, sq16b)[(s2 % 4) // 2]
            co = (s2 % 2) * 512
            P.add("tensor", lambda e, h2=h2, ec2=ec2, cb=cb, co=co: e.matmul(
                psG[64 * h2:64 * h2 + 1, 0:512], csb16[:, 0:1], cb[:, co:co + 512],
                start=(ec2 == 0), stop=(ec2 == KC - 1)),
                [("VV", cast_mk[s2])])
            if h2 == 0:
                sq_out, sq_st = psG[32:33, 0:512], (ec2 == 0)
            else:
                sq_out, sq_st = psE[0:1, 0:512], (ec2 == 0)
            P.add("tensor", lambda e, ec2=ec2, qb=qb, co=co, sq_out=sq_out, sq_st=sq_st: e.matmul(
                sq_out, csb16[:, 0:1], qb[:, co:co + 512],
                start=sq_st, stop=(ec2 == KC - 1)),
                [("VV", sq_mk[s2])], inc="TT")
            stat_tm[s2] = P.c("TT")

        for s, (h, ec) in enumerate(wo_seq):
            us = "U%d" % (s % 4)
            P.dma("gpsimd", lambda e, ec=ec, s=s: e.dma_start(
                out=wosb[s % 4][:, :].rearrange("p (kc c) -> p kc c", kc=KC),
                in_=d_wot[ec].rearrange("kc p c -> p kc c")),
                waits=[("TT", wo_tm.get(s - 4, 0))], sem=us)
            wdm = (us, P.c(us))
            kco = list(range(0, KC, 2)) + list(range(1, KC, 2))
            bank = (psA, psB, psC, psD)[s % 4]
            for ki, kc in enumerate(kco):
                wv = []
                if ki == 0:
                    wv = [wdm, ld_att1A, ("VV", wo_ev.get(s - 4, 0))]
                elif ki == 8:
                    wv = [ld_att1B]
                P.add("tensor", lambda e, ec=ec, h=h, kc=kc, ki=ki, s=s, bank=bank: e.matmul(
                    bank[:, 0:512], wosb[s % 4][:, kc * 128:(kc + 1) * 128],
                    att1[:, kc * T + h * 512: kc * T + (h + 1) * 512],
                    start=(ki == 0), stop=(ki == KC - 1)),
                    wv, inc="TT" if ki == KC - 1 else None)
            tmark = P.c("TT")
            wo_tm[s] = tmark
            P.add("vector", lambda e, ec=ec, h=h, bank=bank: e.tensor_tensor(
                xb2[:, ec * T + h * 512: ec * T + (h + 1) * 512], bank[:, 0:512],
                xb2[:, ec * T + h * 512: ec * T + (h + 1) * 512], ALU.add),
                [("TT", tmark), ("XB", xb_ld)], inc="VV")
            wo_ev[s] = P.c("VV")
            cb = (sc16a, sc16b)[(s % 4) // 2]
            qb = (sq16a, sq16b)[(s % 4) // 2]
            co = (s % 2) * 512
            P.add("vector", lambda e, ec=ec, h=h, cb=cb, co=co: e.tensor_copy(
                cb[:, co:co + 512], xb2[:, ec * T + h * 512: ec * T + (h + 1) * 512]),
                [("TT", stat_tm.get(s - 4, 0))], inc="VV")
            cast_mk[s] = P.c("VV")
            P.add("vector", lambda e, cb=cb, qb=qb, co=co: e.tensor_tensor(
                qb[:, co:co + 512], cb[:, co:co + 512], cb[:, co:co + 512], ALU.mult),
                [], inc="VV")
            sq_mk[s] = P.c("VV")
            if s >= 1:
                emit_stat_mms(s - 1)
        emit_stat_mms(2 * KC - 1)
        v_wo_all = P.c("VV")
        P.add("vector", lambda e: e.tensor_copy(
            strow[0:1, T + 512:2 * T], psE[0:1, 0:512]),
            [("TT", stat_tm[2 * KC - 1])], inc="VV")
        v_ex2h1 = P.c("VV")
        wstack.close()

        h16stack = ExitStack()
        h16 = self.sb("h16", [128, KC * T], BF16, h16stack)
        sc_last = {}
        for h in range(2):
            hs_ = h * 512
            rbb = psE if h == 0 else psC
            abb = psF if h == 0 else psD
            sp_sem = "SP" if h == 0 else "SP2"
            w0 = [("TT", stat_tm[2 * KC - 1])]
            P.add("vector", lambda e, hs_=hs_, h=h: e.tensor_copy(
                strow[0:1, hs_:hs_ + 512], psG[64 * h:64 * h + 1, 0:512]), w0, inc="VV")
            if h == 0:
                P.add("vector", lambda e: e.tensor_copy(
                    strow[0:1, T:T + 512], psG[32:33, 0:512]), [], inc="VV")
            P.add("vector", lambda e, hs_=hs_: e.tensor_tensor(
                strow[0:1, 2 * T + hs_:2 * T + hs_ + 512], strow[0:1, hs_:hs_ + 512],
                strow[0:1, hs_:hs_ + 512], ALU.mult), [], inc="VV")
            P.add("vector", lambda e, hs_=hs_: e.tensor_tensor(
                strow[0:1, T + hs_:T + hs_ + 512], strow[0:1, T + hs_:T + hs_ + 512],
                strow[0:1, 2 * T + hs_:2 * T + hs_ + 512], ALU.subtract), [], inc="VV")
            v_var = P.c("VV")
            P.tap("strow%d" % h, strow[0:1, :], [1, 4 * T], F32)
            P.add("scalar", lambda e, hs_=hs_: e.activation(
                strow[0:1, 2 * T + hs_:2 * T + hs_ + 512], strow[0:1, T + hs_:T + hs_ + 512],
                AF.Ln, bias=csb[0:1, 2:3]), [("VV", v_var)], inc="SC")
            P.add("scalar", lambda e, hs_=hs_: e.activation(
                strow[0:1, 3 * T + hs_:3 * T + hs_ + 512],
                strow[0:1, 2 * T + hs_:2 * T + hs_ + 512], AF.Exp, scale=-0.5), [], inc="SC")
            s_r = P.c("SC")
            P.add("vector", lambda e, hs_=hs_: e.tensor_tensor(
                strow[0:1, 2 * T + hs_:2 * T + hs_ + 512], strow[0:1, hs_:hs_ + 512],
                strow[0:1, 3 * T + hs_:3 * T + hs_ + 512], ALU.mult), [("SC", s_r)], inc="VV")
            v_a = P.c("VV")
            wb1 = [("SC", s_r)] + ([("VV", v_wo_all)] if h == 1 else [("VV", v_ex2h1)])
            P.add("tensor", lambda e, hs_=hs_, rbb=rbb: e.matmul(
                rbb[:, 0:512], crow32[0:1, 0:128],
                strow[0:1, 3 * T + hs_:3 * T + hs_ + 512], start=True, stop=True), wb1)
            wb2 = [("VV", v_a)] + ([("VV", v_wo_all)] if h == 1 else [])
            P.add("tensor", lambda e, hs_=hs_, abb=abb: e.matmul(
                abb[:, 0:512], crow32[0:1, 0:128],
                strow[0:1, 2 * T + hs_:2 * T + hs_ + 512], start=True, stop=True),
                wb2, inc="TT")
            t_bch = P.c("TT")
            for kc in range(KC):
                P.dma("sync", lambda e, kc=kc, hs_=hs_: e.dma_start(
                    out=d_x1sp[kc][:, hs_:hs_ + 512],
                    in_=xb2[:, kc * T + hs_:kc * T + hs_ + 512]),
                    waits=[("VV", wo_ev[h * KC + kc])], sem=sp_sem)
            sp_half = P.c(sp_sem)
            ap_mk = {}
            for kc in range(KC):
                w = []
                if kc == 0:
                    w = [("TT", t_bch), (sp_sem, sp_half)]
                    if h == 1:
                        w.append(("SC", sc_last[0]))
                P.add("vector", lambda e, kc=kc, hs_=hs_, rbb=rbb: e.tensor_tensor(
                    tmp32[:, 0:512], xb2[:, kc * T + hs_:kc * T + hs_ + 512],
                    rbb[:, 0:512], ALU.mult), w, inc="VV")
                P.add("vector", lambda e, kc=kc, hs_=hs_, abb=abb: e.tensor_tensor(
                    xb2[:, kc * T + hs_:kc * T + hs_ + 512], tmp32[:, 0:512],
                    abb[:, 0:512], ALU.subtract), [], inc="VV")
                ap_mk[kc] = P.c("VV")
                P.dma("sync", lambda e, kc=kc, hs_=hs_: e.dma_start(
                    out=d_hssp[kc][:, hs_:hs_ + 512],
                    in_=xb2[:, kc * T + hs_:kc * T + hs_ + 512]),
                    waits=[("VV", ap_mk[kc])], sem=sp_sem)
            P.add("vector", lambda e, hs_=hs_: e.tensor_scalar(
                strow[0:1, 2 * T + hs_:2 * T + hs_ + 512],
                strow[0:1, T + hs_:T + hs_ + 512], 1e-5, None, ALU.add), [], inc="VV")
            P.add("vector", lambda e, hs_=hs_: e.reciprocal(
                strow[0:1, hs_:hs_ + 512], strow[0:1, 2 * T + hs_:2 * T + hs_ + 512]),
                [], inc="VV")
            P.add("vector", lambda e, hs_=hs_: e.tensor_tensor(
                strow[0:1, 2 * T + hs_:2 * T + hs_ + 512], strow[0:1, T + hs_:T + hs_ + 512],
                strow[0:1, hs_:hs_ + 512], ALU.mult), [], inc="VV")
            v_rr = P.c("VV")
            P.add("scalar", lambda e, hs_=hs_: e.activation(
                strow[0:1, hs_:hs_ + 512], strow[0:1, 2 * T + hs_:2 * T + hs_ + 512],
                AF.Ln, bias=csb[0:1, 3:4]), [("VV", v_rr)], inc="SC")
            P.add("scalar", lambda e, hs_=hs_: e.activation(
                strow[0:1, 3 * T + hs_:3 * T + hs_ + 512], strow[0:1, hs_:hs_ + 512],
                AF.Exp, scale=-0.5), [], inc="SC")
            s_r2 = P.c("SC")
            P.add("tensor", lambda e, hs_=hs_, rbb=rbb: e.matmul(
                rbb[:, 0:512], crow32[0:1, 0:128],
                strow[0:1, 3 * T + hs_:3 * T + hs_ + 512], start=True, stop=True),
                [("SC", s_r2), ("VV", ap_mk[KC - 1])], inc="TT")
            t_b2 = P.c("TT")
            sc_mk = {}
            for kc in range(KC):
                wv = [("TT", t_b2)] if kc == 0 else []
                if kc >= 2:
                    wv.append(("SC", sc_mk[kc - 2]))
                P.add("vector", lambda e, kc=kc, hs_=hs_, rbb=rbb: e.tensor_tensor(
                    tmp32[:, (kc % 2) * 512:(kc % 2) * 512 + 512],
                    xb2[:, kc * T + hs_:kc * T + hs_ + 512], rbb[:, 0:512], ALU.mult),
                    wv, inc="VV")
                vmk = P.c("VV")
                stg = (sc16a, sq16a)[kc % 2]
                qs = "R%d" % (2 * h + kc % 2)
                ws = [("VV", vmk)]
                if h == 0 and kc < 2:
                    ws.append(("TT", stat_tm[2 * KC - 1]))
                if kc >= 2:
                    ws.append((qs, P.c(qs)))
                elif h == 1:
                    pq = "R%d" % (kc % 2)
                    ws.append((pq, P.c(pq)))
                P.add("scalar", lambda e, kc=kc, stg=stg: e.activation(
                    stg[:, 0:512], tmp32[:, (kc % 2) * 512:(kc % 2) * 512 + 512],
                    AF.Copy, scale=lnw_sb[:, kc:kc + 1]), ws, inc="SC")
                P.add("scalar", lambda e, kc=kc, hs_=hs_: e.activation(
                    h16[:, kc * T + hs_:kc * T + hs_ + 512],
                    tmp32[:, (kc % 2) * 512:(kc % 2) * 512 + 512],
                    AF.Copy, scale=lnw_sb[:, KC + kc:KC + kc + 1]), [], inc="SC")
                sc_mk[kc] = P.c("SC")
                P.dma("sync", lambda e, kc=kc, h=h, stg=stg: e.dma_start(
                    out=d_ag2i[h][kc], in_=stg[:, 0:512]),
                    waits=[("SC", sc_mk[kc])], sem=qs)
            sc_last[h] = sc_mk[KC - 1]
            ra, rb_ = "R%d" % (2 * h), "R%d" % (2 * h + 1)
            P.add("gpsimd", lambda e, h=h: e.collective_compute(
                "AllGather", ALU.bypass, replica_groups=[list(range(NCORE))],
                ins=[d_ag2i[h].opt()], outs=[d_ag2o[h].opt()]),
                [(ra, P.c(ra)), (rb_, P.c(rb_))], inc="CC3" + "ab"[h])
        P.tap("x1", d_x1sp, [KC, 128, T], F32)
        P.tap("hs", d_hssp, [KC, 128, T], F32)
        xstack2.close()
        scstack.close()

        P.barrier()
        a2stack = ExitStack()
        q2 = [self.sb("q2_%d" % i, [128, S], BF16, a2stack) for i in range(2)]
        k2 = [self.sb("k2_%d" % i, [128, S], BF16, a2stack) for i in range(2)]
        v2 = [self.sb("v2_%d" % i, [128, NB * VSTRIDE + 64], BF16, a2stack) for i in range(2)]
        w8stack = ExitStack()
        w216 = self.sb("w216", [128, KC * 6 * 128], BF16, w8stack)
        xst = [self.sb("xst%d" % i, [128, KC * 512], BF16, w8stack) for i in range(2)]
        vtmp2 = self.sb("vtmp2", [128, 512], BF16, w8stack)
        P.dma("gpsimd", lambda e: e.dma_start(
            out=w216[:, :].rearrange("p (kc m c) -> p kc m c", kc=KC, m=6),
            in_=d_w2qkv.rearrange("kc m p c -> p kc m c")), sem="WD")
        wd_w2 = P.c("WD")
        for hh in range(2):
            P.add("vector", lambda e, hh=hh: e.memset(
                v2[hh][:, :NB * VSTRIDE].rearrange("p (n c) -> p n c", c=VSTRIDE)[:, :, 128:129], 1.0),
                [], inc="VV")

        def a2_evac(tc, m, bank, tmark):
            hh, role = m // 3, m % 3
            cols = slice(tc * 512, (tc + 1) * 512)
            if role == 0:
                P.add("scalar", lambda e, hh=hh, cols=cols, bank=bank: e.activation(
                    q2[hh][:, cols], bank[:, 0:512], AF.Copy), [("TT", tmark)], inc="SC")
                return ("SC", P.c("SC"))
            if role == 1:
                P.add("scalar", lambda e, hh=hh, cols=cols, bank=bank: e.activation(
                    k2[hh][:, cols], bank[:, 0:512], AF.Copy), [("TT", tmark)], inc="SC")
                return ("SC", P.c("SC"))
            P.add("vector", lambda e, bank=bank: e.tensor_copy(vtmp2[:, :], bank[:, 0:512]),
                  [("TT", tmark), ("TT", t_tr_prev[0])], inc="VV")
            vm = P.c("VV")
            for j in range(4):
                P.add("tensor", lambda e, j=j: e.transpose(
                    psT[:, j * 128:(j + 1) * 128], vtmp2[:, j * 128:(j + 1) * 128], id16[:, :]),
                    [("VV", vm), ("VV", v_tr_prev[0])] if j == 0 else [],
                    inc="TT" if j == 3 else None)
            ttm = P.c("TT")
            t_tr_prev[0] = ttm
            for j in range(4):
                bidx = 4 * tc + j
                P.add("vector", lambda e, hh=hh, j=j, bidx=bidx: e.tensor_copy(
                    v2[hh][:, bidx * VSTRIDE:bidx * VSTRIDE + 128], psT[:, j * 128:(j + 1) * 128]),
                    [("TT", ttm)] if j == 0 else [], inc="VV" if j == 3 else None)
            v_tr_prev[0] = P.c("VV")
            return ("VV", v_tr_prev[0])

        emit_qkv_produce("a2", d_ag2o, ("CC3a", "CC3b"), w216, wd_w2, 6, a2_evac, xst)
        w8stack.close()

        P.barrier()
        pt2 = self.sb("pt2", [128, 3 * 1024], BF16, a2stack)
        out20 = self.sb("out20", [128, S], BF16, a2stack)
        out21 = self.sb("out21", [128, S], BF16, a2stack)
        emit_attention(q2[0], k2[0], v2[0], out20, pt2, mixer=False, means=None)
        stage_a2a(out20, d_a2a2i[0], d_a2a2o[0], "CC4a", "SG3")
        emit_attention(q2[1], k2[1], v2[1], out21, pt2, mixer=False, means=None)
        stage_a2a(out21, d_a2a2i[1], d_a2a2o[1], "CC4b", "SG4")
        a2stack.close()

        P.barrier()
        ph10stack = ExitStack()
        att2 = self.sb("att2", [128, KC * T], BF16, ph10stack)
        P.dma("gpsimd", lambda e: e.dma_start(
            out=wota16[:, :].rearrange("p (dc r) -> p dc r", dc=KC),
            in_=d_wotao.rearrange("dc p r -> p dc r")), waits=[("CC5", 1)], sem="WT")
        wt_ld = P.c("WT")
        for r in range(NCORE):
            P.dma("sync", lambda e, r=r: e.dma_start(
                out=att2[:, (2 * r) * T:(2 * r + 1) * T], in_=d_a2a2o[0][r]),
                waits=[("CC4a", 1)] if r == 0 else [], sem="YA")
        ld_att2A = ("YA", P.c("YA"))
        for r in range(NCORE):
            P.dma("sync", lambda e, r=r: e.dma_start(
                out=att2[:, (2 * r + 1) * T:(2 * r + 2) * T], in_=d_a2a2o[1][r]),
                waits=[("CC4b", 1)] if r == 0 else [], sem="YB")
        ld_att2B = ("YB", P.c("YB"))
        dco = list(range(0, KC, 2)) + list(range(1, KC, 2))
        for h in range(2):
            bank = (psA, psB)[h]
            for di, dc in enumerate(dco):
                wv = []
                if di == 0:
                    wv = [ld_att2A, ("WT", wt_ld)]
                elif di == 8:
                    wv = [ld_att2B]
                P.add("tensor", lambda e, h=h, dc=dc, di=di, bank=bank: e.matmul(
                    bank[0:16, 0:512], wota16[:, dc * 16:(dc + 1) * 16],
                    att2[:, dc * T + h * 512: dc * T + (h + 1) * 512],
                    start=(di == 0), stop=(di == KC - 1)),
                    wv, inc="TT" if di == KC - 1 else None)
            tmark = P.c("TT")
            P.add("scalar", lambda e, h=h, bank=bank: e.activation(
                m1_32[0:16, h * 512:(h + 1) * 512], bank[0:16, 0:512], AF.Tanh),
                [("TT", tmark)], inc="SC")
        s_tanh = P.c("SC")
        for h in range(2):
            bank = (psE, psF)[h]
            for dc in range(KC):
                P.add("tensor", lambda e, h=h, dc=dc, bank=bank: e.matmul(
                    bank[0:16, 0:512], a16[:, dc * 16:(dc + 1) * 16],
                    h16[:, dc * T + h * 512: dc * T + (h + 1) * 512],
                    start=(dc == 0), stop=(dc == KC - 1)), [],
                    inc="TT" if dc == KC - 1 else None)
            tmark = P.c("TT")
            P.add("vector", lambda e, h=h, bank=bank: e.scalar_tensor_tensor(
                modhA[0:16, h * 512:(h + 1) * 512], bank[0:16, 0:512], 0.1,
                m1_32[0:16, h * 512:(h + 1) * 512], ALU.mult, ALU.mult),
                [("TT", tmark), ("SC", s_tanh)], inc="VV")
        v_modhA = P.c("VV")
        P.tap("modhA", modhA[0:16, :], [16, T], BF16)
        ph10stack.close()

        P.barrier()
        gustack = ExitStack()
        gu = self.sb("gu", [128, FC * T], BF16, gustack, side="right")
        ph11stack = ExitStack()
        winb = [self.sb("winb%d" % i, [128, KC * 128], BF16, ph11stack) for i in range(2)]
        wupb = [self.sb("wupb%d" % i, [128, KC * 128], BF16, ph11stack) for i in range(2)]
        sil32 = [self.sb("sil32_%d" % i, [128, 512], F32, ph11stack) for i in range(2)]
        b16s = [self.sb("b16s%d" % i, [16, 16 * 128], BF16, ph11stack) for i in range(2)]
        b_marks = {}
        f_tm = {}
        f_sil = {}
        f_guv = {}

        def load_bstrip(g4):
            us = "U%d" % (2 + g4 % 2)
            P.dma("gpsimd", lambda e, g4=g4: e.dma_start(
                out=b16s[g4 % 2][0:16, :].rearrange("p (fc c) -> p fc c", fc=16),
                in_=d_bmat[g4 * 16:(g4 + 1) * 16].rearrange("fc p c -> p fc c")),
                waits=[("TT", f_tm.get((g4 - 1) * 16 - 1, 0))], sem=us)
            b_marks[g4] = (us, P.c(us))

        load_bstrip(0)
        load_bstrip(1)
        for fc in range(FC):
            if fc >= 16 and fc % 16 == 0 and (fc // 16 + 1) < 4:
                load_bstrip(fc // 16 + 1)
            us = "U%d" % (fc % 2)
            P.dma("gpsimd", lambda e, fc=fc: e.dma_start(
                out=winb[fc % 2][:, :].rearrange("p (kc c) -> p kc c", kc=KC),
                in_=d_win[fc].rearrange("kc p c -> p kc c")),
                waits=[("TT", f_tm.get(fc - 2, 0))], sem=us)
            P.dma("gpsimd", lambda e, fc=fc: e.dma_start(
                out=wupb[fc % 2][:, :].rearrange("p (kc c) -> p kc c", kc=KC),
                in_=d_wup[fc].rearrange("kc p c -> p kc c")), sem=us)
            wdm = (us, P.c(us))
            g4 = fc // 16
            for h in range(2):
                gbank = (psA, psB)[h]
                ubank = (psC, psD)[h]
                for kc in range(KC):
                    wv = [wdm, b_marks[g4],
                          ("SC", f_sil.get((fc - 1, h), 0)), ("VV", v_modhA)] if kc == 0 else []
                    P.add("tensor", lambda e, fc=fc, h=h, kc=kc, gbank=gbank: e.matmul(
                        gbank[:, 0:512], winb[fc % 2][:, kc * 128:(kc + 1) * 128],
                        h16[:, kc * T + h * 512: kc * T + (h + 1) * 512],
                        start=(kc == 0), stop=False), wv)
                P.add("tensor", lambda e, fc=fc, h=h, gbank=gbank, g4=g4: e.matmul(
                    gbank[:, 0:512], b16s[g4 % 2][0:16, (fc % 16) * 128:(fc % 16 + 1) * 128],
                    modhA[0:16, h * 512:(h + 1) * 512],
                    start=False, stop=True), [], inc="TT")
                tg = P.c("TT")
                for kc in range(KC):
                    wv = [("VV", f_guv.get((fc - 1, h), 0))] if kc == 0 else []
                    P.add("tensor", lambda e, fc=fc, h=h, kc=kc, ubank=ubank: e.matmul(
                        ubank[:, 0:512], wupb[fc % 2][:, kc * 128:(kc + 1) * 128],
                        h16[:, kc * T + h * 512: kc * T + (h + 1) * 512],
                        start=(kc == 0), stop=(kc == KC - 1)), wv,
                        inc="TT" if kc == KC - 1 else None)
                tu = P.c("TT")
                if h == 1:
                    f_tm[fc] = tu
                P.add("scalar", lambda e, h=h, gbank=gbank: e.activation(
                    sil32[h][:, :], gbank[:, 0:512], AF.Silu), [("TT", tg)], inc="SC")
                f_sil[(fc, h)] = P.c("SC")
                P.add("vector", lambda e, fc=fc, h=h, ubank=ubank: e.tensor_tensor(
                    gu[:, fc * T + h * 512: fc * T + (h + 1) * 512], ubank[:, 0:512],
                    sil32[h][:, :], ALU.mult), [("SC", f_sil[(fc, h)]), ("TT", tu)], inc="VV")
                f_guv[(fc, h)] = P.c("VV")
        ph11stack.close()
        h16stack.close()

        P.barrier()
        ph12stack = ExitStack()
        wdnb = [self.sb("wdnb%d" % i, [128, FC * 128], BF16, ph12stack) for i in range(2)]
        x1t = [self.sb("x1t%d" % i, [128, T], F32, ph12stack) for i in range(2)]
        hst = [self.sb("hst%d" % i, [128, T], F32, ph12stack) for i in range(2)]
        outst = [self.sb("outst%d" % i, [128, T], F32, ph12stack) for i in range(2)]
        d_tm = {}
        d_ev = {}
        st_marks = {}
        for ec in range(KC):
            us = "U%d" % (ec % 2)
            qs = "Q%d" % (ec % 2)
            qo = "Q%d" % (2 + ec % 2)
            P.dma("gpsimd", lambda e, ec=ec: e.dma_start(
                out=wdnb[ec % 2][:, :].rearrange("p (fc c) -> p fc c", fc=FC),
                in_=d_wdn[ec].rearrange("fc p c -> p fc c")),
                waits=[("TT", d_tm.get(ec - 2, 0))], sem=us)
            wdm = (us, P.c(us))
            P.dma("sync", lambda e, ec=ec: e.dma_start(out=x1t[ec % 2][:, :], in_=d_x1sp[ec]),
                  waits=[("VV", d_ev.get(ec - 2, 0))], sem=qs)
            P.dma("sync", lambda e, ec=ec: e.dma_start(out=hst[ec % 2][:, :], in_=d_hssp[ec]),
                  sem=qs)
            ldm = (qs, P.c(qs))
            for h in range(2):
                bank = (psA, psB)[h]
                for fc in range(FC):
                    wv = [wdm, ("VV", d_ev.get((ec - 1, h), 0))] if fc == 0 else []
                    P.add("tensor", lambda e, ec=ec, h=h, fc=fc, bank=bank: e.matmul(
                        bank[:, 0:512], wdnb[ec % 2][:, fc * 128:(fc + 1) * 128],
                        gu[:, fc * T + h * 512: fc * T + (h + 1) * 512],
                        start=(fc == 0), stop=(fc == FC - 1)), wv,
                        inc="TT" if fc == FC - 1 else None)
                tmark = P.c("TT")
                if h == 1:
                    d_tm[ec] = tmark
                P.add("vector", lambda e, ec=ec, h=h, bank=bank: e.tensor_tensor(
                    tmp32[:, h * 512:(h + 1) * 512], bank[:, 0:512],
                    x1t[ec % 2][:, h * 512:(h + 1) * 512], ALU.add),
                    [("TT", tmark), ldm, (qo, st_marks.get(ec - 2, 0))], inc="VV")
                P.add("vector", lambda e, ec=ec, h=h: e.tensor_tensor(
                    outst[ec % 2][:, h * 512:(h + 1) * 512], tmp32[:, h * 512:(h + 1) * 512],
                    hst[ec % 2][:, h * 512:(h + 1) * 512], ALU.add), [], inc="VV")
                d_ev[(ec, h)] = P.c("VV")
            d_ev[ec] = P.c("VV")
            P.dma("sync", lambda e, ec=ec: e.dma_start(out=d_out[ec], in_=outst[ec % 2][:, :]),
                  waits=[("VV", d_ev[ec])], sem=qo)
            st_marks[ec] = P.c(qo)
        P.add("sync", lambda e: e.nop(),
              waits=[("Q2", P.c("Q2")), ("Q3", P.c("Q3"))])
        P.add("gpsimd", lambda e: e.nop(),
              waits=[("Q2", P.c("Q2")), ("Q3", P.c("Q3"))])

        self.emit_and_finalize()
        ph12stack.close()
        gustack.close()
        self.ctx.close()
        return nc


def build_program(debug_taps=()):
    b = Builder(debug_taps)
    b.build()
    return b.nc


def _host_stage(inputs):
    x = np.asarray(inputs["L0_attn_x"])
    wq = np.asarray(inputs["L0_attn_wq"])
    wk = np.asarray(inputs["L0_attn_wk"])
    wv = np.asarray(inputs["L0_attn_wv"])
    wo = np.asarray(inputs["L0_attn_wo"])
    ln1w = np.asarray(inputs["L0_ffn_ln1_w"])
    ln2w = np.asarray(inputs["L0_ffn_ln2_w"])
    Wq2 = np.asarray(inputs["L0_ffn_Wq"])
    Wk2 = np.asarray(inputs["L0_ffn_Wk"])
    Wv2 = np.asarray(inputs["L0_ffn_Wv"])
    Wo2 = np.asarray(inputs["L0_ffn_Wo"])
    A = np.asarray(inputs["L0_ffn_A"])
    B = np.asarray(inputs["L0_ffn_B"])
    Win = np.asarray(inputs["L0_ffn_W_in_base"])
    Wup = np.asarray(inputs["L0_ffn_W_up_base"])
    Wdn = np.asarray(inputs["L0_ffn_W_down"])

    perm = np.concatenate([np.arange(0, 128, 2), np.arange(1, 128, 2)])
    inv_freq = 1.0 / (10000.0 ** (np.arange(0, 128, 2, dtype=np.float32) / 128.0))
    ang = np.arange(S, dtype=np.float32)[:, None] * inv_freq[None, :]
    cossin = np.empty((128, S), np.float32)
    cossin[0:64] = np.cos(ang).T
    cossin[64:128] = np.sin(ang).T

    wot = np.ascontiguousarray(wo.reshape(KC, 128, KC, 128).transpose(2, 0, 1, 3)).astype(np.float32)
    winT = np.ascontiguousarray(Win.reshape(FC, 128, KC, 128).transpose(0, 2, 3, 1)).astype(np.float32)
    wupT = np.ascontiguousarray(Wup.reshape(FC, 128, KC, 128).transpose(0, 2, 3, 1)).astype(np.float32)
    wdnT = np.ascontiguousarray(Wdn.reshape(KC, 128, FC, 128).transpose(0, 2, 3, 1)).astype(np.float32)
    afull = np.ascontiguousarray(A.reshape(KC, 128, 16)).astype(np.float32)
    bmat = np.ascontiguousarray(B.reshape(16, FC, 128).transpose(1, 0, 2)).astype(np.float32)
    lnw_arr = np.empty((128, 2 * KC), np.float32)
    lnw_arr[:, :KC] = ln1w.reshape(KC, 128).T
    lnw_arr[:, KC:] = ln2w.reshape(KC, 128).T
    consts = np.zeros((128, 4), np.float32)
    consts[:, 0] = 1.0 / 2048.0
    consts[:, 1] = 1.0
    consts[:, 2] = 1e-5
    consts[:, 3] = 1e-6
    crow = np.ones((1, 256), np.float32)
    ident = np.eye(128, dtype=np.float32)
    ii = np.arange(128)
    tril01 = (ii[:, None] > ii[None, :]).astype(np.float32)
    trilneg = (tril01 * NEG_CAUSAL).astype(np.float32)

    in_maps = []
    for c in range(NCORE):
        sl = slice(c * T, (c + 1) * T)
        xTc = np.ascontiguousarray(x[0, sl, :].T).reshape(KC, 128, T).astype(np.float32)
        q0 = wq[:, (2 * c) * 128:(2 * c + 1) * 128][:, perm]
        q1 = wq[:, (2 * c + 1) * 128:(2 * c + 2) * 128][:, perm]
        kk = wk[:, c * 128:(c + 1) * 128][:, perm]
        vv = wv[:, c * 128:(c + 1) * 128]
        mw = np.stack([q0, q1, kk, vv], axis=1)
        mwqkv = np.ascontiguousarray(
            mw.reshape(KC, 128, 4, 128).transpose(0, 2, 1, 3)).astype(np.float32)
        h0 = slice((2 * c) * 128, (2 * c + 1) * 128)
        h1 = slice((2 * c + 1) * 128, (2 * c + 2) * 128)
        w2 = np.stack([Wq2[h0].T, Wk2[h0].T, Wv2[h0].T,
                       Wq2[h1].T, Wk2[h1].T, Wv2[h1].T], axis=1)
        w2qkv = np.ascontiguousarray(
            w2.reshape(KC, 128, 6, 128).transpose(0, 2, 1, 3)).astype(np.float32)
        esl = slice(c * 256, (c + 1) * 256)
        wo2s = np.ascontiguousarray(Wo2[esl].reshape(2, 128, KC, 128).transpose(0, 2, 1, 3)).astype(np.float32)
        a2s = np.ascontiguousarray(A[esl].reshape(2, 128, 16)).astype(np.float32)
        in_maps.append({
            "xT": xTc, "cossin": cossin,
            "mwqkv": mwqkv, "wot": wot, "w2qkv": w2qkv,
            "wo2s": wo2s, "a2s": a2s, "afull": afull, "bmat": bmat,
            "lnw": lnw_arr, "winT": winT, "wupT": wupT, "wdnT": wdnT,
            "consts": consts, "crow": crow, "ident": ident,
            "tril01": tril01, "trilneg": trilneg,
        })
    return in_maps


def kernel(**inputs):
    in_maps = _host_stage(inputs)
    nc = build_program()
    res = run_bass_kernel_spmd(nc, in_maps, core_ids=list(range(NCORE)))
    full = np.empty((1, S, D), np.float32)
    for c in range(NCORE):
        o = np.asarray(res.results[c]["out"], np.float32)
        full[0, c * T:(c + 1) * T, :] = o.reshape(D, T).T
    return full


# version 26
# speedup vs baseline: 1.0042x; 1.0042x over previous
import math
from contextlib import ExitStack

import numpy as np
import ml_dtypes

import concourse.bass as bass
import concourse.mybir as mybir
from concourse.bass_utils import run_bass_kernel_spmd
from concourse.aot_env import allow_ici_all_to_all

F32 = mybir.dt.float32
BF16 = mybir.dt.bfloat16
AF = mybir.ActivationFunctionType
ALU = mybir.AluOpType
AX = mybir.AxisListType

NCORE = 8
S = 8192
D = 2048
T = 1024
KC = 16
NB = 64
FC = 64
ISD = 1.0 / math.sqrt(128.0)
NEG_CAUSAL = -1.0e6
NEG_PEN = -113152.0

ENGINES = ("sync", "gpsimd", "tensor", "vector", "scalar")
OWN_COMPUTE = {"tensor": "TT", "vector": "VV", "scalar": "SC"}
VSTRIDE = 129


class Builder:
    def __init__(self, debug_taps=()):
        self.nc = bass.Bass("TRN2", target_bir_lowering=False)
        self.ops = []
        self.cnt = {}
        self.pending = {}
        self.ctx = ExitStack()
        self.sems = {}
        self.debug_taps = set(debug_taps)
        self.slot_pv = [0, 0, 0]
        self.bank_exp = [0, 0]
        self.pv_free = [0, 0, 0, 0]
        self.g_mi = 0
        self.gidx = 0
        self.rb_free = {}

    def c(self, sem):
        return self.cnt.get(sem, 0)

    def add(self, eng, fn, waits=(), inc=None, n=1):
        w = list(self.pending.pop(eng, [])) + [(s, v) for (s, v) in waits if v > 0]
        self.ops.append((eng, fn, w, inc, n))
        if inc is not None:
            self.cnt[inc] = self.cnt.get(inc, 0) + n
        return self.cnt.get(inc, 0) if inc else None

    def dma(self, eng, fn, waits=(), sem="LD"):
        return self.add(eng, fn, waits, inc=sem, n=16)

    def barrier(self):
        snap = [(s, v) for (s, v) in self.cnt.items()
                if v > 0 and not s.startswith("CC")]
        for e in ENGINES:
            own = OWN_COMPUTE.get(e)
            self.pending[e] = [(s, v) for (s, v) in snap if s != own]

    def sb(self, name, shape, dt, stack=None, side=None):
        return (stack or self.ctx).enter_context(
            self.nc.sbuf_tensor(name, shape, dt, side=side))

    def tap(self, name, ap_src, shape, dt):
        if name not in self.debug_taps:
            return
        dst = self.nc.dram_tensor("dbg_" + name, shape, dt, kind="ExternalOutput").ap()
        waits = [(s, v) for (s, v) in self.cnt.items() if v > 0]
        self.dma("sync", lambda e: e.dma_start(out=dst, in_=ap_src), waits=waits)

    def emit_and_finalize(self):
        block = self.ctx.enter_context(self.nc.Block())
        sems, ops = self.sems, self.ops

        def engine_body(en):
            def f(eng):
                for (oen, fn, waits, inc, n) in ops:
                    if oen != en:
                        continue
                    for (s, v) in waits:
                        eng.wait_ge(sems[s], v)
                    inst = fn(eng)
                    if inc is not None:
                        inst.then_inc(sems[inc], n)
            return f

        block.sync(engine_body("sync"))
        block.gpsimd(engine_body("gpsimd"))
        block.tensor(engine_body("tensor"))
        block.vector(engine_body("vector"))
        block.scalar(engine_body("scalar"))

    def build(self):
        nc = self.nc
        P = self
        EI = lambda name, shape, dt=F32: nc.dram_tensor(name, shape, dt, kind="ExternalInput").ap()
        EO = lambda name, shape, dt=F32: nc.dram_tensor(name, shape, dt, kind="ExternalOutput").ap()
        IN = lambda name, shape, dt: nc.dram_tensor(name, shape, dt).ap()
        SH = lambda name, shape, dt: nc.dram_tensor(
            name, shape, dt, addr_space="Shared").ap()

        d_xT = EI("xT", [KC, 128, T])
        d_cossin = EI("cossin", [128, S])
        d_mwqkv = EI("mwqkv", [KC, 4, 128, 128])
        d_wot = EI("wot", [KC, KC, 128, 128])
        d_w2qkv = EI("w2qkv", [KC, 6, 128, 128])
        d_wo2s = EI("wo2s", [2, KC, 128, 128])
        d_a2s = EI("a2s", [2, 128, 16])
        d_afull = EI("afull", [KC, 128, 16])
        d_bmat = EI("bmat", [FC, 16, 128])
        d_lnw = EI("lnw", [128, 2 * KC])
        d_win = EI("winT", [FC, KC, 128, 128])
        d_wup = EI("wupT", [FC, KC, 128, 128])
        d_wdn = EI("wdnT", [KC, FC, 128, 128])
        d_consts = EI("consts", [128, 4])
        d_crow = EI("crow", [1, 256])
        d_ident = EI("ident", [128, 128])
        d_tril = EI("tril01", [128, 128])
        d_trilneg = EI("trilneg", [128, 128])
        d_out = EO("out", [KC, 128, T])

        d_ag1i = [IN("ag1_in%d" % i, [KC, 128, 512], BF16) for i in range(2)]
        d_ag1o = [SH("ag1_out%d" % i, [NCORE, KC, 128, 512], BF16) for i in range(2)]
        d_ag2i = [IN("ag2_in%d" % i, [KC, 128, 512], BF16) for i in range(2)]
        d_ag2o = [SH("ag2_out%d" % i, [NCORE, KC, 128, 512], BF16) for i in range(2)]
        d_a2a1i = [IN("a2a1_in%d" % i, [NCORE, 128, T], BF16) for i in range(2)]
        d_a2a1o = [IN("a2a1_out%d" % i, [NCORE, 128, T], BF16) for i in range(2)]
        d_a2a2i = [IN("a2a2_in%d" % i, [NCORE, 128, T], BF16) for i in range(2)]
        d_a2a2o = [IN("a2a2_out%d" % i, [NCORE, 128, T], BF16) for i in range(2)]
        d_wotai = IN("wota_in", [KC, 128, 16], F32)
        d_wotao = SH("wota_out", [KC, 128, 16], F32)
        d_x1sp = IN("x1_sp", [KC, 128, T], F32)
        d_hssp = IN("hs_sp", [KC, 128, T], F32)

        for s in ("LD", "WD", "TT", "VV", "SC", "CC1a", "CC1b", "CC1c", "CC1d",
                  "CC2a", "CC2b",
                  "CC3a", "CC3b", "CC3c", "CC3d", "CC4a", "CC4b", "CC5",
                  "Q0", "Q1", "Q2", "Q3", "Q4", "Q5", "Q6", "Q7", "Q8", "Q9",
                  "R0", "R1", "R2", "R3", "R4", "R5", "R6", "R7", "WI",
                  "U0", "U1", "U2", "U3",
                  "YA", "YB", "SG1", "SG2", "SG3", "SG4", "SP", "SP2", "WT", "XB"):
            self.sems[s] = self.ctx.enter_context(nc.semaphore(s))

        csb = self.sb("csb", [128, 4], F32)
        csb16 = self.sb("csb16", [128, 4], BF16)
        crow16 = self.sb("crow16", [1, 256], BF16)
        crow32 = self.sb("crow32", [1, 256], F32)
        id16 = self.sb("id16", [128, 128], BF16)
        tril_sb = self.sb("tril_sb", [128, 128], F32)
        trilneg_sb = self.sb("trilneg_sb", [128, 128], F32)
        lnw_sb = self.sb("lnw_sb", [128, 2 * KC], F32)
        tmp32 = self.sb("tmp32", [128, 1024], F32)
        wota16 = self.sb("wota16", [128, KC * 16], BF16)
        a16 = self.sb("a16", [128, KC * 16], BF16)
        modhA = self.sb("modhA", [16, T], BF16)
        m1_32 = self.sb("m1_32", [16, T], F32)
        grow4 = self.sb("grow4", [1, 512], BF16)
        gt32_4 = self.sb("gt32_4", [64, 512], F32)
        cmp4 = self.sb("cmp4", [64, 512], BF16)
        msk_sb = self.sb("msk_sb", [128, 128], F32)
        rc_sb = self.sb("rc_sb", [128, 4], F32)
        otm16 = self.sb("otm16", [128, 512], BF16)

        scstack = ExitStack()
        strow = self.sb("strow", [1, 4 * T], F32, scstack, side="right")
        sc16a = self.sb("sc16a", [128, T], BF16, scstack, side="right")
        sc16b = self.sb("sc16b", [128, T], BF16, scstack, side="right")
        sq16a = self.sb("sq16a", [128, T], BF16, scstack, side="right")
        sq16b = self.sb("sq16b", [128, T], BF16, scstack, side="right")
        mwstack = ExitStack()
        mw16 = self.sb("mw16", [128, KC * 4 * 128], BF16, mwstack, side="right")
        xstack1 = ExitStack()
        xbuf = self.sb("xbuf", [128, KC * T], F32, xstack1, side="right")

        scU = self.ctx.enter_context(nc.psum_tensor("scU", [128, 1024], F32))
        scV = self.ctx.enter_context(nc.psum_tensor("scV", [128, 1024], F32))
        psE = self.ctx.enter_context(nc.psum_tensor("psE", [128, 512], F32))
        psF = self.ctx.enter_context(nc.psum_tensor("psF", [128, 512], F32))
        psG = self.ctx.enter_context(nc.psum_tensor("psG", [128, 512], F32))
        psT = self.ctx.enter_context(nc.psum_tensor("psT", [128, 1024], BF16))

        class PsView:
            def __init__(self, t, off):
                self.t, self.off = t, off

            def __getitem__(self, idx):
                p, c = idx
                return self.t[p, slice(c.start + self.off, c.stop + self.off)]

        psA = PsView(scU, 0)
        psB = PsView(scU, 512)
        psC = PsView(scV, 0)
        psD = PsView(scV, 512)
        PS = [psA, psB, psC, psD, psE, psF]

        P.dma("sync", lambda e: e.dma_start(out=csb[:, :], in_=d_consts))
        P.dma("sync", lambda e: e.dma_start(out=crow32[:, :], in_=d_crow))
        P.dma("sync", lambda e: e.dma_start(out=tril_sb[:, :], in_=d_tril))
        P.dma("sync", lambda e: e.dma_start(out=trilneg_sb[:, :], in_=d_trilneg))
        P.dma("sync", lambda e: e.dma_start(out=lnw_sb[:, :], in_=d_lnw))
        ld_c = ("LD", P.c("LD"))
        x_marks = {}
        for pc, eng in enumerate(("scalar", "sync", "gpsimd", "gpsimd")):
            P.dma(eng, lambda e, pc=pc: e.dma_start(
                out=xbuf[:, pc * 4 * T:(pc + 1) * 4 * T].rearrange(
                    "p (kc t) -> p kc t", kc=4),
                in_=d_xT[4 * pc:4 * pc + 4].rearrange("kc p t -> p kc t")),
                sem="R%d" % pc)
            for kc in range(4 * pc, 4 * pc + 4):
                x_marks[kc] = ("R%d" % pc, P.c("R%d" % pc))
        P.dma("gpsimd", lambda e: e.dma_start(out=csb16[:, :], in_=d_consts), sem="WD")
        P.dma("gpsimd", lambda e: e.dma_start(out=crow16[:, :], in_=d_crow), sem="WD")
        P.dma("gpsimd", lambda e: e.dma_start(out=id16[:, :], in_=d_ident), sem="WD")
        P.dma("gpsimd", lambda e: e.dma_start(
            out=a16[:, :].rearrange("p (dc r) -> p dc r", dc=KC),
            in_=d_afull.rearrange("dc p r -> p dc r")), sem="WD")
        ld0 = P.c("LD")

        wostack = ExitStack()
        wo2sb = self.sb("wo2sb", [128, 2 * KC * 128], BF16, wostack)
        a2sb = self.sb("a2sb", [128, 2 * 16], BF16, wostack)
        wotp = self.sb("wotp", [128, KC * 16], F32, wostack)
        P.dma("gpsimd", lambda e: e.dma_start(
            out=wo2sb[:, :].rearrange("p (e dc c) -> p e dc c", e=2, dc=KC),
            in_=d_wo2s.rearrange("e dc p c -> p e dc c")), sem="WD")
        P.dma("gpsimd", lambda e: e.dma_start(
            out=a2sb[:, :].rearrange("p (e r) -> p e r", e=2),
            in_=d_a2s.rearrange("e p r -> p e r")), sem="WD")
        P.dma("gpsimd", lambda e: e.dma_start(
            out=mw16[:, :].rearrange("p (kc m c) -> p kc m c", kc=KC, m=4),
            in_=d_mwqkv.rearrange("kc m p c -> p kc m c")), sem="WD")
        wd0 = P.c("WD")
        wd_mw = wd0
        wd_wo2 = wd0
        v_prev = 0
        for dc in range(KC):
            for e2 in range(2):
                P.add("tensor", lambda e, dc=dc, e2=e2: e.matmul(
                    psG[0:128, 0:16],
                    wo2sb[:, (e2 * KC + dc) * 128:(e2 * KC + dc + 1) * 128],
                    a2sb[:, e2 * 16:(e2 + 1) * 16],
                    start=(e2 == 0), stop=(e2 == 1)),
                    ([("WD", wd_wo2), ("VV", v_prev)] if e2 == 0 else []),
                    inc="TT" if e2 == 1 else None)
            tmark = P.c("TT")
            P.add("vector", lambda e, dc=dc: e.tensor_copy(
                wotp[:, dc * 16:(dc + 1) * 16], psG[:, 0:16]), [("TT", tmark)], inc="VV")
            v_prev = P.c("VV")
        P.dma("sync", lambda e: e.dma_start(
            out=d_wotai.rearrange("dc p r -> p dc r"),
            in_=wotp[:, :].rearrange("p (dc r) -> p dc r", dc=KC)),
            waits=[("VV", v_prev)], sem="WI")
        ld_wotai = P.c("WI")
        wostack.close()

        def emit_stats(src, *, with_mean, eps_col, prereq, x_marks=None):
            sc16 = (sc16a, sc16b)
            sq16 = (sq16a, sq16b)
            marks = []
            for kc in range(KC):
                w = list(prereq) if kc < 2 else [("TT", marks[kc - 2])]
                if x_marks is not None:
                    w = [x_marks[kc]] + w
                cm = P.add("vector", lambda e, kc=kc: e.tensor_copy(
                    sc16[kc % 2][:, :], src[:, kc * T:(kc + 1) * T]), w, inc="VV")
                sm = P.add("vector", lambda e, kc=kc: e.tensor_tensor(
                    sq16[kc % 2][:, :], sc16[kc % 2][:, :], sc16[kc % 2][:, :], ALU.mult), [], inc="VV")
                for h in range(2):
                    if with_mean:
                        P.add("tensor", lambda e, kc=kc, h=h: e.matmul(
                            psA[(32 * h):(32 * h + 1), 0:512], csb16[:, 0:1],
                            sc16[kc % 2][:, h * 512:(h + 1) * 512],
                            start=(kc == 0), stop=(kc == KC - 1)),
                            [("VV", cm)] if h == 0 else [])
                    P.add("tensor", lambda e, kc=kc, h=h: e.matmul(
                        psB[(32 * h):(32 * h + 1), 0:512], csb16[:, 0:1],
                        sq16[kc % 2][:, h * 512:(h + 1) * 512],
                        start=(kc == 0), stop=(kc == KC - 1)),
                        [("VV", sm)] if h == 0 else [],
                        inc="TT" if h == 1 else None)
                marks.append(P.c("TT"))
            t_sums = P.c("TT")
            if with_mean:
                P.add("vector", lambda e: e.tensor_copy(strow[0:1, 0:512], psA[0:1, 0:512]),
                      [("TT", t_sums)], inc="VV")
                P.add("vector", lambda e: e.tensor_copy(strow[0:1, 512:1024], psA[32:33, 0:512]), [], inc="VV")
            P.add("vector", lambda e: e.tensor_copy(strow[0:1, T:T + 512], psB[0:1, 0:512]),
                  [("TT", t_sums)], inc="VV")
            P.add("vector", lambda e: e.tensor_copy(strow[0:1, T + 512:2 * T], psB[32:33, 0:512]), [], inc="VV")
            if with_mean:
                P.add("vector", lambda e: e.tensor_tensor(
                    strow[0:1, 2 * T:3 * T], strow[0:1, 0:T], strow[0:1, 0:T], ALU.mult), [], inc="VV")
                P.add("vector", lambda e: e.tensor_tensor(
                    strow[0:1, T:2 * T], strow[0:1, T:2 * T], strow[0:1, 2 * T:3 * T], ALU.subtract), [], inc="VV")
            v_var = P.c("VV")
            P.add("scalar", lambda e: e.activation(
                strow[0:1, 2 * T:3 * T], strow[0:1, T:2 * T], AF.Ln,
                bias=csb[0:1, eps_col:eps_col + 1]),
                [("VV", v_var)], inc="SC")
            P.add("scalar", lambda e: e.activation(
                strow[0:1, 3 * T:4 * T], strow[0:1, 2 * T:3 * T], AF.Exp, scale=-0.5), [], inc="SC")
            s_rstd = P.c("SC")
            if with_mean:
                P.add("vector", lambda e: e.tensor_tensor(
                    strow[0:1, 2 * T:3 * T], strow[0:1, 0:T], strow[0:1, 3 * T:4 * T], ALU.mult),
                    [("SC", s_rstd)], inc="VV")
            return P.c("VV"), s_rstd

        def emit_bcasts(*, with_mean, s_rstd, v_a):
            P.add("tensor", lambda e: e.matmul(psE[:, 0:512], crow32[0:1, 0:128],
                                               strow[0:1, 3 * T:3 * T + 512], start=True, stop=True),
                  [("SC", s_rstd)])
            P.add("tensor", lambda e: e.matmul(psF[:, 0:512], crow32[0:1, 0:128],
                                               strow[0:1, 3 * T + 512:4 * T], start=True, stop=True),
                  [], inc="TT")
            if with_mean:
                P.add("tensor", lambda e: e.matmul(psC[:, 0:512], crow32[0:1, 0:128],
                                                   strow[0:1, 2 * T:2 * T + 512], start=True, stop=True),
                      [("VV", v_a)])
                P.add("tensor", lambda e: e.matmul(psD[:, 0:512], crow32[0:1, 0:128],
                                                   strow[0:1, 2 * T + 512:3 * T], start=True, stop=True),
                      [], inc="TT")
            return P.c("TT")

        v0, s0 = emit_stats(xbuf, with_mean=True, eps_col=2,
                            prereq=[ld_c, ("WD", wd0)], x_marks=x_marks)
        t_bc0 = emit_bcasts(with_mean=True, s_rstd=s0, v_a=v0)
        for h in range(2):
            rb = psE if h == 0 else psF
            ab = psC if h == 0 else psD
            for kc in range(KC):
                dst = (sc16a, sq16a)[kc % 2]
                qs = "R%d" % (2 * h + kc % 2)
                w = [("TT", t_bc0)]
                if h == 1 and kc < 2:
                    pq = "R%d" % (kc % 2)
                    w.append((pq, P.c(pq)))
                elif kc >= 2:
                    w.append((qs, P.c(qs)))
                P.add("vector", lambda e, kc=kc, h=h, rb=rb: e.tensor_tensor(
                    tmp32[:, 0:512], xbuf[:, kc * T + h * 512: kc * T + (h + 1) * 512],
                    rb[:, 0:512], ALU.mult), w, inc="VV")
                P.add("vector", lambda e, dst=dst, ab=ab: e.tensor_tensor(
                    dst[:, 0:512], tmp32[:, 0:512], ab[:, 0:512], ALU.subtract),
                    [], inc="VV")
                vm = P.c("VV")
                P.dma("sync", lambda e, kc=kc, h=h, dst=dst: e.dma_start(
                    out=d_ag1i[h][kc], in_=dst[:, 0:512]), waits=[("VV", vm)], sem=qs)
            ra, rb_ = "R%d" % (2 * h), "R%d" % (2 * h + 1)
            P.add("gpsimd", lambda e, h=h: e.collective_compute(
                "AllGather", ALU.bypass, replica_groups=[list(range(NCORE))],
                ins=[d_ag1i[h].opt()], outs=[d_ag1o[h].opt()]),
                [(ra, P.c(ra)), (rb_, P.c(rb_))], inc="CC1" + "ab"[h])
        P.add("gpsimd", lambda e: e.collective_compute(
            "AllReduce", ALU.add, replica_groups=[list(range(NCORE))],
            ins=[d_wotai.opt()], outs=[d_wotao.opt()]), [("WI", ld_wotai)], inc="CC5")
        xstack1.close()

        P.barrier()
        mstackA = ExitStack()
        aQ0 = self.sb("aQ0", [128, S], BF16, mstackA)
        aQ1 = self.sb("aQ1", [128, S], BF16, mstackA)
        aK = self.sb("aK", [128, S], BF16, mstackA)
        aV = self.sb("aV", [128, NB * VSTRIDE + 64], BF16, mstackA)
        means16 = self.sb("means16", [128, 64], BF16, mstackA)
        m3stack = ExitStack()
        lnst = [self.sb("lnst%d" % i, [128, KC * 512], BF16, m3stack) for i in range(2)]
        cosb = self.sb("cosb", [64, S], BF16, m3stack)
        sinb = self.sb("sinb", [64, S], BF16, m3stack)
        rtmp = self.sb("rtmp", [64, 4 * 512], BF16, m3stack)
        rblo = [self.sb("rblo%d" % i, [64, 512], BF16, m3stack) for i in range(2)]
        rbhi = [self.sb("rbhi%d" % i, [64, 512], BF16, m3stack) for i in range(2)]
        vtmp = self.sb("vtmp", [128, 512], BF16, m3stack)
        P.add("vector", lambda e: e.memset(
            aV[:, :NB * VSTRIDE].rearrange("p (n c) -> p n c", c=VSTRIDE)[:, :, 128:129], 1.0),
            [], inc="VV")

        v_tr_prev = [0]
        t_tr_prev = [0]

        def emit_qkv_produce(nm, d_src, cc_sem, wsb, wd_w, nroles, on_evac, strbufs):
            ev = {}
            ld_marks = {}
            tt_roles = {}
            tc_list = list(range(0, 16, 2)) + list(range(1, 16, 2))
            for idx, tc in enumerate(tc_list):
                rk, hf = tc // 2, tc % 2
                slot = idx % 2
                qj = "Q%d" % (2 + slot)
                if idx < 2:
                    w = [(cc_sem[hf], 1)]
                else:
                    w = [("TT", tt_roles.get((tc_list[idx - 2], nroles - 1), 0))]
                    if idx in (8, 9):
                        w.append((cc_sem[1], 1))
                P.dma("sync", lambda e, rk=rk, hf=hf, slot=slot: e.dma_start(
                    out=strbufs[slot][:, :].rearrange("p (kc t) -> p kc t", kc=KC),
                    in_=d_src[hf][rk].rearrange("kc p t -> p kc t")),
                    waits=w, sem=qj)
                ld_marks[tc] = (qj, P.c(qj))
                for m in range(nroles):
                    si = idx * nroles + m
                    bank = PS[si % 6]
                    prev = si - 6
                    wv = []
                    if prev >= 0:
                        wv.append(ev[(tc_list[prev // nroles], prev % nroles)])
                    for kc in range(KC):
                        if kc == 0:
                            w_mm = [ld_marks[tc], ("WD", wd_w)] + wv
                        else:
                            w_mm = []
                        P.add("tensor", lambda e, m=m, kc=kc, slot=slot, bank=bank: e.matmul(
                            bank[:, 0:512],
                            wsb[:, (kc * nroles + m) * 128:(kc * nroles + m + 1) * 128],
                            strbufs[slot][:, kc * 512:(kc + 1) * 512],
                            start=(kc == 0), stop=(kc == KC - 1)),
                            w_mm, inc="TT" if kc == KC - 1 else None)
                    tt_roles[(tc, m)] = P.c("TT")
                    ev[(tc, m)] = on_evac(tc, m, bank, tt_roles[(tc, m)])
            return ev

        def mx_evac(tc, m, bank, tmark):
            cols = slice(tc * 512, (tc + 1) * 512)
            if m < 3:
                dst = (aQ0, aQ1, aK)[m]
                cs = slice(tc * 512, (tc + 1) * 512)
                rl, rh = rblo[m % 2], rbhi[m % 2]
                P.add("scalar", lambda e, bank=bank, rl=rl: e.activation(
                    rl[:, :], bank[0:64, 0:512], AF.Copy),
                    [("TT", tmark), ld_cs, ("VV", self.rb_free.get(m % 2, 0))], inc="SC")
                P.add("scalar", lambda e, bank=bank, rh=rh: e.activation(
                    rh[:, :], bank[64:128, 0:512], AF.Copy), [], inc="SC")
                scm = P.c("SC")
                P.add("vector", lambda e, rl=rl, cs=cs: e.tensor_tensor(
                    rtmp[:, 0:512], rl[:, :], cosb[:, cs], ALU.mult),
                    [("SC", scm)], inc="VV")
                P.add("vector", lambda e, rh=rh, cs=cs: e.tensor_tensor(
                    rtmp[:, 512:1024], rh[:, :], sinb[:, cs], ALU.mult), [], inc="VV")
                P.add("vector", lambda e, rl=rl, cs=cs: e.tensor_tensor(
                    rtmp[:, 1024:1536], rl[:, :], sinb[:, cs], ALU.mult), [], inc="VV")
                P.add("vector", lambda e, rh=rh, cs=cs: e.tensor_tensor(
                    rtmp[:, 1536:2048], rh[:, :], cosb[:, cs], ALU.mult), [], inc="VV")
                self.rb_free[m % 2] = P.c("VV")
                P.add("vector", lambda e, dst=dst, cols=cols: e.tensor_tensor(
                    dst[0:64, cols], rtmp[:, 0:512], rtmp[:, 512:1024], ALU.subtract), [], inc="VV")
                P.add("vector", lambda e, dst=dst, cols=cols: e.tensor_tensor(
                    dst[64:128, cols], rtmp[:, 1024:1536], rtmp[:, 1536:2048], ALU.add), [], inc="VV")
                return ("VV", P.c("VV"))
            P.add("vector", lambda e, bank=bank: e.tensor_copy(vtmp[:, :], bank[:, 0:512]),
                  [("TT", tmark), ("TT", t_tr_prev[0])], inc="VV")
            vm = P.c("VV")
            for j in range(4):
                P.add("tensor", lambda e, j=j: e.transpose(
                    psT[:, j * 128:(j + 1) * 128], vtmp[:, j * 128:(j + 1) * 128], id16[:, :]),
                    [("VV", vm), ("VV", v_tr_prev[0])] if j == 0 else [],
                    inc="TT" if j == 3 else None)
            ttm = P.c("TT")
            t_tr_prev[0] = ttm
            for j in range(4):
                bidx = 4 * tc + j
                P.add("vector", lambda e, j=j, bidx=bidx: e.tensor_copy(
                    aV[:, bidx * VSTRIDE:bidx * VSTRIDE + 128], psT[:, j * 128:(j + 1) * 128]),
                    [("TT", ttm)] if j == 0 else [], inc="VV" if j == 3 else None)
            v_tr_prev[0] = P.c("VV")
            return ("VV", v_tr_prev[0])

        P.dma("gpsimd", lambda e: e.dma_start(out=cosb[:, :], in_=d_cossin[0:64, :]), sem="WD")
        P.dma("gpsimd", lambda e: e.dma_start(out=sinb[:, :], in_=d_cossin[64:128, :]), sem="WD")
        ld_cs = ("WD", P.c("WD"))
        ev_mx = emit_qkv_produce("mx", d_ag1o, ("CC1a", "CC1b"), mw16, wd_mw, 4, mx_evac, lnst)

        def _means_op(e):
            with nc.allow_low_precision(reason="gating block sums feed comparisons only"):
                return e.reduce_sum(
                    means16[:, :].rearrange("p (n o) -> p n o", o=1),
                    aK[:, :].rearrange("p (n w) -> p n w", w=128), axis=AX.X)
        P.add("vector", _means_op, [ev_mx[(15, 2)]], inc="VV")
        P.tap("q0", aQ0[:, 0:1024], [128, 1024], BF16)
        P.tap("k", aK[:, 0:1024], [128, 1024], BF16)
        m3stack.close()
        mwstack.close()

        SCB2 = [scU, scV]

        def pvloc(i):
            return (psE if i < 2 else psF), (0 if i % 2 == 0 else 256)

        def emit_pen_precompute(Qs, means, penbuf):
            m_s3 = 0
            m_s4 = 0
            m_s6 = 0
            m_s8 = 0
            for tb in range(14):
                t0 = 8 + 4 * tb
                w1 = [("VV", m_s3), ("VV", m_s8)]
                w2 = [("VV", m_s4)]
                if tb == 0:
                    w1 += [("VV", self.g_mi)]
                    w2 += [("VV", self.pv_free[0]), ("VV", self.pv_free[1])]
                for i in range(4):
                    P.add("tensor", lambda e, t=t0 + i, i=i: e.matmul(
                        psG[0:1, i * 128:(i + 1) * 128], means[:, t:t + 1],
                        Qs[:, t * 128:(t + 1) * 128], start=True, stop=True),
                        w1 if i == 0 else [], inc=None)
                for i in range(4):
                    P.add("tensor", lambda e, t=t0 + i, i=i: e.matmul(
                        psE[0:64, i * 128:(i + 1) * 128], means[:, 0:64],
                        Qs[:, t * 128:(t + 1) * 128], start=True, stop=True),
                        w2 if i == 0 else [],
                        inc="TT" if i == 3 else None)
                m_s12 = P.c("TT")
                P.add("vector", lambda e: e.tensor_copy(grow4[0:1, :], psG[0:1, 0:512]),
                      [("TT", m_s12)], inc="VV")
                m_s3 = P.c("VV")
                P.add("vector", lambda e: e.tensor_copy(gt32_4[:, :], psE[0:64, 0:512]),
                      [], inc="VV")
                m_s4 = P.c("VV")
                w5 = [("VV", m_s4), ("VV", m_s6)]
                if tb == 0:
                    w5 += [("VV", self.pv_free[2]), ("VV", self.pv_free[3])]
                for i in range(4):
                    P.add("tensor", lambda e, i=i: e.matmul(
                        psF[0:64, i * 128:(i + 1) * 128], crow16[0:1, 0:64],
                        grow4[0:1, i * 128:(i + 1) * 128], start=True, stop=True),
                        w5 if i == 0 else [],
                        inc="TT" if i == 3 else None)
                m_s5 = P.c("TT")
                P.add("vector", lambda e: e.tensor_tensor(
                    cmp4[:, :], gt32_4[:, :], psF[0:64, 0:512], ALU.is_ge),
                    [("TT", m_s5)], inc="VV")
                m_s6 = P.c("VV")
                for i in range(4):
                    P.add("tensor", lambda e, t=t0 + i, i=i: e.matmul(
                        psG[0:1, i * 128:(i + 1) * 128], csb16[0:t + 1, 1:2],
                        cmp4[0:t + 1, i * 128:(i + 1) * 128], start=True, stop=True),
                        [("VV", m_s6)] if i == 0 else [],
                        inc="TT" if i == 3 else None)
                m_s7 = P.c("TT")
                P.add("vector", lambda e, t0=t0: e.tensor_scalar(
                    penbuf[0:1, (t0 - 8) * 128:(t0 - 4) * 128], psG[0:1, 0:512],
                    8.5, NEG_PEN, ALU.is_gt, ALU.mult), [("TT", m_s7)], inc="VV")
                m_s8 = P.c("VV")
            m_mi = 0
            for tb in range(14):
                for i in range(4):
                    t = 8 + 4 * tb + i
                    P.add("tensor", lambda e, t=t: e.matmul(
                        psG[:, 256:384], crow16[0:1, 0:128],
                        penbuf[0:1, (t - 8) * 128:(t - 7) * 128],
                        start=True, stop=True),
                        [("VV", m_s8 if (tb == 0 and i == 0) else 0), ("VV", m_mi)],
                        inc="TT")
                    m_bc = P.c("TT")
                    P.add("vector", lambda e, t=t: e.tensor_tensor(
                        mskbuf[:, (t - 8) * 128:(t - 7) * 128], tril_sb[:, :],
                        psG[:, 256:384], ALU.mult), [("TT", m_bc)], inc="VV")
                    m_mi = P.c("VV")
            self.pv_free[0] = max(self.pv_free[0], m_s4)
            self.pv_free[1] = max(self.pv_free[1], m_s4)
            self.pv_free[2] = max(self.pv_free[2], m_s6)
            self.pv_free[3] = max(self.pv_free[3], m_s6)
            self.g_mi = m_mi
            return P.c("VV")

        def emit_attention(Qs, Ks, Vtm, outb, ptb, *, mixer, means, penbuf=None):
            if mixer:
                v_pen = emit_pen_precompute(Qs, means, penbuf)
            pending = []
            norm_q = []

            def emit_norm_step():
                if not norm_q:
                    return
                i, t, vm = norm_q.pop(0)
                P.add("tensor", lambda e, i=i: e.transpose(
                    psT[:, i * 128:(i + 1) * 128], otm16[:, i * 128:(i + 1) * 128], id16[:, :]),
                    [("VV", vm), ("VV", v_tr_prev[0])], inc="TT")
                tm2 = P.c("TT")
                P.add("vector", lambda e, i=i, t=t: e.tensor_copy(
                    outb[:, t * 128:(t + 1) * 128], psT[:, i * 128:(i + 1) * 128]),
                    [("TT", tm2)], inc="VV")
                v_tr_prev[0] = P.c("VV")

            def emit_pv(ctx):
                slot, grp, rr, smark = ctx
                for j, (i, b, pj) in enumerate(grp):
                    t = 4 * rr + i
                    last = (b == t) if not (mixer and rr < 2) else (b == 7)
                    pvb, off = pvloc(i)
                    wv = [("SC", smark)] if j == 0 else []
                    if b == 0:
                        wv.append(("VV", self.pv_free[i]))
                        if i % 2 == 0:
                            wv.append(("VV", self.pv_free[i + 1]))
                    P.add("tensor", lambda e, b=b, pj=pj, slot=slot, last=last,
                          i=i, pvb=pvb, off=off: e.matmul(
                        pvb[:, off:off + 129],
                        ptb[:, slot * 1024 + pj * 128: slot * 1024 + (pj + 1) * 128],
                        Vtm[:, b * VSTRIDE:b * VSTRIDE + 129],
                        start=(b == 0 and i % 2 == 0), stop=last),
                        wv, inc="TT" if j == len(grp) - 1 else None)
                self.slot_pv[slot] = P.c("TT")

            def push_pv(ctx):
                pending.append(ctx)
                if len(pending) > 2:
                    emit_pv(pending.pop(0))

            def flush():
                while pending:
                    emit_pv(pending.pop(0))

            for r in range(16):
                n_common = 8 if (mixer and r < 2) else 4 * r
                qcols = slice(512 * r, 512 * r + 512)
                for pb in range(n_common // 2):
                    b0 = 2 * pb
                    g = self.gidx; self.gidx += 1
                    sc = SCB2[g % 2]
                    slot = g % 3
                    P.add("tensor", lambda e, b0=b0, sc=sc, qcols=qcols: e.matmul(
                        sc[:, 0:512], Ks[:, b0 * 128:(b0 + 1) * 128], Qs[:, qcols],
                        start=True, stop=True),
                        [("SC", self.bank_exp[g % 2])], inc=None)
                    P.add("tensor", lambda e, b0=b0, sc=sc, qcols=qcols: e.matmul(
                        sc[:, 512:1024], Ks[:, (b0 + 1) * 128:(b0 + 2) * 128], Qs[:, qcols],
                        start=True, stop=True), [], inc="TT")
                    tmark = P.c("TT")
                    emit_norm_step()
                    P.add("scalar", lambda e, sc=sc, slot=slot: e.activation(
                        ptb[:, slot * 1024:(slot + 1) * 1024], sc[:, 0:1024], AF.Exp, scale=ISD),
                        [("TT", tmark), ("TT", self.slot_pv[slot])], inc="SC")
                    self.bank_exp[g % 2] = P.c("SC")
                    push_pv((slot,
                             [(i, b0, i) for i in (0, 2, 1, 3)]
                             + [(i, b0 + 1, 4 + i) for i in (0, 2, 1, 3)],
                             r, P.c("SC")))
                if not (mixer and r < 2):
                    pieces = [(i, 4 * r + b) for i in range(4) for b in range(i + 1)]
                    groups = [pieces[0:8], pieces[8:10]]
                    rag_slots = []
                    for grp in groups:
                        g = self.gidx; self.gidx += 1
                        sc = SCB2[g % 2]
                        slot = g % 3
                        rag_slots.append(slot)
                        for j, (i, b) in enumerate(grp):
                            t = 4 * r + i
                            wv = [("SC", self.bank_exp[g % 2])] if j == 0 else []
                            P.add("tensor", lambda e, b=b, j=j, t=t, sc=sc: e.matmul(
                                sc[:, j * 128:(j + 1) * 128],
                                Ks[:, b * 128:(b + 1) * 128], Qs[:, t * 128:(t + 1) * 128],
                                start=True, stop=True),
                                wv, inc="TT" if j == len(grp) - 1 else None)
                        gmark = P.c("TT")
                        vmask = 0
                        for j, (i, b) in enumerate(grp):
                            t = 4 * r + i
                            if b != t:
                                continue
                            if not mixer:
                                P.add("vector", lambda e, j=j, sc=sc: e.tensor_tensor(
                                    sc[:, j * 128:(j + 1) * 128], sc[:, j * 128:(j + 1) * 128],
                                    trilneg_sb[:, :], ALU.add), [("TT", gmark)], inc="VV")
                                vmask = P.c("VV")
                            else:
                                P.add("vector", lambda e, j=j, t=t, sc=sc: e.tensor_tensor(
                                    sc[:, j * 128:(j + 1) * 128], sc[:, j * 128:(j + 1) * 128],
                                    mskbuf[:, (t - 8) * 128:(t - 7) * 128], ALU.add),
                                    [("TT", gmark), ("VV", v_pen)], inc="VV")
                                vmask = P.c("VV")
                        wexp = [("TT", gmark), ("TT", self.slot_pv[slot])]
                        if vmask:
                            wexp.append(("VV", vmask))
                        P.add("scalar", lambda e, sc=sc, slot=slot, n=len(grp): e.activation(
                            ptb[:, slot * 1024: slot * 1024 + n * 128], sc[:, 0:n * 128],
                            AF.Exp, scale=ISD), wexp, inc="SC")
                        self.bank_exp[g % 2] = P.c("SC")
                        push_pv((slot, [(i, b, j) for j, (i, b) in enumerate(grp)],
                                 r, P.c("SC")))
                flush()
                while norm_q:
                    emit_norm_step()
                if not (mixer and r < 2):
                    tA = self.slot_pv[rag_slots[0]]
                    tB = self.slot_pv[rag_slots[1]]
                else:
                    tA = tB = P.c("TT")
                for lo, hi, tm in ((0, 2, tA), (2, 4, tB)):
                    rms = []
                    for i in range(lo, hi):
                        pvb, off = pvloc(i)
                        P.add("vector", lambda e, i=i, pvb=pvb, off=off: e.reciprocal(
                            rc_sb[:, i:i + 1], pvb[:, off + 128:off + 129]),
                            [("TT", tm)], inc="VV")
                        rms.append(P.c("VV"))
                    for i in range(lo, hi):
                        t = 4 * r + i
                        pvb, off = pvloc(i)
                        P.add("vector", lambda e, i=i, pvb=pvb, off=off: e.tensor_scalar(
                            otm16[:, i * 128:(i + 1) * 128], pvb[:, off:off + 128],
                            rc_sb[:, i:i + 1], None, ALU.mult),
                            [("VV", rms[1])], inc="VV")
                        self.pv_free[i] = P.c("VV")
                        norm_q.append((i, t, P.c("VV")))
            while norm_q:
                emit_norm_step()

        P.barrier()
        mstackB = ExitStack()
        pt_sb = self.sb("pt_sb", [128, 3 * 1024], BF16, mstackB)
        outh0 = self.sb("outh0", [128, S], BF16, mstackB)
        outh1 = self.sb("outh1", [128, S], BF16, mstackB)
        penbuf = self.sb("penbuf", [1, 56 * 128], BF16, mstackB)
        mskbuf = self.sb("mskbuf", [128, 56 * 128], F32, mstackB)
        def stage_a2a(outb, d_in, d_out, cc, sg):
            v_att = P.c("VV")
            for r in range(NCORE):
                P.dma("sync", lambda e, r=r, outb=outb, d_in=d_in: e.dma_start(
                    out=d_in[r], in_=outb[:, r * T:(r + 1) * T]),
                    waits=[("VV", v_att)] if r == 0 else [], sem=sg)
            ldm = P.c(sg)

            def _op(e, d_in=d_in, d_out=d_out):
                with allow_ici_all_to_all():
                    return e.collective_compute(
                        "AllToAll", ALU.bypass, replica_groups=[list(range(NCORE))],
                        ins=[d_in.opt()], outs=[d_out.opt()])
            P.add("gpsimd", _op, [(sg, ldm)], inc=cc)

        emit_attention(aQ0, aK, aV, outh0, pt_sb, mixer=True, means=means16, penbuf=penbuf)
        stage_a2a(outh0, d_a2a1i[0], d_a2a1o[0], "CC2a", "SG1")
        emit_attention(aQ1, aK, aV, outh1, pt_sb, mixer=True, means=means16, penbuf=penbuf)
        stage_a2a(outh1, d_a2a1i[1], d_a2a1o[1], "CC2b", "SG2")
        mstackB.close()
        mstackA.close()

        P.barrier()
        wstack = ExitStack()
        att1 = self.sb("att1", [128, KC * T], BF16, wstack)
        wosb = [self.sb("wosb%d" % i, [128, KC * 128], BF16, wstack) for i in range(4)]
        xstack2 = ExitStack()
        xb2 = self.sb("xb2", [128, KC * T], F32, xstack2, side="right")
        P.dma("sync", lambda e: e.dma_start(
            out=xb2[:, :].rearrange("p (kc t) -> p kc t", kc=KC),
            in_=d_xT.rearrange("kc p t -> p kc t")), sem="XB")
        xb_ld = P.c("XB")
        for r in range(NCORE):
            P.dma("sync", lambda e, r=r: e.dma_start(
                out=att1[:, (2 * r) * T:(2 * r + 1) * T], in_=d_a2a1o[0][r]),
                waits=[("CC2a", 1)] if r == 0 else [], sem="YA")
        ld_att1A = ("YA", P.c("YA"))
        for r in range(NCORE):
            P.dma("sync", lambda e, r=r: e.dma_start(
                out=att1[:, (2 * r + 1) * T:(2 * r + 2) * T], in_=d_a2a1o[1][r]),
                waits=[("CC2b", 1)] if r == 0 else [], sem="YB")
        ld_att1B = ("YB", P.c("YB"))
        wo_tm = {}
        wo_ev = {}
        cast_mk = {}
        sq_mk = {}
        stat_tm = {}
        wo_seq = [(h, ec) for h in range(2) for ec in range(KC)]

        def emit_stat_mms(s2):
            h2, ec2 = wo_seq[s2]
            cb = (sc16a, sc16b)[(s2 % 4) // 2]
            qb = (sq16a, sq16b)[(s2 % 4) // 2]
            co = (s2 % 2) * 512
            P.add("tensor", lambda e, h2=h2, ec2=ec2, cb=cb, co=co: e.matmul(
                psG[64 * h2:64 * h2 + 1, 0:512], csb16[:, 0:1], cb[:, co:co + 512],
                start=(ec2 == 0), stop=(ec2 == KC - 1)),
                [("VV", cast_mk[s2])])
            if h2 == 0:
                sq_out, sq_st = psG[32:33, 0:512], (ec2 == 0)
            else:
                sq_out, sq_st = psE[0:1, 0:512], (ec2 == 0)
            P.add("tensor", lambda e, ec2=ec2, qb=qb, co=co, sq_out=sq_out, sq_st=sq_st: e.matmul(
                sq_out, csb16[:, 0:1], qb[:, co:co + 512],
                start=sq_st, stop=(ec2 == KC - 1)),
                [("VV", sq_mk[s2])], inc="TT")
            stat_tm[s2] = P.c("TT")

        for s, (h, ec) in enumerate(wo_seq):
            us = "U%d" % (s % 4)
            P.dma("gpsimd", lambda e, ec=ec, s=s: e.dma_start(
                out=wosb[s % 4][:, :].rearrange("p (kc c) -> p kc c", kc=KC),
                in_=d_wot[ec].rearrange("kc p c -> p kc c")),
                waits=[("TT", wo_tm.get(s - 4, 0))], sem=us)
            wdm = (us, P.c(us))
            kco = list(range(0, KC, 2)) + list(range(1, KC, 2))
            bank = (psA, psB, psC, psD)[s % 4]
            for ki, kc in enumerate(kco):
                wv = []
                if ki == 0:
                    wv = [wdm, ld_att1A, ("VV", wo_ev.get(s - 4, 0))]
                elif ki == 8:
                    wv = [ld_att1B]
                P.add("tensor", lambda e, ec=ec, h=h, kc=kc, ki=ki, s=s, bank=bank: e.matmul(
                    bank[:, 0:512], wosb[s % 4][:, kc * 128:(kc + 1) * 128],
                    att1[:, kc * T + h * 512: kc * T + (h + 1) * 512],
                    start=(ki == 0), stop=(ki == KC - 1)),
                    wv, inc="TT" if ki == KC - 1 else None)
            tmark = P.c("TT")
            wo_tm[s] = tmark
            P.add("vector", lambda e, ec=ec, h=h, bank=bank: e.tensor_tensor(
                xb2[:, ec * T + h * 512: ec * T + (h + 1) * 512], bank[:, 0:512],
                xb2[:, ec * T + h * 512: ec * T + (h + 1) * 512], ALU.add),
                [("TT", tmark), ("XB", xb_ld)], inc="VV")
            wo_ev[s] = P.c("VV")
            cb = (sc16a, sc16b)[(s % 4) // 2]
            qb = (sq16a, sq16b)[(s % 4) // 2]
            co = (s % 2) * 512
            P.add("vector", lambda e, ec=ec, h=h, cb=cb, co=co: e.tensor_copy(
                cb[:, co:co + 512], xb2[:, ec * T + h * 512: ec * T + (h + 1) * 512]),
                [("TT", stat_tm.get(s - 4, 0))], inc="VV")
            cast_mk[s] = P.c("VV")
            P.add("vector", lambda e, cb=cb, qb=qb, co=co: e.tensor_tensor(
                qb[:, co:co + 512], cb[:, co:co + 512], cb[:, co:co + 512], ALU.mult),
                [], inc="VV")
            sq_mk[s] = P.c("VV")
            if s >= 1:
                emit_stat_mms(s - 1)
        emit_stat_mms(2 * KC - 1)
        v_wo_all = P.c("VV")
        P.add("vector", lambda e: e.tensor_copy(
            strow[0:1, T + 512:2 * T], psE[0:1, 0:512]),
            [("TT", stat_tm[2 * KC - 1])], inc="VV")
        v_ex2h1 = P.c("VV")
        wstack.close()

        h16stack = ExitStack()
        h16 = self.sb("h16", [128, KC * T], BF16, h16stack)
        sc_last = {}
        for h in range(2):
            hs_ = h * 512
            rbb = psE if h == 0 else psC
            abb = psF if h == 0 else psD
            sp_sem = "SP" if h == 0 else "SP2"
            w0 = [("TT", stat_tm[2 * KC - 1])]
            P.add("vector", lambda e, hs_=hs_, h=h: e.tensor_copy(
                strow[0:1, hs_:hs_ + 512], psG[64 * h:64 * h + 1, 0:512]), w0, inc="VV")
            if h == 0:
                P.add("vector", lambda e: e.tensor_copy(
                    strow[0:1, T:T + 512], psG[32:33, 0:512]), [], inc="VV")
            P.add("vector", lambda e, hs_=hs_: e.tensor_tensor(
                strow[0:1, 2 * T + hs_:2 * T + hs_ + 512], strow[0:1, hs_:hs_ + 512],
                strow[0:1, hs_:hs_ + 512], ALU.mult), [], inc="VV")
            P.add("vector", lambda e, hs_=hs_: e.tensor_tensor(
                strow[0:1, T + hs_:T + hs_ + 512], strow[0:1, T + hs_:T + hs_ + 512],
                strow[0:1, 2 * T + hs_:2 * T + hs_ + 512], ALU.subtract), [], inc="VV")
            v_var = P.c("VV")
            P.tap("strow%d" % h, strow[0:1, :], [1, 4 * T], F32)
            P.add("scalar", lambda e, hs_=hs_: e.activation(
                strow[0:1, 2 * T + hs_:2 * T + hs_ + 512], strow[0:1, T + hs_:T + hs_ + 512],
                AF.Ln, bias=csb[0:1, 2:3]), [("VV", v_var)], inc="SC")
            P.add("scalar", lambda e, hs_=hs_: e.activation(
                strow[0:1, 3 * T + hs_:3 * T + hs_ + 512],
                strow[0:1, 2 * T + hs_:2 * T + hs_ + 512], AF.Exp, scale=-0.5), [], inc="SC")
            s_r = P.c("SC")
            P.add("vector", lambda e, hs_=hs_: e.tensor_tensor(
                strow[0:1, 2 * T + hs_:2 * T + hs_ + 512], strow[0:1, hs_:hs_ + 512],
                strow[0:1, 3 * T + hs_:3 * T + hs_ + 512], ALU.mult), [("SC", s_r)], inc="VV")
            v_a = P.c("VV")
            wb1 = [("SC", s_r)] + ([("VV", v_wo_all)] if h == 1 else [("VV", v_ex2h1)])
            P.add("tensor", lambda e, hs_=hs_, rbb=rbb: e.matmul(
                rbb[:, 0:512], crow32[0:1, 0:128],
                strow[0:1, 3 * T + hs_:3 * T + hs_ + 512], start=True, stop=True), wb1)
            wb2 = [("VV", v_a)] + ([("VV", v_wo_all)] if h == 1 else [])
            P.add("tensor", lambda e, hs_=hs_, abb=abb: e.matmul(
                abb[:, 0:512], crow32[0:1, 0:128],
                strow[0:1, 2 * T + hs_:2 * T + hs_ + 512], start=True, stop=True),
                wb2, inc="TT")
            t_bch = P.c("TT")
            for kc in range(KC):
                P.dma("sync", lambda e, kc=kc, hs_=hs_: e.dma_start(
                    out=d_x1sp[kc][:, hs_:hs_ + 512],
                    in_=xb2[:, kc * T + hs_:kc * T + hs_ + 512]),
                    waits=[("VV", wo_ev[h * KC + kc])], sem=sp_sem)
            sp_half = P.c(sp_sem)
            ap_mk = {}
            for kc in range(KC):
                w = []
                if kc == 0:
                    w = [("TT", t_bch), (sp_sem, sp_half)]
                    if h == 1:
                        w.append(("SC", sc_last[0]))
                P.add("vector", lambda e, kc=kc, hs_=hs_, rbb=rbb: e.tensor_tensor(
                    tmp32[:, 0:512], xb2[:, kc * T + hs_:kc * T + hs_ + 512],
                    rbb[:, 0:512], ALU.mult), w, inc="VV")
                P.add("vector", lambda e, kc=kc, hs_=hs_, abb=abb: e.tensor_tensor(
                    xb2[:, kc * T + hs_:kc * T + hs_ + 512], tmp32[:, 0:512],
                    abb[:, 0:512], ALU.subtract), [], inc="VV")
                ap_mk[kc] = P.c("VV")
                P.dma("sync", lambda e, kc=kc, hs_=hs_: e.dma_start(
                    out=d_hssp[kc][:, hs_:hs_ + 512],
                    in_=xb2[:, kc * T + hs_:kc * T + hs_ + 512]),
                    waits=[("VV", ap_mk[kc])], sem=sp_sem)
            P.add("vector", lambda e, hs_=hs_: e.tensor_scalar(
                strow[0:1, 2 * T + hs_:2 * T + hs_ + 512],
                strow[0:1, T + hs_:T + hs_ + 512], 1e-5, None, ALU.add), [], inc="VV")
            P.add("vector", lambda e, hs_=hs_: e.reciprocal(
                strow[0:1, hs_:hs_ + 512], strow[0:1, 2 * T + hs_:2 * T + hs_ + 512]),
                [], inc="VV")
            P.add("vector", lambda e, hs_=hs_: e.tensor_tensor(
                strow[0:1, 2 * T + hs_:2 * T + hs_ + 512], strow[0:1, T + hs_:T + hs_ + 512],
                strow[0:1, hs_:hs_ + 512], ALU.mult), [], inc="VV")
            v_rr = P.c("VV")
            P.add("scalar", lambda e, hs_=hs_: e.activation(
                strow[0:1, hs_:hs_ + 512], strow[0:1, 2 * T + hs_:2 * T + hs_ + 512],
                AF.Ln, bias=csb[0:1, 3:4]), [("VV", v_rr)], inc="SC")
            P.add("scalar", lambda e, hs_=hs_: e.activation(
                strow[0:1, 3 * T + hs_:3 * T + hs_ + 512], strow[0:1, hs_:hs_ + 512],
                AF.Exp, scale=-0.5), [], inc="SC")
            s_r2 = P.c("SC")
            P.add("tensor", lambda e, hs_=hs_, rbb=rbb: e.matmul(
                rbb[:, 0:512], crow32[0:1, 0:128],
                strow[0:1, 3 * T + hs_:3 * T + hs_ + 512], start=True, stop=True),
                [("SC", s_r2), ("VV", ap_mk[KC - 1])], inc="TT")
            t_b2 = P.c("TT")
            sc_mk = {}
            for kc in range(KC):
                wv = [("TT", t_b2)] if kc == 0 else []
                if kc >= 2:
                    wv.append(("SC", sc_mk[kc - 2]))
                P.add("vector", lambda e, kc=kc, hs_=hs_, rbb=rbb: e.tensor_tensor(
                    tmp32[:, (kc % 2) * 512:(kc % 2) * 512 + 512],
                    xb2[:, kc * T + hs_:kc * T + hs_ + 512], rbb[:, 0:512], ALU.mult),
                    wv, inc="VV")
                vmk = P.c("VV")
                stg = (sc16a, sq16a)[kc % 2]
                qs = "R%d" % (2 * h + kc % 2)
                ws = [("VV", vmk)]
                if h == 0 and kc < 2:
                    ws.append(("TT", stat_tm[2 * KC - 1]))
                if kc >= 2:
                    ws.append((qs, P.c(qs)))
                elif h == 1:
                    pq = "R%d" % (kc % 2)
                    ws.append((pq, P.c(pq)))
                P.add("scalar", lambda e, kc=kc, stg=stg: e.activation(
                    stg[:, 0:512], tmp32[:, (kc % 2) * 512:(kc % 2) * 512 + 512],
                    AF.Copy, scale=lnw_sb[:, kc:kc + 1]), ws, inc="SC")
                P.add("scalar", lambda e, kc=kc, hs_=hs_: e.activation(
                    h16[:, kc * T + hs_:kc * T + hs_ + 512],
                    tmp32[:, (kc % 2) * 512:(kc % 2) * 512 + 512],
                    AF.Copy, scale=lnw_sb[:, KC + kc:KC + kc + 1]), [], inc="SC")
                sc_mk[kc] = P.c("SC")
                P.dma("sync", lambda e, kc=kc, h=h, stg=stg: e.dma_start(
                    out=d_ag2i[h][kc], in_=stg[:, 0:512]),
                    waits=[("SC", sc_mk[kc])], sem=qs)
            sc_last[h] = sc_mk[KC - 1]
            ra, rb_ = "R%d" % (2 * h), "R%d" % (2 * h + 1)
            P.add("gpsimd", lambda e, h=h: e.collective_compute(
                "AllGather", ALU.bypass, replica_groups=[list(range(NCORE))],
                ins=[d_ag2i[h].opt()], outs=[d_ag2o[h].opt()]),
                [(ra, P.c(ra)), (rb_, P.c(rb_))], inc="CC3" + "ab"[h])
        P.tap("x1", d_x1sp, [KC, 128, T], F32)
        P.tap("hs", d_hssp, [KC, 128, T], F32)
        xstack2.close()
        scstack.close()

        P.barrier()
        a2stack = ExitStack()
        q2 = [self.sb("q2_%d" % i, [128, S], BF16, a2stack) for i in range(2)]
        k2 = [self.sb("k2_%d" % i, [128, S], BF16, a2stack) for i in range(2)]
        v2 = [self.sb("v2_%d" % i, [128, NB * VSTRIDE + 64], BF16, a2stack) for i in range(2)]
        w8stack = ExitStack()
        w216 = self.sb("w216", [128, KC * 6 * 128], BF16, w8stack)
        xst = [self.sb("xst%d" % i, [128, KC * 512], BF16, w8stack) for i in range(2)]
        vtmp2 = self.sb("vtmp2", [128, 512], BF16, w8stack)
        P.dma("gpsimd", lambda e: e.dma_start(
            out=w216[:, :].rearrange("p (kc m c) -> p kc m c", kc=KC, m=6),
            in_=d_w2qkv.rearrange("kc m p c -> p kc m c")), sem="WD")
        wd_w2 = P.c("WD")
        for hh in range(2):
            P.add("vector", lambda e, hh=hh: e.memset(
                v2[hh][:, :NB * VSTRIDE].rearrange("p (n c) -> p n c", c=VSTRIDE)[:, :, 128:129], 1.0),
                [], inc="VV")

        def a2_evac(tc, m, bank, tmark):
            hh, role = m // 3, m % 3
            cols = slice(tc * 512, (tc + 1) * 512)
            if role == 0:
                P.add("scalar", lambda e, hh=hh, cols=cols, bank=bank: e.activation(
                    q2[hh][:, cols], bank[:, 0:512], AF.Copy), [("TT", tmark)], inc="SC")
                return ("SC", P.c("SC"))
            if role == 1:
                P.add("scalar", lambda e, hh=hh, cols=cols, bank=bank: e.activation(
                    k2[hh][:, cols], bank[:, 0:512], AF.Copy), [("TT", tmark)], inc="SC")
                return ("SC", P.c("SC"))
            P.add("vector", lambda e, bank=bank: e.tensor_copy(vtmp2[:, :], bank[:, 0:512]),
                  [("TT", tmark), ("TT", t_tr_prev[0])], inc="VV")
            vm = P.c("VV")
            for j in range(4):
                P.add("tensor", lambda e, j=j: e.transpose(
                    psT[:, j * 128:(j + 1) * 128], vtmp2[:, j * 128:(j + 1) * 128], id16[:, :]),
                    [("VV", vm), ("VV", v_tr_prev[0])] if j == 0 else [],
                    inc="TT" if j == 3 else None)
            ttm = P.c("TT")
            t_tr_prev[0] = ttm
            for j in range(4):
                bidx = 4 * tc + j
                P.add("vector", lambda e, hh=hh, j=j, bidx=bidx: e.tensor_copy(
                    v2[hh][:, bidx * VSTRIDE:bidx * VSTRIDE + 128], psT[:, j * 128:(j + 1) * 128]),
                    [("TT", ttm)] if j == 0 else [], inc="VV" if j == 3 else None)
            v_tr_prev[0] = P.c("VV")
            return ("VV", v_tr_prev[0])

        emit_qkv_produce("a2", d_ag2o, ("CC3a", "CC3b"), w216, wd_w2, 6, a2_evac, xst)
        w8stack.close()

        P.barrier()
        pt2 = self.sb("pt2", [128, 3 * 1024], BF16, a2stack)
        out20 = self.sb("out20", [128, S], BF16, a2stack)
        out21 = self.sb("out21", [128, S], BF16, a2stack)
        emit_attention(q2[0], k2[0], v2[0], out20, pt2, mixer=False, means=None)
        stage_a2a(out20, d_a2a2i[0], d_a2a2o[0], "CC4a", "SG3")
        emit_attention(q2[1], k2[1], v2[1], out21, pt2, mixer=False, means=None)
        stage_a2a(out21, d_a2a2i[1], d_a2a2o[1], "CC4b", "SG4")
        a2stack.close()

        P.barrier()
        ph10stack = ExitStack()
        att2 = self.sb("att2", [128, KC * T], BF16, ph10stack)
        P.dma("gpsimd", lambda e: e.dma_start(
            out=wota16[:, :].rearrange("p (dc r) -> p dc r", dc=KC),
            in_=d_wotao.rearrange("dc p r -> p dc r")), waits=[("CC5", 1)], sem="WT")
        wt_ld = P.c("WT")
        for r in range(NCORE):
            P.dma("sync", lambda e, r=r: e.dma_start(
                out=att2[:, (2 * r) * T:(2 * r + 1) * T], in_=d_a2a2o[0][r]),
                waits=[("CC4a", 1)] if r == 0 else [], sem="YA")
        ld_att2A = ("YA", P.c("YA"))
        for r in range(NCORE):
            P.dma("sync", lambda e, r=r: e.dma_start(
                out=att2[:, (2 * r + 1) * T:(2 * r + 2) * T], in_=d_a2a2o[1][r]),
                waits=[("CC4b", 1)] if r == 0 else [], sem="YB")
        ld_att2B = ("YB", P.c("YB"))
        dco = list(range(0, KC, 2)) + list(range(1, KC, 2))
        for h in range(2):
            bank = (psA, psB)[h]
            for di, dc in enumerate(dco):
                wv = []
                if di == 0:
                    wv = [ld_att2A, ("WT", wt_ld)]
                elif di == 8:
                    wv = [ld_att2B]
                P.add("tensor", lambda e, h=h, dc=dc, di=di, bank=bank: e.matmul(
                    bank[0:16, 0:512], wota16[:, dc * 16:(dc + 1) * 16],
                    att2[:, dc * T + h * 512: dc * T + (h + 1) * 512],
                    start=(di == 0), stop=(di == KC - 1)),
                    wv, inc="TT" if di == KC - 1 else None)
            tmark = P.c("TT")
            P.add("scalar", lambda e, h=h, bank=bank: e.activation(
                m1_32[0:16, h * 512:(h + 1) * 512], bank[0:16, 0:512], AF.Tanh),
                [("TT", tmark)], inc="SC")
        s_tanh = P.c("SC")
        for h in range(2):
            bank = (psE, psF)[h]
            for dc in range(KC):
                P.add("tensor", lambda e, h=h, dc=dc, bank=bank: e.matmul(
                    bank[0:16, 0:512], a16[:, dc * 16:(dc + 1) * 16],
                    h16[:, dc * T + h * 512: dc * T + (h + 1) * 512],
                    start=(dc == 0), stop=(dc == KC - 1)), [],
                    inc="TT" if dc == KC - 1 else None)
            tmark = P.c("TT")
            P.add("vector", lambda e, h=h, bank=bank: e.scalar_tensor_tensor(
                modhA[0:16, h * 512:(h + 1) * 512], bank[0:16, 0:512], 0.1,
                m1_32[0:16, h * 512:(h + 1) * 512], ALU.mult, ALU.mult),
                [("TT", tmark), ("SC", s_tanh)], inc="VV")
        v_modhA = P.c("VV")
        P.tap("modhA", modhA[0:16, :], [16, T], BF16)
        ph10stack.close()

        P.barrier()
        gustack = ExitStack()
        gu = self.sb("gu", [128, FC * T], BF16, gustack, side="right")
        ph11stack = ExitStack()
        winb = [self.sb("winb%d" % i, [128, KC * 128], BF16, ph11stack) for i in range(2)]
        wupb = [self.sb("wupb%d" % i, [128, KC * 128], BF16, ph11stack) for i in range(2)]
        sil32 = [self.sb("sil32_%d" % i, [128, 512], F32, ph11stack) for i in range(2)]
        b16s = [self.sb("b16s%d" % i, [16, 16 * 128], BF16, ph11stack) for i in range(2)]
        b_marks = {}
        f_tm = {}
        f_sil = {}
        f_guv = {}

        def load_bstrip(g4):
            us = "U%d" % (2 + g4 % 2)
            P.dma("gpsimd", lambda e, g4=g4: e.dma_start(
                out=b16s[g4 % 2][0:16, :].rearrange("p (fc c) -> p fc c", fc=16),
                in_=d_bmat[g4 * 16:(g4 + 1) * 16].rearrange("fc p c -> p fc c")),
                waits=[("TT", f_tm.get((g4 - 1) * 16 - 1, 0))], sem=us)
            b_marks[g4] = (us, P.c(us))

        load_bstrip(0)
        load_bstrip(1)
        for fc in range(FC):
            if fc >= 16 and fc % 16 == 0 and (fc // 16 + 1) < 4:
                load_bstrip(fc // 16 + 1)
            us = "U%d" % (fc % 2)
            P.dma("gpsimd", lambda e, fc=fc: e.dma_start(
                out=winb[fc % 2][:, :].rearrange("p (kc c) -> p kc c", kc=KC),
                in_=d_win[fc].rearrange("kc p c -> p kc c")),
                waits=[("TT", f_tm.get(fc - 2, 0))], sem=us)
            P.dma("gpsimd", lambda e, fc=fc: e.dma_start(
                out=wupb[fc % 2][:, :].rearrange("p (kc c) -> p kc c", kc=KC),
                in_=d_wup[fc].rearrange("kc p c -> p kc c")), sem=us)
            wdm = (us, P.c(us))
            g4 = fc // 16
            for h in range(2):
                gbank = (psA, psB)[h]
                ubank = (psC, psD)[h]
                for kc in range(KC):
                    wv = [wdm, b_marks[g4],
                          ("SC", f_sil.get((fc - 1, h), 0)), ("VV", v_modhA)] if kc == 0 else []
                    P.add("tensor", lambda e, fc=fc, h=h, kc=kc, gbank=gbank: e.matmul(
                        gbank[:, 0:512], winb[fc % 2][:, kc * 128:(kc + 1) * 128],
                        h16[:, kc * T + h * 512: kc * T + (h + 1) * 512],
                        start=(kc == 0), stop=False), wv)
                P.add("tensor", lambda e, fc=fc, h=h, gbank=gbank, g4=g4: e.matmul(
                    gbank[:, 0:512], b16s[g4 % 2][0:16, (fc % 16) * 128:(fc % 16 + 1) * 128],
                    modhA[0:16, h * 512:(h + 1) * 512],
                    start=False, stop=True), [], inc="TT")
                tg = P.c("TT")
                for kc in range(KC):
                    wv = [("VV", f_guv.get((fc - 1, h), 0))] if kc == 0 else []
                    P.add("tensor", lambda e, fc=fc, h=h, kc=kc, ubank=ubank: e.matmul(
                        ubank[:, 0:512], wupb[fc % 2][:, kc * 128:(kc + 1) * 128],
                        h16[:, kc * T + h * 512: kc * T + (h + 1) * 512],
                        start=(kc == 0), stop=(kc == KC - 1)), wv,
                        inc="TT" if kc == KC - 1 else None)
                tu = P.c("TT")
                if h == 1:
                    f_tm[fc] = tu
                P.add("scalar", lambda e, h=h, gbank=gbank: e.activation(
                    sil32[h][:, :], gbank[:, 0:512], AF.Silu), [("TT", tg)], inc="SC")
                f_sil[(fc, h)] = P.c("SC")
                P.add("vector", lambda e, fc=fc, h=h, ubank=ubank: e.tensor_tensor(
                    gu[:, fc * T + h * 512: fc * T + (h + 1) * 512], ubank[:, 0:512],
                    sil32[h][:, :], ALU.mult), [("SC", f_sil[(fc, h)]), ("TT", tu)], inc="VV")
                f_guv[(fc, h)] = P.c("VV")
        ph11stack.close()
        h16stack.close()

        P.barrier()
        ph12stack = ExitStack()
        wdnb = [self.sb("wdnb%d" % i, [128, FC * 128], BF16, ph12stack) for i in range(2)]
        x1t = [self.sb("x1t%d" % i, [128, T], F32, ph12stack) for i in range(2)]
        hst = [self.sb("hst%d" % i, [128, T], F32, ph12stack) for i in range(2)]
        outst = [self.sb("outst%d" % i, [128, T], F32, ph12stack) for i in range(2)]
        d_tm = {}
        d_ev = {}
        st_marks = {}
        for ec in range(KC):
            us = "U%d" % (ec % 2)
            qs = "Q%d" % (ec % 2)
            qo = "Q%d" % (2 + ec % 2)
            P.dma("gpsimd", lambda e, ec=ec: e.dma_start(
                out=wdnb[ec % 2][:, :].rearrange("p (fc c) -> p fc c", fc=FC),
                in_=d_wdn[ec].rearrange("fc p c -> p fc c")),
                waits=[("TT", d_tm.get(ec - 2, 0))], sem=us)
            wdm = (us, P.c(us))
            P.dma("sync", lambda e, ec=ec: e.dma_start(out=x1t[ec % 2][:, :], in_=d_x1sp[ec]),
                  waits=[("VV", d_ev.get(ec - 2, 0))], sem=qs)
            P.dma("sync", lambda e, ec=ec: e.dma_start(out=hst[ec % 2][:, :], in_=d_hssp[ec]),
                  sem=qs)
            ldm = (qs, P.c(qs))
            for h in range(2):
                bank = (psA, psB)[h]
                for fc in range(FC):
                    wv = [wdm, ("VV", d_ev.get((ec - 1, h), 0))] if fc == 0 else []
                    P.add("tensor", lambda e, ec=ec, h=h, fc=fc, bank=bank: e.matmul(
                        bank[:, 0:512], wdnb[ec % 2][:, fc * 128:(fc + 1) * 128],
                        gu[:, fc * T + h * 512: fc * T + (h + 1) * 512],
                        start=(fc == 0), stop=(fc == FC - 1)), wv,
                        inc="TT" if fc == FC - 1 else None)
                tmark = P.c("TT")
                if h == 1:
                    d_tm[ec] = tmark
                P.add("vector", lambda e, ec=ec, h=h, bank=bank: e.tensor_tensor(
                    tmp32[:, h * 512:(h + 1) * 512], bank[:, 0:512],
                    x1t[ec % 2][:, h * 512:(h + 1) * 512], ALU.add),
                    [("TT", tmark), ldm, (qo, st_marks.get(ec - 2, 0))], inc="VV")
                P.add("vector", lambda e, ec=ec, h=h: e.tensor_tensor(
                    outst[ec % 2][:, h * 512:(h + 1) * 512], tmp32[:, h * 512:(h + 1) * 512],
                    hst[ec % 2][:, h * 512:(h + 1) * 512], ALU.add), [], inc="VV")
                d_ev[(ec, h)] = P.c("VV")
            d_ev[ec] = P.c("VV")
            P.dma("sync", lambda e, ec=ec: e.dma_start(out=d_out[ec], in_=outst[ec % 2][:, :]),
                  waits=[("VV", d_ev[ec])], sem=qo)
            st_marks[ec] = P.c(qo)
        P.add("sync", lambda e: e.nop(),
              waits=[("Q2", P.c("Q2")), ("Q3", P.c("Q3"))])
        P.add("gpsimd", lambda e: e.nop(),
              waits=[("Q2", P.c("Q2")), ("Q3", P.c("Q3"))])

        self.emit_and_finalize()
        ph12stack.close()
        gustack.close()
        self.ctx.close()
        return nc


def build_program(debug_taps=()):
    b = Builder(debug_taps)
    b.build()
    return b.nc


def _host_stage(inputs):
    x = np.asarray(inputs["L0_attn_x"])
    wq = np.asarray(inputs["L0_attn_wq"])
    wk = np.asarray(inputs["L0_attn_wk"])
    wv = np.asarray(inputs["L0_attn_wv"])
    wo = np.asarray(inputs["L0_attn_wo"])
    ln1w = np.asarray(inputs["L0_ffn_ln1_w"])
    ln2w = np.asarray(inputs["L0_ffn_ln2_w"])
    Wq2 = np.asarray(inputs["L0_ffn_Wq"])
    Wk2 = np.asarray(inputs["L0_ffn_Wk"])
    Wv2 = np.asarray(inputs["L0_ffn_Wv"])
    Wo2 = np.asarray(inputs["L0_ffn_Wo"])
    A = np.asarray(inputs["L0_ffn_A"])
    B = np.asarray(inputs["L0_ffn_B"])
    Win = np.asarray(inputs["L0_ffn_W_in_base"])
    Wup = np.asarray(inputs["L0_ffn_W_up_base"])
    Wdn = np.asarray(inputs["L0_ffn_W_down"])

    perm = np.concatenate([np.arange(0, 128, 2), np.arange(1, 128, 2)])
    inv_freq = 1.0 / (10000.0 ** (np.arange(0, 128, 2, dtype=np.float32) / 128.0))
    ang = np.arange(S, dtype=np.float32)[:, None] * inv_freq[None, :]
    cossin = np.empty((128, S), np.float32)
    cossin[0:64] = np.cos(ang).T
    cossin[64:128] = np.sin(ang).T

    wot = np.ascontiguousarray(wo.reshape(KC, 128, KC, 128).transpose(2, 0, 1, 3)).astype(np.float32)
    winT = np.ascontiguousarray(Win.reshape(FC, 128, KC, 128).transpose(0, 2, 3, 1)).astype(np.float32)
    wupT = np.ascontiguousarray(Wup.reshape(FC, 128, KC, 128).transpose(0, 2, 3, 1)).astype(np.float32)
    wdnT = np.ascontiguousarray(Wdn.reshape(KC, 128, FC, 128).transpose(0, 2, 3, 1)).astype(np.float32)
    afull = np.ascontiguousarray(A.reshape(KC, 128, 16)).astype(np.float32)
    bmat = np.ascontiguousarray(B.reshape(16, FC, 128).transpose(1, 0, 2)).astype(np.float32)
    lnw_arr = np.empty((128, 2 * KC), np.float32)
    lnw_arr[:, :KC] = ln1w.reshape(KC, 128).T
    lnw_arr[:, KC:] = ln2w.reshape(KC, 128).T
    consts = np.zeros((128, 4), np.float32)
    consts[:, 0] = 1.0 / 2048.0
    consts[:, 1] = 1.0
    consts[:, 2] = 1e-5
    consts[:, 3] = 1e-6
    crow = np.ones((1, 256), np.float32)
    ident = np.eye(128, dtype=np.float32)
    ii = np.arange(128)
    tril01 = (ii[:, None] > ii[None, :]).astype(np.float32)
    trilneg = (tril01 * NEG_CAUSAL).astype(np.float32)

    in_maps = []
    for c in range(NCORE):
        sl = slice(c * T, (c + 1) * T)
        xTc = np.ascontiguousarray(x[0, sl, :].T).reshape(KC, 128, T).astype(np.float32)
        q0 = wq[:, (2 * c) * 128:(2 * c + 1) * 128][:, perm]
        q1 = wq[:, (2 * c + 1) * 128:(2 * c + 2) * 128][:, perm]
        kk = wk[:, c * 128:(c + 1) * 128][:, perm]
        vv = wv[:, c * 128:(c + 1) * 128]
        mw = np.stack([q0, q1, kk, vv], axis=1)
        mwqkv = np.ascontiguousarray(
            mw.reshape(KC, 128, 4, 128).transpose(0, 2, 1, 3)).astype(np.float32)
        h0 = slice((2 * c) * 128, (2 * c + 1) * 128)
        h1 = slice((2 * c + 1) * 128, (2 * c + 2) * 128)
        w2 = np.stack([Wq2[h0].T, Wk2[h0].T, Wv2[h0].T,
                       Wq2[h1].T, Wk2[h1].T, Wv2[h1].T], axis=1)
        w2qkv = np.ascontiguousarray(
            w2.reshape(KC, 128, 6, 128).transpose(0, 2, 1, 3)).astype(np.float32)
        esl = slice(c * 256, (c + 1) * 256)
        wo2s = np.ascontiguousarray(Wo2[esl].reshape(2, 128, KC, 128).transpose(0, 2, 1, 3)).astype(np.float32)
        a2s = np.ascontiguousarray(A[esl].reshape(2, 128, 16)).astype(np.float32)
        in_maps.append({
            "xT": xTc, "cossin": cossin,
            "mwqkv": mwqkv, "wot": wot, "w2qkv": w2qkv,
            "wo2s": wo2s, "a2s": a2s, "afull": afull, "bmat": bmat,
            "lnw": lnw_arr, "winT": winT, "wupT": wupT, "wdnT": wdnT,
            "consts": consts, "crow": crow, "ident": ident,
            "tril01": tril01, "trilneg": trilneg,
        })
    return in_maps


def kernel(**inputs):
    in_maps = _host_stage(inputs)
    nc = build_program()
    res = run_bass_kernel_spmd(nc, in_maps, core_ids=list(range(NCORE)))
    full = np.empty((1, S, D), np.float32)
    for c in range(NCORE):
        o = np.asarray(res.results[c]["out"], np.float32)
        full[0, c * T:(c + 1) * T, :] = o.reshape(D, T).T
    return full


# version 27
# speedup vs baseline: 1.0058x; 1.0016x over previous
import math
from contextlib import ExitStack

import numpy as np
import ml_dtypes

import concourse.bass as bass
import concourse.mybir as mybir
from concourse.bass_utils import run_bass_kernel_spmd
from concourse.aot_env import allow_ici_all_to_all

F32 = mybir.dt.float32
BF16 = mybir.dt.bfloat16
AF = mybir.ActivationFunctionType
ALU = mybir.AluOpType
AX = mybir.AxisListType

NCORE = 8
S = 8192
D = 2048
T = 1024
KC = 16
NB = 64
FC = 64
ISD = 1.0 / math.sqrt(128.0)
NEG_CAUSAL = -1.0e6
NEG_PEN = -113152.0

ENGINES = ("sync", "gpsimd", "tensor", "vector", "scalar")
OWN_COMPUTE = {"tensor": "TT", "vector": "VV", "scalar": "SC"}
VSTRIDE = 129


class Builder:
    def __init__(self, debug_taps=()):
        self.nc = bass.Bass("TRN2", target_bir_lowering=False)
        self.ops = []
        self.cnt = {}
        self.pending = {}
        self.ctx = ExitStack()
        self.sems = {}
        self.debug_taps = set(debug_taps)
        self.slot_pv = [0, 0, 0]
        self.bank_exp = [0, 0]
        self.pv_free = [0, 0, 0, 0]
        self.g_mi = 0
        self.gidx = 0
        self.rb_free = {}

    def c(self, sem):
        return self.cnt.get(sem, 0)

    def add(self, eng, fn, waits=(), inc=None, n=1):
        w = list(self.pending.pop(eng, [])) + [(s, v) for (s, v) in waits if v > 0]
        self.ops.append((eng, fn, w, inc, n))
        if inc is not None:
            self.cnt[inc] = self.cnt.get(inc, 0) + n
        return self.cnt.get(inc, 0) if inc else None

    def dma(self, eng, fn, waits=(), sem="LD"):
        return self.add(eng, fn, waits, inc=sem, n=16)

    def barrier(self):
        snap = [(s, v) for (s, v) in self.cnt.items()
                if v > 0 and not s.startswith("CC")]
        for e in ENGINES:
            own = OWN_COMPUTE.get(e)
            self.pending[e] = [(s, v) for (s, v) in snap if s != own]

    def sb(self, name, shape, dt, stack=None, side=None):
        return (stack or self.ctx).enter_context(
            self.nc.sbuf_tensor(name, shape, dt, side=side))

    def tap(self, name, ap_src, shape, dt):
        if name not in self.debug_taps:
            return
        dst = self.nc.dram_tensor("dbg_" + name, shape, dt, kind="ExternalOutput").ap()
        waits = [(s, v) for (s, v) in self.cnt.items() if v > 0]
        self.dma("sync", lambda e: e.dma_start(out=dst, in_=ap_src), waits=waits)

    def emit_and_finalize(self):
        block = self.ctx.enter_context(self.nc.Block())
        sems, ops = self.sems, self.ops

        def engine_body(en):
            def f(eng):
                for (oen, fn, waits, inc, n) in ops:
                    if oen != en:
                        continue
                    for (s, v) in waits:
                        eng.wait_ge(sems[s], v)
                    inst = fn(eng)
                    if inc is not None:
                        inst.then_inc(sems[inc], n)
            return f

        block.sync(engine_body("sync"))
        block.gpsimd(engine_body("gpsimd"))
        block.tensor(engine_body("tensor"))
        block.vector(engine_body("vector"))
        block.scalar(engine_body("scalar"))

    def build(self):
        nc = self.nc
        P = self
        EI = lambda name, shape, dt=F32: nc.dram_tensor(name, shape, dt, kind="ExternalInput").ap()
        EO = lambda name, shape, dt=F32: nc.dram_tensor(name, shape, dt, kind="ExternalOutput").ap()
        IN = lambda name, shape, dt: nc.dram_tensor(name, shape, dt).ap()
        SH = lambda name, shape, dt: nc.dram_tensor(
            name, shape, dt, addr_space="Shared").ap()

        d_xT = EI("xT", [KC, 128, T])
        d_cossin = EI("cossin", [128, S])
        d_mwqkv = EI("mwqkv", [KC, 4, 128, 128])
        d_wot = EI("wot", [KC, KC, 128, 128])
        d_w2qkv = EI("w2qkv", [KC, 6, 128, 128])
        d_wo2s = EI("wo2s", [2, KC, 128, 128])
        d_a2s = EI("a2s", [2, 128, 16])
        d_afull = EI("afull", [KC, 128, 16])
        d_bmat = EI("bmat", [FC, 16, 128])
        d_lnw = EI("lnw", [128, 2 * KC])
        d_win = EI("winT", [FC, KC, 128, 128])
        d_wup = EI("wupT", [FC, KC, 128, 128])
        d_wdn = EI("wdnT", [KC, FC, 128, 128])
        d_consts = EI("consts", [128, 4])
        d_crow = EI("crow", [1, 256])
        d_ident = EI("ident", [128, 128])
        d_tril = EI("tril01", [128, 128])
        d_trilneg = EI("trilneg", [128, 128])
        d_out = EO("out", [KC, 128, T])

        d_ag1i = [IN("ag1_in%d" % i, [KC // 2, 128, 512], BF16) for i in range(4)]
        d_ag1o = [SH("ag1_out%d" % i, [NCORE, KC // 2, 128, 512], BF16) for i in range(4)]
        d_ag2i = [IN("ag2_in%d" % i, [KC // 2, 128, 512], BF16) for i in range(4)]
        d_ag2o = [SH("ag2_out%d" % i, [NCORE, KC // 2, 128, 512], BF16) for i in range(4)]
        d_a2a1i = [IN("a2a1_in%d" % i, [NCORE, 128, T], BF16) for i in range(2)]
        d_a2a1o = [IN("a2a1_out%d" % i, [NCORE, 128, T], BF16) for i in range(2)]
        d_a2a2i = [IN("a2a2_in%d" % i, [NCORE, 128, T], BF16) for i in range(2)]
        d_a2a2o = [IN("a2a2_out%d" % i, [NCORE, 128, T], BF16) for i in range(2)]
        d_wotai = IN("wota_in", [KC, 128, 16], F32)
        d_wotao = SH("wota_out", [KC, 128, 16], F32)
        d_x1sp = IN("x1_sp", [KC, 128, T], F32)
        d_hssp = IN("hs_sp", [KC, 128, T], F32)

        for s in ("LD", "WD", "TT", "VV", "SC", "CC1a", "CC1b", "CC1c", "CC1d",
                  "CC2a", "CC2b",
                  "CC3a", "CC3b", "CC3c", "CC3d", "CC4a", "CC4b", "CC5",
                  "Q0", "Q1", "Q2", "Q3", "Q4", "Q5", "Q6", "Q7", "Q8", "Q9",
                  "R0", "R1", "R2", "R3", "R4", "R5", "R6", "R7", "WI",
                  "U0", "U1", "U2", "U3",
                  "YA", "YB", "SG1", "SG2", "SG3", "SG4", "SP", "SP2", "WT", "XB"):
            self.sems[s] = self.ctx.enter_context(nc.semaphore(s))

        csb = self.sb("csb", [128, 4], F32)
        csb16 = self.sb("csb16", [128, 4], BF16)
        crow16 = self.sb("crow16", [1, 256], BF16)
        crow32 = self.sb("crow32", [1, 256], F32)
        id16 = self.sb("id16", [128, 128], BF16)
        tril_sb = self.sb("tril_sb", [128, 128], F32)
        trilneg_sb = self.sb("trilneg_sb", [128, 128], F32)
        lnw_sb = self.sb("lnw_sb", [128, 2 * KC], F32)
        tmp32 = self.sb("tmp32", [128, 1024], F32)
        wota16 = self.sb("wota16", [128, KC * 16], BF16)
        a16 = self.sb("a16", [128, KC * 16], BF16)
        modhA = self.sb("modhA", [16, T], BF16)
        m1_32 = self.sb("m1_32", [16, T], F32)
        grow4 = self.sb("grow4", [1, 512], BF16)
        gt32_4 = self.sb("gt32_4", [64, 512], F32)
        cmp4 = self.sb("cmp4", [64, 512], BF16)
        msk_sb = self.sb("msk_sb", [128, 128], F32)
        rc_sb = self.sb("rc_sb", [128, 4], F32)
        otm16 = self.sb("otm16", [128, 512], BF16)

        scstack = ExitStack()
        strow = self.sb("strow", [1, 4 * T], F32, scstack, side="right")
        sc16a = self.sb("sc16a", [128, T], BF16, scstack, side="right")
        sc16b = self.sb("sc16b", [128, T], BF16, scstack, side="right")
        sq16a = self.sb("sq16a", [128, T], BF16, scstack, side="right")
        sq16b = self.sb("sq16b", [128, T], BF16, scstack, side="right")
        mwstack = ExitStack()
        mw16 = self.sb("mw16", [128, KC * 4 * 128], BF16, mwstack, side="right")
        xstack1 = ExitStack()
        xbuf = self.sb("xbuf", [128, KC * T], F32, xstack1, side="right")

        scU = self.ctx.enter_context(nc.psum_tensor("scU", [128, 1024], F32))
        scV = self.ctx.enter_context(nc.psum_tensor("scV", [128, 1024], F32))
        psE = self.ctx.enter_context(nc.psum_tensor("psE", [128, 512], F32))
        psF = self.ctx.enter_context(nc.psum_tensor("psF", [128, 512], F32))
        psG = self.ctx.enter_context(nc.psum_tensor("psG", [128, 512], F32))
        psT = self.ctx.enter_context(nc.psum_tensor("psT", [128, 1024], BF16))

        class PsView:
            def __init__(self, t, off):
                self.t, self.off = t, off

            def __getitem__(self, idx):
                p, c = idx
                return self.t[p, slice(c.start + self.off, c.stop + self.off)]

        psA = PsView(scU, 0)
        psB = PsView(scU, 512)
        psC = PsView(scV, 0)
        psD = PsView(scV, 512)
        PS = [psA, psB, psC, psD, psE, psF]

        P.dma("sync", lambda e: e.dma_start(out=csb[:, :], in_=d_consts))
        P.dma("sync", lambda e: e.dma_start(out=crow32[:, :], in_=d_crow))
        P.dma("sync", lambda e: e.dma_start(out=tril_sb[:, :], in_=d_tril))
        P.dma("sync", lambda e: e.dma_start(out=trilneg_sb[:, :], in_=d_trilneg))
        P.dma("sync", lambda e: e.dma_start(out=lnw_sb[:, :], in_=d_lnw))
        ld_c = ("LD", P.c("LD"))
        x_marks = {}
        for pc, eng in enumerate(("scalar", "sync", "gpsimd", "gpsimd")):
            P.dma(eng, lambda e, pc=pc: e.dma_start(
                out=xbuf[:, pc * 4 * T:(pc + 1) * 4 * T].rearrange(
                    "p (kc t) -> p kc t", kc=4),
                in_=d_xT[4 * pc:4 * pc + 4].rearrange("kc p t -> p kc t")),
                sem="R%d" % pc)
            for kc in range(4 * pc, 4 * pc + 4):
                x_marks[kc] = ("R%d" % pc, P.c("R%d" % pc))
        P.dma("gpsimd", lambda e: e.dma_start(out=csb16[:, :], in_=d_consts), sem="WD")
        P.dma("gpsimd", lambda e: e.dma_start(out=crow16[:, :], in_=d_crow), sem="WD")
        P.dma("gpsimd", lambda e: e.dma_start(out=id16[:, :], in_=d_ident), sem="WD")
        P.dma("gpsimd", lambda e: e.dma_start(
            out=a16[:, :].rearrange("p (dc r) -> p dc r", dc=KC),
            in_=d_afull.rearrange("dc p r -> p dc r")), sem="WD")
        ld0 = P.c("LD")

        wostack = ExitStack()
        wo2sb = self.sb("wo2sb", [128, 2 * KC * 128], BF16, wostack)
        a2sb = self.sb("a2sb", [128, 2 * 16], BF16, wostack)
        wotp = self.sb("wotp", [128, KC * 16], F32, wostack)
        P.dma("gpsimd", lambda e: e.dma_start(
            out=wo2sb[:, :].rearrange("p (e dc c) -> p e dc c", e=2, dc=KC),
            in_=d_wo2s.rearrange("e dc p c -> p e dc c")), sem="WD")
        P.dma("gpsimd", lambda e: e.dma_start(
            out=a2sb[:, :].rearrange("p (e r) -> p e r", e=2),
            in_=d_a2s.rearrange("e p r -> p e r")), sem="WD")
        P.dma("gpsimd", lambda e: e.dma_start(
            out=mw16[:, :].rearrange("p (kc m c) -> p kc m c", kc=KC, m=4),
            in_=d_mwqkv.rearrange("kc m p c -> p kc m c")), sem="WD")
        wd0 = P.c("WD")
        wd_mw = wd0
        wd_wo2 = wd0
        v_prev = 0
        for dc in range(KC):
            for e2 in range(2):
                P.add("tensor", lambda e, dc=dc, e2=e2: e.matmul(
                    psG[0:128, 0:16],
                    wo2sb[:, (e2 * KC + dc) * 128:(e2 * KC + dc + 1) * 128],
                    a2sb[:, e2 * 16:(e2 + 1) * 16],
                    start=(e2 == 0), stop=(e2 == 1)),
                    ([("WD", wd_wo2), ("VV", v_prev)] if e2 == 0 else []),
                    inc="TT" if e2 == 1 else None)
            tmark = P.c("TT")
            P.add("vector", lambda e, dc=dc: e.tensor_copy(
                wotp[:, dc * 16:(dc + 1) * 16], psG[:, 0:16]), [("TT", tmark)], inc="VV")
            v_prev = P.c("VV")
        P.dma("sync", lambda e: e.dma_start(
            out=d_wotai.rearrange("dc p r -> p dc r"),
            in_=wotp[:, :].rearrange("p (dc r) -> p dc r", dc=KC)),
            waits=[("VV", v_prev)], sem="WI")
        ld_wotai = P.c("WI")
        wostack.close()

        def emit_stats(src, *, with_mean, eps_col, prereq, x_marks=None):
            sc16 = (sc16a, sc16b)
            sq16 = (sq16a, sq16b)
            marks = []
            for kc in range(KC):
                w = list(prereq) if kc < 2 else [("TT", marks[kc - 2])]
                if x_marks is not None:
                    w = [x_marks[kc]] + w
                cm = P.add("vector", lambda e, kc=kc: e.tensor_copy(
                    sc16[kc % 2][:, :], src[:, kc * T:(kc + 1) * T]), w, inc="VV")
                sm = P.add("vector", lambda e, kc=kc: e.tensor_tensor(
                    sq16[kc % 2][:, :], sc16[kc % 2][:, :], sc16[kc % 2][:, :], ALU.mult), [], inc="VV")
                for h in range(2):
                    if with_mean:
                        P.add("tensor", lambda e, kc=kc, h=h: e.matmul(
                            psA[(32 * h):(32 * h + 1), 0:512], csb16[:, 0:1],
                            sc16[kc % 2][:, h * 512:(h + 1) * 512],
                            start=(kc == 0), stop=(kc == KC - 1)),
                            [("VV", cm)] if h == 0 else [])
                    P.add("tensor", lambda e, kc=kc, h=h: e.matmul(
                        psB[(32 * h):(32 * h + 1), 0:512], csb16[:, 0:1],
                        sq16[kc % 2][:, h * 512:(h + 1) * 512],
                        start=(kc == 0), stop=(kc == KC - 1)),
                        [("VV", sm)] if h == 0 else [],
                        inc="TT" if h == 1 else None)
                marks.append(P.c("TT"))
            t_sums = P.c("TT")
            if with_mean:
                P.add("vector", lambda e: e.tensor_copy(strow[0:1, 0:512], psA[0:1, 0:512]),
                      [("TT", t_sums)], inc="VV")
                P.add("vector", lambda e: e.tensor_copy(strow[0:1, 512:1024], psA[32:33, 0:512]), [], inc="VV")
            P.add("vector", lambda e: e.tensor_copy(strow[0:1, T:T + 512], psB[0:1, 0:512]),
                  [("TT", t_sums)], inc="VV")
            P.add("vector", lambda e: e.tensor_copy(strow[0:1, T + 512:2 * T], psB[32:33, 0:512]), [], inc="VV")
            if with_mean:
                P.add("vector", lambda e: e.tensor_tensor(
                    strow[0:1, 2 * T:3 * T], strow[0:1, 0:T], strow[0:1, 0:T], ALU.mult), [], inc="VV")
                P.add("vector", lambda e: e.tensor_tensor(
                    strow[0:1, T:2 * T], strow[0:1, T:2 * T], strow[0:1, 2 * T:3 * T], ALU.subtract), [], inc="VV")
            v_var = P.c("VV")
            P.add("scalar", lambda e: e.activation(
                strow[0:1, 2 * T:3 * T], strow[0:1, T:2 * T], AF.Ln,
                bias=csb[0:1, eps_col:eps_col + 1]),
                [("VV", v_var)], inc="SC")
            P.add("scalar", lambda e: e.activation(
                strow[0:1, 3 * T:4 * T], strow[0:1, 2 * T:3 * T], AF.Exp, scale=-0.5), [], inc="SC")
            s_rstd = P.c("SC")
            if with_mean:
                P.add("vector", lambda e: e.tensor_tensor(
                    strow[0:1, 2 * T:3 * T], strow[0:1, 0:T], strow[0:1, 3 * T:4 * T], ALU.mult),
                    [("SC", s_rstd)], inc="VV")
            return P.c("VV"), s_rstd

        def emit_bcasts(*, with_mean, s_rstd, v_a):
            P.add("tensor", lambda e: e.matmul(psE[:, 0:512], crow32[0:1, 0:128],
                                               strow[0:1, 3 * T:3 * T + 512], start=True, stop=True),
                  [("SC", s_rstd)])
            P.add("tensor", lambda e: e.matmul(psF[:, 0:512], crow32[0:1, 0:128],
                                               strow[0:1, 3 * T + 512:4 * T], start=True, stop=True),
                  [], inc="TT")
            if with_mean:
                P.add("tensor", lambda e: e.matmul(psC[:, 0:512], crow32[0:1, 0:128],
                                                   strow[0:1, 2 * T:2 * T + 512], start=True, stop=True),
                      [("VV", v_a)])
                P.add("tensor", lambda e: e.matmul(psD[:, 0:512], crow32[0:1, 0:128],
                                                   strow[0:1, 2 * T + 512:3 * T], start=True, stop=True),
                      [], inc="TT")
            return P.c("TT")

        v0, s0 = emit_stats(xbuf, with_mean=True, eps_col=2,
                            prereq=[ld_c, ("WD", wd0)], x_marks=x_marks)
        t_bc0 = emit_bcasts(with_mean=True, s_rstd=s0, v_a=v0)
        for h in range(2):
            rb = psE if h == 0 else psF
            ab = psC if h == 0 else psD
            for kc in range(KC):
                dst = (sc16a, sq16a)[kc % 2]
                qs = "R%d" % (4 * h + 2 * (kc // 8) + kc % 2)
                w = [("TT", t_bc0)]
                lin = h * KC + kc
                if lin >= 2:
                    hp, kcp = divmod(lin - 2, KC)
                    pq = "R%d" % (4 * hp + 2 * (kcp // 8) + kcp % 2)
                    w.append((pq, P.c(pq)))
                P.add("vector", lambda e, kc=kc, h=h, rb=rb: e.tensor_tensor(
                    tmp32[:, 0:512], xbuf[:, kc * T + h * 512: kc * T + (h + 1) * 512],
                    rb[:, 0:512], ALU.mult), w, inc="VV")
                P.add("vector", lambda e, dst=dst, ab=ab: e.tensor_tensor(
                    dst[:, 0:512], tmp32[:, 0:512], ab[:, 0:512], ALU.subtract),
                    [], inc="VV")
                vm = P.c("VV")
                P.dma("sync", lambda e, kc=kc, h=h, dst=dst: e.dma_start(
                    out=d_ag1i[2 * h + kc // 8][kc % 8], in_=dst[:, 0:512]),
                    waits=[("VV", vm)], sem=qs)
                if kc % 8 == 7:
                    q4 = 2 * h + kc // 8
                    ra, rb_ = "R%d" % (2 * q4), "R%d" % (2 * q4 + 1)
                    P.add("gpsimd", lambda e, q4=q4: e.collective_compute(
                        "AllGather", ALU.bypass, replica_groups=[list(range(NCORE))],
                        ins=[d_ag1i[q4].opt()], outs=[d_ag1o[q4].opt()]),
                        [(ra, P.c(ra)), (rb_, P.c(rb_))], inc="CC1" + "abcd"[q4])
        P.add("gpsimd", lambda e: e.collective_compute(
            "AllReduce", ALU.add, replica_groups=[list(range(NCORE))],
            ins=[d_wotai.opt()], outs=[d_wotao.opt()]), [("WI", ld_wotai)], inc="CC5")
        xstack1.close()

        P.barrier()
        mstackA = ExitStack()
        aQ0 = self.sb("aQ0", [128, S], BF16, mstackA)
        aQ1 = self.sb("aQ1", [128, S], BF16, mstackA)
        aK = self.sb("aK", [128, S], BF16, mstackA)
        aV = self.sb("aV", [128, NB * VSTRIDE + 64], BF16, mstackA)
        means16 = self.sb("means16", [128, 64], BF16, mstackA)
        m3stack = ExitStack()
        lnst = [self.sb("lnst%d" % i, [128, KC * 512], BF16, m3stack) for i in range(2)]
        cosb = self.sb("cosb", [64, S], BF16, m3stack)
        sinb = self.sb("sinb", [64, S], BF16, m3stack)
        rtmp = self.sb("rtmp", [64, 4 * 512], BF16, m3stack)
        rblo = [self.sb("rblo%d" % i, [64, 512], BF16, m3stack) for i in range(2)]
        rbhi = [self.sb("rbhi%d" % i, [64, 512], BF16, m3stack) for i in range(2)]
        vtmp = self.sb("vtmp", [128, 512], BF16, m3stack)
        P.add("vector", lambda e: e.memset(
            aV[:, :NB * VSTRIDE].rearrange("p (n c) -> p n c", c=VSTRIDE)[:, :, 128:129], 1.0),
            [], inc="VV")

        v_tr_prev = [0]
        t_tr_prev = [0]

        def emit_qkv_produce(nm, d_src, cc_sem, wsb, wd_w, nroles, on_evac, strbufs):
            ev = {}
            ld_marks = {}
            tt_roles = {}
            tc_list = list(range(0, 16, 2)) + list(range(1, 16, 2))
            for idx, tc in enumerate(tc_list):
                rk, hf = tc // 2, tc % 2
                slot = idx % 2
                for j in range(2):
                    qj = "Q%d" % (2 + 2 * j + slot)
                    if idx < 2:
                        w = [(cc_sem[2 * hf + j], 1)]
                    else:
                        w = ([("TT", tt_roles.get((tc_list[idx - 2], nroles - 1), 0))]
                             if j == 0 else [])
                        if idx in (8, 9):
                            w.append((cc_sem[2 + j], 1))
                    P.dma("sync", lambda e, rk=rk, hf=hf, slot=slot, j=j: e.dma_start(
                        out=strbufs[slot][:, j * 8 * 512:(j + 1) * 8 * 512]
                            .rearrange("p (kc t) -> p kc t", kc=KC // 2),
                        in_=d_src[2 * hf + j][rk].rearrange("kc p t -> p kc t")),
                        waits=w, sem=qj)
                    ld_marks[(tc, j)] = (qj, P.c(qj))
                for m in range(nroles):
                    si = idx * nroles + m
                    bank = PS[si % 6]
                    prev = si - 6
                    wv = []
                    if prev >= 0:
                        wv.append(ev[(tc_list[prev // nroles], prev % nroles)])
                    for kc in range(KC):
                        if kc == 0:
                            w_mm = [ld_marks[(tc, 0)], ("WD", wd_w)] + wv
                        elif kc == 8:
                            w_mm = [ld_marks[(tc, 1)]]
                        else:
                            w_mm = []
                        P.add("tensor", lambda e, m=m, kc=kc, slot=slot, bank=bank: e.matmul(
                            bank[:, 0:512],
                            wsb[:, (kc * nroles + m) * 128:(kc * nroles + m + 1) * 128],
                            strbufs[slot][:, kc * 512:(kc + 1) * 512],
                            start=(kc == 0), stop=(kc == KC - 1)),
                            w_mm, inc="TT" if kc == KC - 1 else None)
                    tt_roles[(tc, m)] = P.c("TT")
                    ev[(tc, m)] = on_evac(tc, m, bank, tt_roles[(tc, m)])
            return ev

        def mx_evac(tc, m, bank, tmark):
            cols = slice(tc * 512, (tc + 1) * 512)
            if m < 3:
                dst = (aQ0, aQ1, aK)[m]
                cs = slice(tc * 512, (tc + 1) * 512)
                rl, rh = rblo[m % 2], rbhi[m % 2]
                P.add("scalar", lambda e, bank=bank, rl=rl: e.activation(
                    rl[:, :], bank[0:64, 0:512], AF.Copy),
                    [("TT", tmark), ld_cs, ("VV", self.rb_free.get(m % 2, 0))], inc="SC")
                P.add("scalar", lambda e, bank=bank, rh=rh: e.activation(
                    rh[:, :], bank[64:128, 0:512], AF.Copy), [], inc="SC")
                scm = P.c("SC")
                P.add("vector", lambda e, rl=rl, cs=cs: e.tensor_tensor(
                    rtmp[:, 0:512], rl[:, :], cosb[:, cs], ALU.mult),
                    [("SC", scm)], inc="VV")
                P.add("vector", lambda e, rh=rh, cs=cs: e.tensor_tensor(
                    rtmp[:, 512:1024], rh[:, :], sinb[:, cs], ALU.mult), [], inc="VV")
                P.add("vector", lambda e, rl=rl, cs=cs: e.tensor_tensor(
                    rtmp[:, 1024:1536], rl[:, :], sinb[:, cs], ALU.mult), [], inc="VV")
                P.add("vector", lambda e, rh=rh, cs=cs: e.tensor_tensor(
                    rtmp[:, 1536:2048], rh[:, :], cosb[:, cs], ALU.mult), [], inc="VV")
                self.rb_free[m % 2] = P.c("VV")
                P.add("vector", lambda e, dst=dst, cols=cols: e.tensor_tensor(
                    dst[0:64, cols], rtmp[:, 0:512], rtmp[:, 512:1024], ALU.subtract), [], inc="VV")
                P.add("vector", lambda e, dst=dst, cols=cols: e.tensor_tensor(
                    dst[64:128, cols], rtmp[:, 1024:1536], rtmp[:, 1536:2048], ALU.add), [], inc="VV")
                return ("VV", P.c("VV"))
            P.add("vector", lambda e, bank=bank: e.tensor_copy(vtmp[:, :], bank[:, 0:512]),
                  [("TT", tmark), ("TT", t_tr_prev[0])], inc="VV")
            vm = P.c("VV")
            for j in range(4):
                P.add("tensor", lambda e, j=j: e.transpose(
                    psT[:, j * 128:(j + 1) * 128], vtmp[:, j * 128:(j + 1) * 128], id16[:, :]),
                    [("VV", vm), ("VV", v_tr_prev[0])] if j == 0 else [],
                    inc="TT" if j == 3 else None)
            ttm = P.c("TT")
            t_tr_prev[0] = ttm
            for j in range(4):
                bidx = 4 * tc + j
                P.add("vector", lambda e, j=j, bidx=bidx: e.tensor_copy(
                    aV[:, bidx * VSTRIDE:bidx * VSTRIDE + 128], psT[:, j * 128:(j + 1) * 128]),
                    [("TT", ttm)] if j == 0 else [], inc="VV" if j == 3 else None)
            v_tr_prev[0] = P.c("VV")
            return ("VV", v_tr_prev[0])

        P.dma("gpsimd", lambda e: e.dma_start(out=cosb[:, :], in_=d_cossin[0:64, :]), sem="WD")
        P.dma("gpsimd", lambda e: e.dma_start(out=sinb[:, :], in_=d_cossin[64:128, :]), sem="WD")
        ld_cs = ("WD", P.c("WD"))
        ev_mx = emit_qkv_produce("mx", d_ag1o, ("CC1a", "CC1b", "CC1c", "CC1d"), mw16, wd_mw, 4, mx_evac, lnst)

        def _means_op(e):
            with nc.allow_low_precision(reason="gating block sums feed comparisons only"):
                return e.reduce_sum(
                    means16[:, :].rearrange("p (n o) -> p n o", o=1),
                    aK[:, :].rearrange("p (n w) -> p n w", w=128), axis=AX.X)
        P.add("vector", _means_op, [ev_mx[(15, 2)]], inc="VV")
        P.tap("q0", aQ0[:, 0:1024], [128, 1024], BF16)
        P.tap("k", aK[:, 0:1024], [128, 1024], BF16)
        m3stack.close()
        mwstack.close()

        SCB2 = [scU, scV]

        def pvloc(i):
            return (psE if i < 2 else psF), (0 if i % 2 == 0 else 256)

        def emit_pen_precompute(Qs, means, penbuf):
            m_s3 = 0
            m_s4 = 0
            m_s6 = 0
            m_s8 = 0
            for tb in range(14):
                t0 = 8 + 4 * tb
                w1 = [("VV", m_s3), ("VV", m_s8)]
                w2 = [("VV", m_s4)]
                if tb == 0:
                    w1 += [("VV", self.g_mi)]
                    w2 += [("VV", self.pv_free[0]), ("VV", self.pv_free[1])]
                for i in range(4):
                    P.add("tensor", lambda e, t=t0 + i, i=i: e.matmul(
                        psG[0:1, i * 128:(i + 1) * 128], means[:, t:t + 1],
                        Qs[:, t * 128:(t + 1) * 128], start=True, stop=True),
                        w1 if i == 0 else [], inc=None)
                for i in range(4):
                    P.add("tensor", lambda e, t=t0 + i, i=i: e.matmul(
                        psE[0:64, i * 128:(i + 1) * 128], means[:, 0:64],
                        Qs[:, t * 128:(t + 1) * 128], start=True, stop=True),
                        w2 if i == 0 else [],
                        inc="TT" if i == 3 else None)
                m_s12 = P.c("TT")
                P.add("vector", lambda e: e.tensor_copy(grow4[0:1, :], psG[0:1, 0:512]),
                      [("TT", m_s12)], inc="VV")
                m_s3 = P.c("VV")
                P.add("vector", lambda e: e.tensor_copy(gt32_4[:, :], psE[0:64, 0:512]),
                      [], inc="VV")
                m_s4 = P.c("VV")
                w5 = [("VV", m_s4), ("VV", m_s6)]
                if tb == 0:
                    w5 += [("VV", self.pv_free[2]), ("VV", self.pv_free[3])]
                for i in range(4):
                    P.add("tensor", lambda e, i=i: e.matmul(
                        psF[0:64, i * 128:(i + 1) * 128], crow16[0:1, 0:64],
                        grow4[0:1, i * 128:(i + 1) * 128], start=True, stop=True),
                        w5 if i == 0 else [],
                        inc="TT" if i == 3 else None)
                m_s5 = P.c("TT")
                P.add("vector", lambda e: e.tensor_tensor(
                    cmp4[:, :], gt32_4[:, :], psF[0:64, 0:512], ALU.is_ge),
                    [("TT", m_s5)], inc="VV")
                m_s6 = P.c("VV")
                for i in range(4):
                    P.add("tensor", lambda e, t=t0 + i, i=i: e.matmul(
                        psG[0:1, i * 128:(i + 1) * 128], csb16[0:t + 1, 1:2],
                        cmp4[0:t + 1, i * 128:(i + 1) * 128], start=True, stop=True),
                        [("VV", m_s6)] if i == 0 else [],
                        inc="TT" if i == 3 else None)
                m_s7 = P.c("TT")
                P.add("vector", lambda e, t0=t0: e.tensor_scalar(
                    penbuf[0:1, (t0 - 8) * 128:(t0 - 4) * 128], psG[0:1, 0:512],
                    8.5, NEG_PEN, ALU.is_gt, ALU.mult), [("TT", m_s7)], inc="VV")
                m_s8 = P.c("VV")
            m_mi = 0
            for tb in range(14):
                for i in range(4):
                    t = 8 + 4 * tb + i
                    P.add("tensor", lambda e, t=t: e.matmul(
                        psG[:, 256:384], crow16[0:1, 0:128],
                        penbuf[0:1, (t - 8) * 128:(t - 7) * 128],
                        start=True, stop=True),
                        [("VV", m_s8 if (tb == 0 and i == 0) else 0), ("VV", m_mi)],
                        inc="TT")
                    m_bc = P.c("TT")
                    P.add("vector", lambda e, t=t: e.tensor_tensor(
                        mskbuf[:, (t - 8) * 128:(t - 7) * 128], tril_sb[:, :],
                        psG[:, 256:384], ALU.mult), [("TT", m_bc)], inc="VV")
                    m_mi = P.c("VV")
            self.pv_free[0] = max(self.pv_free[0], m_s4)
            self.pv_free[1] = max(self.pv_free[1], m_s4)
            self.pv_free[2] = max(self.pv_free[2], m_s6)
            self.pv_free[3] = max(self.pv_free[3], m_s6)
            self.g_mi = m_mi
            return P.c("VV")

        def emit_attention(Qs, Ks, Vtm, outb, ptb, *, mixer, means, penbuf=None):
            if mixer:
                v_pen = emit_pen_precompute(Qs, means, penbuf)
            pending = []
            norm_q = []

            def emit_norm_step():
                if not norm_q:
                    return
                i, t, vm = norm_q.pop(0)
                P.add("tensor", lambda e, i=i: e.transpose(
                    psT[:, i * 128:(i + 1) * 128], otm16[:, i * 128:(i + 1) * 128], id16[:, :]),
                    [("VV", vm), ("VV", v_tr_prev[0])], inc="TT")
                tm2 = P.c("TT")
                P.add("vector", lambda e, i=i, t=t: e.tensor_copy(
                    outb[:, t * 128:(t + 1) * 128], psT[:, i * 128:(i + 1) * 128]),
                    [("TT", tm2)], inc="VV")
                v_tr_prev[0] = P.c("VV")

            def emit_pv(ctx):
                slot, grp, rr, smark = ctx
                for j, (i, b, pj) in enumerate(grp):
                    t = 4 * rr + i
                    last = (b == t) if not (mixer and rr < 2) else (b == 7)
                    pvb, off = pvloc(i)
                    wv = [("SC", smark)] if j == 0 else []
                    if b == 0:
                        wv.append(("VV", self.pv_free[i]))
                        if i % 2 == 0:
                            wv.append(("VV", self.pv_free[i + 1]))
                    P.add("tensor", lambda e, b=b, pj=pj, slot=slot, last=last,
                          i=i, pvb=pvb, off=off: e.matmul(
                        pvb[:, off:off + 129],
                        ptb[:, slot * 1024 + pj * 128: slot * 1024 + (pj + 1) * 128],
                        Vtm[:, b * VSTRIDE:b * VSTRIDE + 129],
                        start=(b == 0 and i % 2 == 0), stop=last),
                        wv, inc="TT" if j == len(grp) - 1 else None)
                self.slot_pv[slot] = P.c("TT")

            def push_pv(ctx):
                pending.append(ctx)
                if len(pending) > 2:
                    emit_pv(pending.pop(0))

            def flush():
                while pending:
                    emit_pv(pending.pop(0))

            for r in range(16):
                n_common = 8 if (mixer and r < 2) else 4 * r
                qcols = slice(512 * r, 512 * r + 512)
                for pb in range(n_common // 2):
                    b0 = 2 * pb
                    g = self.gidx; self.gidx += 1
                    sc = SCB2[g % 2]
                    slot = g % 3
                    P.add("tensor", lambda e, b0=b0, sc=sc, qcols=qcols: e.matmul(
                        sc[:, 0:512], Ks[:, b0 * 128:(b0 + 1) * 128], Qs[:, qcols],
                        start=True, stop=True),
                        [("SC", self.bank_exp[g % 2])], inc=None)
                    P.add("tensor", lambda e, b0=b0, sc=sc, qcols=qcols: e.matmul(
                        sc[:, 512:1024], Ks[:, (b0 + 1) * 128:(b0 + 2) * 128], Qs[:, qcols],
                        start=True, stop=True), [], inc="TT")
                    tmark = P.c("TT")
                    emit_norm_step()
                    P.add("scalar", lambda e, sc=sc, slot=slot: e.activation(
                        ptb[:, slot * 1024:(slot + 1) * 1024], sc[:, 0:1024], AF.Exp, scale=ISD),
                        [("TT", tmark), ("TT", self.slot_pv[slot])], inc="SC")
                    self.bank_exp[g % 2] = P.c("SC")
                    push_pv((slot,
                             [(i, b0, i) for i in (0, 2, 1, 3)]
                             + [(i, b0 + 1, 4 + i) for i in (0, 2, 1, 3)],
                             r, P.c("SC")))
                if not (mixer and r < 2):
                    pieces = [(i, 4 * r + b) for i in range(4) for b in range(i + 1)]
                    groups = [pieces[0:8], pieces[8:10]]
                    rag_slots = []
                    for grp in groups:
                        g = self.gidx; self.gidx += 1
                        sc = SCB2[g % 2]
                        slot = g % 3
                        rag_slots.append(slot)
                        for j, (i, b) in enumerate(grp):
                            t = 4 * r + i
                            wv = [("SC", self.bank_exp[g % 2])] if j == 0 else []
                            P.add("tensor", lambda e, b=b, j=j, t=t, sc=sc: e.matmul(
                                sc[:, j * 128:(j + 1) * 128],
                                Ks[:, b * 128:(b + 1) * 128], Qs[:, t * 128:(t + 1) * 128],
                                start=True, stop=True),
                                wv, inc="TT" if j == len(grp) - 1 else None)
                        gmark = P.c("TT")
                        vmask = 0
                        for j, (i, b) in enumerate(grp):
                            t = 4 * r + i
                            if b != t:
                                continue
                            if not mixer:
                                P.add("vector", lambda e, j=j, sc=sc: e.tensor_tensor(
                                    sc[:, j * 128:(j + 1) * 128], sc[:, j * 128:(j + 1) * 128],
                                    trilneg_sb[:, :], ALU.add), [("TT", gmark)], inc="VV")
                                vmask = P.c("VV")
                            else:
                                P.add("vector", lambda e, j=j, t=t, sc=sc: e.tensor_tensor(
                                    sc[:, j * 128:(j + 1) * 128], sc[:, j * 128:(j + 1) * 128],
                                    mskbuf[:, (t - 8) * 128:(t - 7) * 128], ALU.add),
                                    [("TT", gmark), ("VV", v_pen)], inc="VV")
                                vmask = P.c("VV")
                        wexp = [("TT", gmark), ("TT", self.slot_pv[slot])]
                        if vmask:
                            wexp.append(("VV", vmask))
                        P.add("scalar", lambda e, sc=sc, slot=slot, n=len(grp): e.activation(
                            ptb[:, slot * 1024: slot * 1024 + n * 128], sc[:, 0:n * 128],
                            AF.Exp, scale=ISD), wexp, inc="SC")
                        self.bank_exp[g % 2] = P.c("SC")
                        push_pv((slot, [(i, b, j) for j, (i, b) in enumerate(grp)],
                                 r, P.c("SC")))
                flush()
                while norm_q:
                    emit_norm_step()
                if not (mixer and r < 2):
                    tA = self.slot_pv[rag_slots[0]]
                    tB = self.slot_pv[rag_slots[1]]
                else:
                    tA = tB = P.c("TT")
                for lo, hi, tm in ((0, 2, tA), (2, 4, tB)):
                    rms = []
                    for i in range(lo, hi):
                        pvb, off = pvloc(i)
                        P.add("vector", lambda e, i=i, pvb=pvb, off=off: e.reciprocal(
                            rc_sb[:, i:i + 1], pvb[:, off + 128:off + 129]),
                            [("TT", tm)], inc="VV")
                        rms.append(P.c("VV"))
                    for i in range(lo, hi):
                        t = 4 * r + i
                        pvb, off = pvloc(i)
                        P.add("vector", lambda e, i=i, pvb=pvb, off=off: e.tensor_scalar(
                            otm16[:, i * 128:(i + 1) * 128], pvb[:, off:off + 128],
                            rc_sb[:, i:i + 1], None, ALU.mult),
                            [("VV", rms[1])], inc="VV")
                        self.pv_free[i] = P.c("VV")
                        norm_q.append((i, t, P.c("VV")))
            while norm_q:
                emit_norm_step()

        P.barrier()
        mstackB = ExitStack()
        pt_sb = self.sb("pt_sb", [128, 3 * 1024], BF16, mstackB)
        outh0 = self.sb("outh0", [128, S], BF16, mstackB)
        outh1 = self.sb("outh1", [128, S], BF16, mstackB)
        penbuf = self.sb("penbuf", [1, 56 * 128], BF16, mstackB)
        mskbuf = self.sb("mskbuf", [128, 56 * 128], F32, mstackB)
        def stage_a2a(outb, d_in, d_out, cc, sg):
            v_att = P.c("VV")
            for r in range(NCORE):
                P.dma("sync", lambda e, r=r, outb=outb, d_in=d_in: e.dma_start(
                    out=d_in[r], in_=outb[:, r * T:(r + 1) * T]),
                    waits=[("VV", v_att)] if r == 0 else [], sem=sg)
            ldm = P.c(sg)

            def _op(e, d_in=d_in, d_out=d_out):
                with allow_ici_all_to_all():
                    return e.collective_compute(
                        "AllToAll", ALU.bypass, replica_groups=[list(range(NCORE))],
                        ins=[d_in.opt()], outs=[d_out.opt()])
            P.add("gpsimd", _op, [(sg, ldm)], inc=cc)

        emit_attention(aQ0, aK, aV, outh0, pt_sb, mixer=True, means=means16, penbuf=penbuf)
        stage_a2a(outh0, d_a2a1i[0], d_a2a1o[0], "CC2a", "SG1")
        emit_attention(aQ1, aK, aV, outh1, pt_sb, mixer=True, means=means16, penbuf=penbuf)
        stage_a2a(outh1, d_a2a1i[1], d_a2a1o[1], "CC2b", "SG2")
        mstackB.close()
        mstackA.close()

        P.barrier()
        wstack = ExitStack()
        att1 = self.sb("att1", [128, KC * T], BF16, wstack)
        wosb = [self.sb("wosb%d" % i, [128, KC * 128], BF16, wstack) for i in range(4)]
        xstack2 = ExitStack()
        xb2 = self.sb("xb2", [128, KC * T], F32, xstack2, side="right")
        P.dma("sync", lambda e: e.dma_start(
            out=xb2[:, :].rearrange("p (kc t) -> p kc t", kc=KC),
            in_=d_xT.rearrange("kc p t -> p kc t")), sem="XB")
        xb_ld = P.c("XB")
        for r in range(NCORE):
            P.dma("sync", lambda e, r=r: e.dma_start(
                out=att1[:, (2 * r) * T:(2 * r + 1) * T], in_=d_a2a1o[0][r]),
                waits=[("CC2a", 1)] if r == 0 else [], sem="YA")
        ld_att1A = ("YA", P.c("YA"))
        for r in range(NCORE):
            P.dma("sync", lambda e, r=r: e.dma_start(
                out=att1[:, (2 * r + 1) * T:(2 * r + 2) * T], in_=d_a2a1o[1][r]),
                waits=[("CC2b", 1)] if r == 0 else [], sem="YB")
        ld_att1B = ("YB", P.c("YB"))
        wo_tm = {}
        wo_ev = {}
        cast_mk = {}
        sq_mk = {}
        stat_tm = {}
        wo_seq = [(h, ec) for h in range(2) for ec in range(KC)]

        def emit_stat_mms(s2):
            h2, ec2 = wo_seq[s2]
            cb = (sc16a, sc16b)[(s2 % 4) // 2]
            qb = (sq16a, sq16b)[(s2 % 4) // 2]
            co = (s2 % 2) * 512
            P.add("tensor", lambda e, h2=h2, ec2=ec2, cb=cb, co=co: e.matmul(
                psG[64 * h2:64 * h2 + 1, 0:512], csb16[:, 0:1], cb[:, co:co + 512],
                start=(ec2 == 0), stop=(ec2 == KC - 1)),
                [("VV", cast_mk[s2])])
            if h2 == 0:
                sq_out, sq_st = psG[32:33, 0:512], (ec2 == 0)
            else:
                sq_out, sq_st = psE[0:1, 0:512], (ec2 == 0)
            P.add("tensor", lambda e, ec2=ec2, qb=qb, co=co, sq_out=sq_out, sq_st=sq_st: e.matmul(
                sq_out, csb16[:, 0:1], qb[:, co:co + 512],
                start=sq_st, stop=(ec2 == KC - 1)),
                [("VV", sq_mk[s2])], inc="TT")
            stat_tm[s2] = P.c("TT")

        for s, (h, ec) in enumerate(wo_seq):
            us = "U%d" % (s % 4)
            P.dma("gpsimd", lambda e, ec=ec, s=s: e.dma_start(
                out=wosb[s % 4][:, :].rearrange("p (kc c) -> p kc c", kc=KC),
                in_=d_wot[ec].rearrange("kc p c -> p kc c")),
                waits=[("TT", wo_tm.get(s - 4, 0))], sem=us)
            wdm = (us, P.c(us))
            kco = list(range(0, KC, 2)) + list(range(1, KC, 2))
            bank = (psA, psB, psC, psD)[s % 4]
            for ki, kc in enumerate(kco):
                wv = []
                if ki == 0:
                    wv = [wdm, ld_att1A, ("VV", wo_ev.get(s - 4, 0))]
                elif ki == 8:
                    wv = [ld_att1B]
                P.add("tensor", lambda e, ec=ec, h=h, kc=kc, ki=ki, s=s, bank=bank: e.matmul(
                    bank[:, 0:512], wosb[s % 4][:, kc * 128:(kc + 1) * 128],
                    att1[:, kc * T + h * 512: kc * T + (h + 1) * 512],
                    start=(ki == 0), stop=(ki == KC - 1)),
                    wv, inc="TT" if ki == KC - 1 else None)
            tmark = P.c("TT")
            wo_tm[s] = tmark
            P.add("vector", lambda e, ec=ec, h=h, bank=bank: e.tensor_tensor(
                xb2[:, ec * T + h * 512: ec * T + (h + 1) * 512], bank[:, 0:512],
                xb2[:, ec * T + h * 512: ec * T + (h + 1) * 512], ALU.add),
                [("TT", tmark), ("XB", xb_ld)], inc="VV")
            wo_ev[s] = P.c("VV")
            cb = (sc16a, sc16b)[(s % 4) // 2]
            qb = (sq16a, sq16b)[(s % 4) // 2]
            co = (s % 2) * 512
            P.add("vector", lambda e, ec=ec, h=h, cb=cb, co=co: e.tensor_copy(
                cb[:, co:co + 512], xb2[:, ec * T + h * 512: ec * T + (h + 1) * 512]),
                [("TT", stat_tm.get(s - 4, 0))], inc="VV")
            cast_mk[s] = P.c("VV")
            P.add("vector", lambda e, cb=cb, qb=qb, co=co: e.tensor_tensor(
                qb[:, co:co + 512], cb[:, co:co + 512], cb[:, co:co + 512], ALU.mult),
                [], inc="VV")
            sq_mk[s] = P.c("VV")
            if s >= 1:
                emit_stat_mms(s - 1)
        emit_stat_mms(2 * KC - 1)
        v_wo_all = P.c("VV")
        P.add("vector", lambda e: e.tensor_copy(
            strow[0:1, T + 512:2 * T], psE[0:1, 0:512]),
            [("TT", stat_tm[2 * KC - 1])], inc="VV")
        v_ex2h1 = P.c("VV")
        wstack.close()

        h16stack = ExitStack()
        h16 = self.sb("h16", [128, KC * T], BF16, h16stack)
        sc_last = {}
        for h in range(2):
            hs_ = h * 512
            rbb = psE if h == 0 else psC
            abb = psF if h == 0 else psD
            sp_sem = "SP" if h == 0 else "SP2"
            w0 = [("TT", stat_tm[2 * KC - 1])]
            P.add("vector", lambda e, hs_=hs_, h=h: e.tensor_copy(
                strow[0:1, hs_:hs_ + 512], psG[64 * h:64 * h + 1, 0:512]), w0, inc="VV")
            if h == 0:
                P.add("vector", lambda e: e.tensor_copy(
                    strow[0:1, T:T + 512], psG[32:33, 0:512]), [], inc="VV")
            P.add("vector", lambda e, hs_=hs_: e.tensor_tensor(
                strow[0:1, 2 * T + hs_:2 * T + hs_ + 512], strow[0:1, hs_:hs_ + 512],
                strow[0:1, hs_:hs_ + 512], ALU.mult), [], inc="VV")
            P.add("vector", lambda e, hs_=hs_: e.tensor_tensor(
                strow[0:1, T + hs_:T + hs_ + 512], strow[0:1, T + hs_:T + hs_ + 512],
                strow[0:1, 2 * T + hs_:2 * T + hs_ + 512], ALU.subtract), [], inc="VV")
            v_var = P.c("VV")
            P.tap("strow%d" % h, strow[0:1, :], [1, 4 * T], F32)
            P.add("scalar", lambda e, hs_=hs_: e.activation(
                strow[0:1, 2 * T + hs_:2 * T + hs_ + 512], strow[0:1, T + hs_:T + hs_ + 512],
                AF.Ln, bias=csb[0:1, 2:3]), [("VV", v_var)], inc="SC")
            P.add("scalar", lambda e, hs_=hs_: e.activation(
                strow[0:1, 3 * T + hs_:3 * T + hs_ + 512],
                strow[0:1, 2 * T + hs_:2 * T + hs_ + 512], AF.Exp, scale=-0.5), [], inc="SC")
            s_r = P.c("SC")
            P.add("vector", lambda e, hs_=hs_: e.tensor_tensor(
                strow[0:1, 2 * T + hs_:2 * T + hs_ + 512], strow[0:1, hs_:hs_ + 512],
                strow[0:1, 3 * T + hs_:3 * T + hs_ + 512], ALU.mult), [("SC", s_r)], inc="VV")
            v_a = P.c("VV")
            wb1 = [("SC", s_r)] + ([("VV", v_wo_all)] if h == 1 else [("VV", v_ex2h1)])
            P.add("tensor", lambda e, hs_=hs_, rbb=rbb: e.matmul(
                rbb[:, 0:512], crow32[0:1, 0:128],
                strow[0:1, 3 * T + hs_:3 * T + hs_ + 512], start=True, stop=True), wb1)
            wb2 = [("VV", v_a)] + ([("VV", v_wo_all)] if h == 1 else [])
            P.add("tensor", lambda e, hs_=hs_, abb=abb: e.matmul(
                abb[:, 0:512], crow32[0:1, 0:128],
                strow[0:1, 2 * T + hs_:2 * T + hs_ + 512], start=True, stop=True),
                wb2, inc="TT")
            t_bch = P.c("TT")
            for kc in range(KC):
                P.dma("sync", lambda e, kc=kc, hs_=hs_: e.dma_start(
                    out=d_x1sp[kc][:, hs_:hs_ + 512],
                    in_=xb2[:, kc * T + hs_:kc * T + hs_ + 512]),
                    waits=[("VV", wo_ev[h * KC + kc])], sem=sp_sem)
            sp_half = P.c(sp_sem)
            ap_mk = {}
            for kc in range(KC):
                w = []
                if kc == 0:
                    w = [("TT", t_bch), (sp_sem, sp_half)]
                    if h == 1:
                        w.append(("SC", sc_last[0]))
                P.add("vector", lambda e, kc=kc, hs_=hs_, rbb=rbb: e.tensor_tensor(
                    tmp32[:, 0:512], xb2[:, kc * T + hs_:kc * T + hs_ + 512],
                    rbb[:, 0:512], ALU.mult), w, inc="VV")
                P.add("vector", lambda e, kc=kc, hs_=hs_, abb=abb: e.tensor_tensor(
                    xb2[:, kc * T + hs_:kc * T + hs_ + 512], tmp32[:, 0:512],
                    abb[:, 0:512], ALU.subtract), [], inc="VV")
                ap_mk[kc] = P.c("VV")
                P.dma("sync", lambda e, kc=kc, hs_=hs_: e.dma_start(
                    out=d_hssp[kc][:, hs_:hs_ + 512],
                    in_=xb2[:, kc * T + hs_:kc * T + hs_ + 512]),
                    waits=[("VV", ap_mk[kc])], sem=sp_sem)
            P.add("vector", lambda e, hs_=hs_: e.tensor_scalar(
                strow[0:1, 2 * T + hs_:2 * T + hs_ + 512],
                strow[0:1, T + hs_:T + hs_ + 512], 1e-5, None, ALU.add), [], inc="VV")
            P.add("vector", lambda e, hs_=hs_: e.reciprocal(
                strow[0:1, hs_:hs_ + 512], strow[0:1, 2 * T + hs_:2 * T + hs_ + 512]),
                [], inc="VV")
            P.add("vector", lambda e, hs_=hs_: e.tensor_tensor(
                strow[0:1, 2 * T + hs_:2 * T + hs_ + 512], strow[0:1, T + hs_:T + hs_ + 512],
                strow[0:1, hs_:hs_ + 512], ALU.mult), [], inc="VV")
            v_rr = P.c("VV")
            P.add("scalar", lambda e, hs_=hs_: e.activation(
                strow[0:1, hs_:hs_ + 512], strow[0:1, 2 * T + hs_:2 * T + hs_ + 512],
                AF.Ln, bias=csb[0:1, 3:4]), [("VV", v_rr)], inc="SC")
            P.add("scalar", lambda e, hs_=hs_: e.activation(
                strow[0:1, 3 * T + hs_:3 * T + hs_ + 512], strow[0:1, hs_:hs_ + 512],
                AF.Exp, scale=-0.5), [], inc="SC")
            s_r2 = P.c("SC")
            P.add("tensor", lambda e, hs_=hs_, rbb=rbb: e.matmul(
                rbb[:, 0:512], crow32[0:1, 0:128],
                strow[0:1, 3 * T + hs_:3 * T + hs_ + 512], start=True, stop=True),
                [("SC", s_r2), ("VV", ap_mk[KC - 1])], inc="TT")
            t_b2 = P.c("TT")
            sc_mk = {}
            for kc in range(KC):
                wv = [("TT", t_b2)] if kc == 0 else []
                if kc >= 2:
                    wv.append(("SC", sc_mk[kc - 2]))
                P.add("vector", lambda e, kc=kc, hs_=hs_, rbb=rbb: e.tensor_tensor(
                    tmp32[:, (kc % 2) * 512:(kc % 2) * 512 + 512],
                    xb2[:, kc * T + hs_:kc * T + hs_ + 512], rbb[:, 0:512], ALU.mult),
                    wv, inc="VV")
                vmk = P.c("VV")
                stg = (sc16a, sq16a)[kc % 2]
                qs = "R%d" % (4 * h + 2 * (kc // 8) + kc % 2)
                ws = [("VV", vmk)]
                if h == 0 and kc < 2:
                    ws.append(("TT", stat_tm[2 * KC - 1]))
                lin = h * KC + kc
                if lin >= 2:
                    hp, kcp = divmod(lin - 2, KC)
                    pq = "R%d" % (4 * hp + 2 * (kcp // 8) + kcp % 2)
                    ws.append((pq, P.c(pq)))
                P.add("scalar", lambda e, kc=kc, stg=stg: e.activation(
                    stg[:, 0:512], tmp32[:, (kc % 2) * 512:(kc % 2) * 512 + 512],
                    AF.Copy, scale=lnw_sb[:, kc:kc + 1]), ws, inc="SC")
                P.add("scalar", lambda e, kc=kc, hs_=hs_: e.activation(
                    h16[:, kc * T + hs_:kc * T + hs_ + 512],
                    tmp32[:, (kc % 2) * 512:(kc % 2) * 512 + 512],
                    AF.Copy, scale=lnw_sb[:, KC + kc:KC + kc + 1]), [], inc="SC")
                sc_mk[kc] = P.c("SC")
                P.dma("sync", lambda e, kc=kc, h=h, stg=stg: e.dma_start(
                    out=d_ag2i[2 * h + kc // 8][kc % 8], in_=stg[:, 0:512]),
                    waits=[("SC", sc_mk[kc])], sem=qs)
                if kc % 8 == 7:
                    q4 = 2 * h + kc // 8
                    ra, rb_ = "R%d" % (2 * q4), "R%d" % (2 * q4 + 1)
                    P.add("gpsimd", lambda e, q4=q4: e.collective_compute(
                        "AllGather", ALU.bypass, replica_groups=[list(range(NCORE))],
                        ins=[d_ag2i[q4].opt()], outs=[d_ag2o[q4].opt()]),
                        [(ra, P.c(ra)), (rb_, P.c(rb_))], inc="CC3" + "abcd"[q4])
            sc_last[h] = sc_mk[KC - 1]
        P.tap("x1", d_x1sp, [KC, 128, T], F32)
        P.tap("hs", d_hssp, [KC, 128, T], F32)
        xstack2.close()
        scstack.close()

        P.barrier()
        a2stack = ExitStack()
        q2 = [self.sb("q2_%d" % i, [128, S], BF16, a2stack) for i in range(2)]
        k2 = [self.sb("k2_%d" % i, [128, S], BF16, a2stack) for i in range(2)]
        v2 = [self.sb("v2_%d" % i, [128, NB * VSTRIDE + 64], BF16, a2stack) for i in range(2)]
        w8stack = ExitStack()
        w216 = self.sb("w216", [128, KC * 6 * 128], BF16, w8stack)
        xst = [self.sb("xst%d" % i, [128, KC * 512], BF16, w8stack) for i in range(2)]
        vtmp2 = self.sb("vtmp2", [128, 512], BF16, w8stack)
        P.dma("gpsimd", lambda e: e.dma_start(
            out=w216[:, :].rearrange("p (kc m c) -> p kc m c", kc=KC, m=6),
            in_=d_w2qkv.rearrange("kc m p c -> p kc m c")), sem="WD")
        wd_w2 = P.c("WD")
        for hh in range(2):
            P.add("vector", lambda e, hh=hh: e.memset(
                v2[hh][:, :NB * VSTRIDE].rearrange("p (n c) -> p n c", c=VSTRIDE)[:, :, 128:129], 1.0),
                [], inc="VV")

        def a2_evac(tc, m, bank, tmark):
            hh, role = m // 3, m % 3
            cols = slice(tc * 512, (tc + 1) * 512)
            if role == 0:
                P.add("scalar", lambda e, hh=hh, cols=cols, bank=bank: e.activation(
                    q2[hh][:, cols], bank[:, 0:512], AF.Copy), [("TT", tmark)], inc="SC")
                return ("SC", P.c("SC"))
            if role == 1:
                P.add("scalar", lambda e, hh=hh, cols=cols, bank=bank: e.activation(
                    k2[hh][:, cols], bank[:, 0:512], AF.Copy), [("TT", tmark)], inc="SC")
                return ("SC", P.c("SC"))
            P.add("vector", lambda e, bank=bank: e.tensor_copy(vtmp2[:, :], bank[:, 0:512]),
                  [("TT", tmark), ("TT", t_tr_prev[0])], inc="VV")
            vm = P.c("VV")
            for j in range(4):
                P.add("tensor", lambda e, j=j: e.transpose(
                    psT[:, j * 128:(j + 1) * 128], vtmp2[:, j * 128:(j + 1) * 128], id16[:, :]),
                    [("VV", vm), ("VV", v_tr_prev[0])] if j == 0 else [],
                    inc="TT" if j == 3 else None)
            ttm = P.c("TT")
            t_tr_prev[0] = ttm
            for j in range(4):
                bidx = 4 * tc + j
                P.add("vector", lambda e, hh=hh, j=j, bidx=bidx: e.tensor_copy(
                    v2[hh][:, bidx * VSTRIDE:bidx * VSTRIDE + 128], psT[:, j * 128:(j + 1) * 128]),
                    [("TT", ttm)] if j == 0 else [], inc="VV" if j == 3 else None)
            v_tr_prev[0] = P.c("VV")
            return ("VV", v_tr_prev[0])

        emit_qkv_produce("a2", d_ag2o, ("CC3a", "CC3b", "CC3c", "CC3d"), w216, wd_w2, 6, a2_evac, xst)
        w8stack.close()

        P.barrier()
        pt2 = self.sb("pt2", [128, 3 * 1024], BF16, a2stack)
        out20 = self.sb("out20", [128, S], BF16, a2stack)
        out21 = self.sb("out21", [128, S], BF16, a2stack)
        emit_attention(q2[0], k2[0], v2[0], out20, pt2, mixer=False, means=None)
        stage_a2a(out20, d_a2a2i[0], d_a2a2o[0], "CC4a", "SG3")
        emit_attention(q2[1], k2[1], v2[1], out21, pt2, mixer=False, means=None)
        stage_a2a(out21, d_a2a2i[1], d_a2a2o[1], "CC4b", "SG4")
        a2stack.close()

        P.barrier()
        ph10stack = ExitStack()
        att2 = self.sb("att2", [128, KC * T], BF16, ph10stack)
        P.dma("gpsimd", lambda e: e.dma_start(
            out=wota16[:, :].rearrange("p (dc r) -> p dc r", dc=KC),
            in_=d_wotao.rearrange("dc p r -> p dc r")), waits=[("CC5", 1)], sem="WT")
        wt_ld = P.c("WT")
        for r in range(NCORE):
            P.dma("sync", lambda e, r=r: e.dma_start(
                out=att2[:, (2 * r) * T:(2 * r + 1) * T], in_=d_a2a2o[0][r]),
                waits=[("CC4a", 1)] if r == 0 else [], sem="YA")
        ld_att2A = ("YA", P.c("YA"))
        for r in range(NCORE):
            P.dma("sync", lambda e, r=r: e.dma_start(
                out=att2[:, (2 * r + 1) * T:(2 * r + 2) * T], in_=d_a2a2o[1][r]),
                waits=[("CC4b", 1)] if r == 0 else [], sem="YB")
        ld_att2B = ("YB", P.c("YB"))
        dco = list(range(0, KC, 2)) + list(range(1, KC, 2))
        for h in range(2):
            bank = (psA, psB)[h]
            for di, dc in enumerate(dco):
                wv = []
                if di == 0:
                    wv = [ld_att2A, ("WT", wt_ld)]
                elif di == 8:
                    wv = [ld_att2B]
                P.add("tensor", lambda e, h=h, dc=dc, di=di, bank=bank: e.matmul(
                    bank[0:16, 0:512], wota16[:, dc * 16:(dc + 1) * 16],
                    att2[:, dc * T + h * 512: dc * T + (h + 1) * 512],
                    start=(di == 0), stop=(di == KC - 1)),
                    wv, inc="TT" if di == KC - 1 else None)
            tmark = P.c("TT")
            P.add("scalar", lambda e, h=h, bank=bank: e.activation(
                m1_32[0:16, h * 512:(h + 1) * 512], bank[0:16, 0:512], AF.Tanh),
                [("TT", tmark)], inc="SC")
        s_tanh = P.c("SC")
        for h in range(2):
            bank = (psE, psF)[h]
            for dc in range(KC):
                P.add("tensor", lambda e, h=h, dc=dc, bank=bank: e.matmul(
                    bank[0:16, 0:512], a16[:, dc * 16:(dc + 1) * 16],
                    h16[:, dc * T + h * 512: dc * T + (h + 1) * 512],
                    start=(dc == 0), stop=(dc == KC - 1)), [],
                    inc="TT" if dc == KC - 1 else None)
            tmark = P.c("TT")
            P.add("vector", lambda e, h=h, bank=bank: e.scalar_tensor_tensor(
                modhA[0:16, h * 512:(h + 1) * 512], bank[0:16, 0:512], 0.1,
                m1_32[0:16, h * 512:(h + 1) * 512], ALU.mult, ALU.mult),
                [("TT", tmark), ("SC", s_tanh)], inc="VV")
        v_modhA = P.c("VV")
        P.tap("modhA", modhA[0:16, :], [16, T], BF16)
        ph10stack.close()

        P.barrier()
        gustack = ExitStack()
        gu = self.sb("gu", [128, FC * T], BF16, gustack, side="right")
        ph11stack = ExitStack()
        winb = [self.sb("winb%d" % i, [128, KC * 128], BF16, ph11stack) for i in range(2)]
        wupb = [self.sb("wupb%d" % i, [128, KC * 128], BF16, ph11stack) for i in range(2)]
        sil32 = [self.sb("sil32_%d" % i, [128, 512], F32, ph11stack) for i in range(2)]
        b16s = [self.sb("b16s%d" % i, [16, 16 * 128], BF16, ph11stack) for i in range(2)]
        b_marks = {}
        f_tm = {}
        f_sil = {}
        f_guv = {}

        def load_bstrip(g4):
            us = "U%d" % (2 + g4 % 2)
            P.dma("gpsimd", lambda e, g4=g4: e.dma_start(
                out=b16s[g4 % 2][0:16, :].rearrange("p (fc c) -> p fc c", fc=16),
                in_=d_bmat[g4 * 16:(g4 + 1) * 16].rearrange("fc p c -> p fc c")),
                waits=[("TT", f_tm.get((g4 - 1) * 16 - 1, 0))], sem=us)
            b_marks[g4] = (us, P.c(us))

        load_bstrip(0)
        load_bstrip(1)
        for fc in range(FC):
            if fc >= 16 and fc % 16 == 0 and (fc // 16 + 1) < 4:
                load_bstrip(fc // 16 + 1)
            us = "U%d" % (fc % 2)
            P.dma("gpsimd", lambda e, fc=fc: e.dma_start(
                out=winb[fc % 2][:, :].rearrange("p (kc c) -> p kc c", kc=KC),
                in_=d_win[fc].rearrange("kc p c -> p kc c")),
                waits=[("TT", f_tm.get(fc - 2, 0))], sem=us)
            P.dma("gpsimd", lambda e, fc=fc: e.dma_start(
                out=wupb[fc % 2][:, :].rearrange("p (kc c) -> p kc c", kc=KC),
                in_=d_wup[fc].rearrange("kc p c -> p kc c")), sem=us)
            wdm = (us, P.c(us))
            g4 = fc // 16
            for h in range(2):
                gbank = (psA, psB)[h]
                ubank = (psC, psD)[h]
                for kc in range(KC):
                    wv = [wdm, b_marks[g4],
                          ("SC", f_sil.get((fc - 1, h), 0)), ("VV", v_modhA)] if kc == 0 else []
                    P.add("tensor", lambda e, fc=fc, h=h, kc=kc, gbank=gbank: e.matmul(
                        gbank[:, 0:512], winb[fc % 2][:, kc * 128:(kc + 1) * 128],
                        h16[:, kc * T + h * 512: kc * T + (h + 1) * 512],
                        start=(kc == 0), stop=False), wv)
                P.add("tensor", lambda e, fc=fc, h=h, gbank=gbank, g4=g4: e.matmul(
                    gbank[:, 0:512], b16s[g4 % 2][0:16, (fc % 16) * 128:(fc % 16 + 1) * 128],
                    modhA[0:16, h * 512:(h + 1) * 512],
                    start=False, stop=True), [], inc="TT")
                tg = P.c("TT")
                for kc in range(KC):
                    wv = [("VV", f_guv.get((fc - 1, h), 0))] if kc == 0 else []
                    P.add("tensor", lambda e, fc=fc, h=h, kc=kc, ubank=ubank: e.matmul(
                        ubank[:, 0:512], wupb[fc % 2][:, kc * 128:(kc + 1) * 128],
                        h16[:, kc * T + h * 512: kc * T + (h + 1) * 512],
                        start=(kc == 0), stop=(kc == KC - 1)), wv,
                        inc="TT" if kc == KC - 1 else None)
                tu = P.c("TT")
                if h == 1:
                    f_tm[fc] = tu
                P.add("scalar", lambda e, h=h, gbank=gbank: e.activation(
                    sil32[h][:, :], gbank[:, 0:512], AF.Silu), [("TT", tg)], inc="SC")
                f_sil[(fc, h)] = P.c("SC")
                P.add("vector", lambda e, fc=fc, h=h, ubank=ubank: e.tensor_tensor(
                    gu[:, fc * T + h * 512: fc * T + (h + 1) * 512], ubank[:, 0:512],
                    sil32[h][:, :], ALU.mult), [("SC", f_sil[(fc, h)]), ("TT", tu)], inc="VV")
                f_guv[(fc, h)] = P.c("VV")
        ph11stack.close()
        h16stack.close()

        P.barrier()
        ph12stack = ExitStack()
        wdnb = [self.sb("wdnb%d" % i, [128, FC * 128], BF16, ph12stack) for i in range(2)]
        x1t = [self.sb("x1t%d" % i, [128, T], F32, ph12stack) for i in range(2)]
        hst = [self.sb("hst%d" % i, [128, T], F32, ph12stack) for i in range(2)]
        outst = [self.sb("outst%d" % i, [128, T], F32, ph12stack) for i in range(2)]
        d_tm = {}
        d_ev = {}
        st_marks = {}
        for ec in range(KC):
            us = "U%d" % (ec % 2)
            qs = "Q%d" % (ec % 2)
            qo = "Q%d" % (2 + ec % 2)
            P.dma("gpsimd", lambda e, ec=ec: e.dma_start(
                out=wdnb[ec % 2][:, :].rearrange("p (fc c) -> p fc c", fc=FC),
                in_=d_wdn[ec].rearrange("fc p c -> p fc c")),
                waits=[("TT", d_tm.get(ec - 2, 0))], sem=us)
            wdm = (us, P.c(us))
            P.dma("sync", lambda e, ec=ec: e.dma_start(out=x1t[ec % 2][:, :], in_=d_x1sp[ec]),
                  waits=[("VV", d_ev.get(ec - 2, 0))], sem=qs)
            P.dma("sync", lambda e, ec=ec: e.dma_start(out=hst[ec % 2][:, :], in_=d_hssp[ec]),
                  sem=qs)
            ldm = (qs, P.c(qs))
            for h in range(2):
                bank = (psA, psB)[h]
                for fc in range(FC):
                    wv = [wdm, ("VV", d_ev.get((ec - 1, h), 0))] if fc == 0 else []
                    P.add("tensor", lambda e, ec=ec, h=h, fc=fc, bank=bank: e.matmul(
                        bank[:, 0:512], wdnb[ec % 2][:, fc * 128:(fc + 1) * 128],
                        gu[:, fc * T + h * 512: fc * T + (h + 1) * 512],
                        start=(fc == 0), stop=(fc == FC - 1)), wv,
                        inc="TT" if fc == FC - 1 else None)
                tmark = P.c("TT")
                if h == 1:
                    d_tm[ec] = tmark
                P.add("vector", lambda e, ec=ec, h=h, bank=bank: e.tensor_tensor(
                    tmp32[:, h * 512:(h + 1) * 512], bank[:, 0:512],
                    x1t[ec % 2][:, h * 512:(h + 1) * 512], ALU.add),
                    [("TT", tmark), ldm, (qo, st_marks.get(ec - 2, 0))], inc="VV")
                P.add("vector", lambda e, ec=ec, h=h: e.tensor_tensor(
                    outst[ec % 2][:, h * 512:(h + 1) * 512], tmp32[:, h * 512:(h + 1) * 512],
                    hst[ec % 2][:, h * 512:(h + 1) * 512], ALU.add), [], inc="VV")
                d_ev[(ec, h)] = P.c("VV")
            d_ev[ec] = P.c("VV")
            P.dma("sync", lambda e, ec=ec: e.dma_start(out=d_out[ec], in_=outst[ec % 2][:, :]),
                  waits=[("VV", d_ev[ec])], sem=qo)
            st_marks[ec] = P.c(qo)
        P.add("sync", lambda e: e.nop(),
              waits=[("Q2", P.c("Q2")), ("Q3", P.c("Q3"))])
        P.add("gpsimd", lambda e: e.nop(),
              waits=[("Q2", P.c("Q2")), ("Q3", P.c("Q3"))])

        self.emit_and_finalize()
        ph12stack.close()
        gustack.close()
        self.ctx.close()
        return nc


def build_program(debug_taps=()):
    b = Builder(debug_taps)
    b.build()
    return b.nc


def _host_stage(inputs):
    x = np.asarray(inputs["L0_attn_x"])
    wq = np.asarray(inputs["L0_attn_wq"])
    wk = np.asarray(inputs["L0_attn_wk"])
    wv = np.asarray(inputs["L0_attn_wv"])
    wo = np.asarray(inputs["L0_attn_wo"])
    ln1w = np.asarray(inputs["L0_ffn_ln1_w"])
    ln2w = np.asarray(inputs["L0_ffn_ln2_w"])
    Wq2 = np.asarray(inputs["L0_ffn_Wq"])
    Wk2 = np.asarray(inputs["L0_ffn_Wk"])
    Wv2 = np.asarray(inputs["L0_ffn_Wv"])
    Wo2 = np.asarray(inputs["L0_ffn_Wo"])
    A = np.asarray(inputs["L0_ffn_A"])
    B = np.asarray(inputs["L0_ffn_B"])
    Win = np.asarray(inputs["L0_ffn_W_in_base"])
    Wup = np.asarray(inputs["L0_ffn_W_up_base"])
    Wdn = np.asarray(inputs["L0_ffn_W_down"])

    perm = np.concatenate([np.arange(0, 128, 2), np.arange(1, 128, 2)])
    inv_freq = 1.0 / (10000.0 ** (np.arange(0, 128, 2, dtype=np.float32) / 128.0))
    ang = np.arange(S, dtype=np.float32)[:, None] * inv_freq[None, :]
    cossin = np.empty((128, S), np.float32)
    cossin[0:64] = np.cos(ang).T
    cossin[64:128] = np.sin(ang).T

    wot = np.ascontiguousarray(wo.reshape(KC, 128, KC, 128).transpose(2, 0, 1, 3)).astype(np.float32)
    winT = np.ascontiguousarray(Win.reshape(FC, 128, KC, 128).transpose(0, 2, 3, 1)).astype(np.float32)
    wupT = np.ascontiguousarray(Wup.reshape(FC, 128, KC, 128).transpose(0, 2, 3, 1)).astype(np.float32)
    wdnT = np.ascontiguousarray(Wdn.reshape(KC, 128, FC, 128).transpose(0, 2, 3, 1)).astype(np.float32)
    afull = np.ascontiguousarray(A.reshape(KC, 128, 16)).astype(np.float32)
    bmat = np.ascontiguousarray(B.reshape(16, FC, 128).transpose(1, 0, 2)).astype(np.float32)
    lnw_arr = np.empty((128, 2 * KC), np.float32)
    lnw_arr[:, :KC] = ln1w.reshape(KC, 128).T
    lnw_arr[:, KC:] = ln2w.reshape(KC, 128).T
    consts = np.zeros((128, 4), np.float32)
    consts[:, 0] = 1.0 / 2048.0
    consts[:, 1] = 1.0
    consts[:, 2] = 1e-5
    consts[:, 3] = 1e-6
    crow = np.ones((1, 256), np.float32)
    ident = np.eye(128, dtype=np.float32)
    ii = np.arange(128)
    tril01 = (ii[:, None] > ii[None, :]).astype(np.float32)
    trilneg = (tril01 * NEG_CAUSAL).astype(np.float32)

    in_maps = []
    for c in range(NCORE):
        sl = slice(c * T, (c + 1) * T)
        xTc = np.ascontiguousarray(x[0, sl, :].T).reshape(KC, 128, T).astype(np.float32)
        q0 = wq[:, (2 * c) * 128:(2 * c + 1) * 128][:, perm]
        q1 = wq[:, (2 * c + 1) * 128:(2 * c + 2) * 128][:, perm]
        kk = wk[:, c * 128:(c + 1) * 128][:, perm]
        vv = wv[:, c * 128:(c + 1) * 128]
        mw = np.stack([q0, q1, kk, vv], axis=1)
        mwqkv = np.ascontiguousarray(
            mw.reshape(KC, 128, 4, 128).transpose(0, 2, 1, 3)).astype(np.float32)
        h0 = slice((2 * c) * 128, (2 * c + 1) * 128)
        h1 = slice((2 * c + 1) * 128, (2 * c + 2) * 128)
        w2 = np.stack([Wq2[h0].T, Wk2[h0].T, Wv2[h0].T,
                       Wq2[h1].T, Wk2[h1].T, Wv2[h1].T], axis=1)
        w2qkv = np.ascontiguousarray(
            w2.reshape(KC, 128, 6, 128).transpose(0, 2, 1, 3)).astype(np.float32)
        esl = slice(c * 256, (c + 1) * 256)
        wo2s = np.ascontiguousarray(Wo2[esl].reshape(2, 128, KC, 128).transpose(0, 2, 1, 3)).astype(np.float32)
        a2s = np.ascontiguousarray(A[esl].reshape(2, 128, 16)).astype(np.float32)
        in_maps.append({
            "xT": xTc, "cossin": cossin,
            "mwqkv": mwqkv, "wot": wot, "w2qkv": w2qkv,
            "wo2s": wo2s, "a2s": a2s, "afull": afull, "bmat": bmat,
            "lnw": lnw_arr, "winT": winT, "wupT": wupT, "wdnT": wdnT,
            "consts": consts, "crow": crow, "ident": ident,
            "tril01": tril01, "trilneg": trilneg,
        })
    return in_maps


def kernel(**inputs):
    in_maps = _host_stage(inputs)
    nc = build_program()
    res = run_bass_kernel_spmd(nc, in_maps, core_ids=list(range(NCORE)))
    full = np.empty((1, S, D), np.float32)
    for c in range(NCORE):
        o = np.asarray(res.results[c]["out"], np.float32)
        full[0, c * T:(c + 1) * T, :] = o.reshape(D, T).T
    return full


# version 28
# speedup vs baseline: 1.0071x; 1.0013x over previous
import math
from contextlib import ExitStack

import numpy as np
import ml_dtypes

import concourse.bass as bass
import concourse.mybir as mybir
from concourse.bass_utils import run_bass_kernel_spmd
from concourse.aot_env import allow_ici_all_to_all

F32 = mybir.dt.float32
BF16 = mybir.dt.bfloat16
AF = mybir.ActivationFunctionType
ALU = mybir.AluOpType
AX = mybir.AxisListType

NCORE = 8
S = 8192
D = 2048
T = 1024
KC = 16
NB = 64
FC = 64
ISD = 1.0 / math.sqrt(128.0)
NEG_CAUSAL = -1.0e6
NEG_PEN = -113152.0

ENGINES = ("sync", "gpsimd", "tensor", "vector", "scalar")
OWN_COMPUTE = {"tensor": "TT", "vector": "VV", "scalar": "SC"}
VSTRIDE = 129


class Builder:
    def __init__(self, debug_taps=()):
        self.nc = bass.Bass("TRN2", target_bir_lowering=False)
        self.ops = []
        self.cnt = {}
        self.pending = {}
        self.ctx = ExitStack()
        self.sems = {}
        self.debug_taps = set(debug_taps)
        self.slot_pv = [0, 0, 0, 0]
        self.bank_exp = [0, 0]
        self.pv_free = [0, 0, 0, 0]
        self.g_mi = 0
        self.gidx = 0
        self.rb_free = {}

    def c(self, sem):
        return self.cnt.get(sem, 0)

    def add(self, eng, fn, waits=(), inc=None, n=1):
        w = list(self.pending.pop(eng, [])) + [(s, v) for (s, v) in waits if v > 0]
        self.ops.append((eng, fn, w, inc, n))
        if inc is not None:
            self.cnt[inc] = self.cnt.get(inc, 0) + n
        return self.cnt.get(inc, 0) if inc else None

    def dma(self, eng, fn, waits=(), sem="LD"):
        return self.add(eng, fn, waits, inc=sem, n=16)

    def barrier(self):
        snap = [(s, v) for (s, v) in self.cnt.items()
                if v > 0 and not s.startswith("CC")]
        for e in ENGINES:
            own = OWN_COMPUTE.get(e)
            self.pending[e] = [(s, v) for (s, v) in snap if s != own]

    def sb(self, name, shape, dt, stack=None, side=None):
        return (stack or self.ctx).enter_context(
            self.nc.sbuf_tensor(name, shape, dt, side=side))

    def tap(self, name, ap_src, shape, dt):
        if name not in self.debug_taps:
            return
        dst = self.nc.dram_tensor("dbg_" + name, shape, dt, kind="ExternalOutput").ap()
        waits = [(s, v) for (s, v) in self.cnt.items() if v > 0]
        self.dma("sync", lambda e: e.dma_start(out=dst, in_=ap_src), waits=waits)

    def emit_and_finalize(self):
        block = self.ctx.enter_context(self.nc.Block())
        sems, ops = self.sems, self.ops

        def engine_body(en):
            def f(eng):
                for (oen, fn, waits, inc, n) in ops:
                    if oen != en:
                        continue
                    for (s, v) in waits:
                        eng.wait_ge(sems[s], v)
                    inst = fn(eng)
                    if inc is not None:
                        inst.then_inc(sems[inc], n)
            return f

        block.sync(engine_body("sync"))
        block.gpsimd(engine_body("gpsimd"))
        block.tensor(engine_body("tensor"))
        block.vector(engine_body("vector"))
        block.scalar(engine_body("scalar"))

    def build(self):
        nc = self.nc
        P = self
        EI = lambda name, shape, dt=F32: nc.dram_tensor(name, shape, dt, kind="ExternalInput").ap()
        EO = lambda name, shape, dt=F32: nc.dram_tensor(name, shape, dt, kind="ExternalOutput").ap()
        IN = lambda name, shape, dt: nc.dram_tensor(name, shape, dt).ap()
        SH = lambda name, shape, dt: nc.dram_tensor(
            name, shape, dt, addr_space="Shared").ap()

        d_xT = EI("xT", [KC, 128, T])
        d_cossin = EI("cossin", [128, S])
        d_mwqkv = EI("mwqkv", [KC, 4, 128, 128])
        d_wot = EI("wot", [KC, KC, 128, 128])
        d_w2qkv = EI("w2qkv", [KC, 6, 128, 128])
        d_wo2s = EI("wo2s", [2, KC, 128, 128])
        d_a2s = EI("a2s", [2, 128, 16])
        d_afull = EI("afull", [KC, 128, 16])
        d_bmat = EI("bmat", [FC, 16, 128])
        d_lnw = EI("lnw", [128, 2 * KC])
        d_win = EI("winT", [FC, KC, 128, 128])
        d_wup = EI("wupT", [FC, KC, 128, 128])
        d_wdn = EI("wdnT", [KC, FC, 128, 128])
        d_consts = EI("consts", [128, 4])
        d_crow = EI("crow", [1, 256])
        d_ident = EI("ident", [128, 128])
        d_tril = EI("tril01", [128, 128])
        d_trilneg = EI("trilneg", [128, 128])
        d_out = EO("out", [KC, 128, T])

        d_ag1i = [IN("ag1_in%d" % i, [KC // 2, 128, 512], BF16) for i in range(4)]
        d_ag1o = [SH("ag1_out%d" % i, [NCORE, KC // 2, 128, 512], BF16) for i in range(4)]
        d_ag2i = [IN("ag2_in%d" % i, [KC // 2, 128, 512], BF16) for i in range(4)]
        d_ag2o = [SH("ag2_out%d" % i, [NCORE, KC // 2, 128, 512], BF16) for i in range(4)]
        d_a2a1i = [IN("a2a1_in%d" % i, [NCORE, 128, T], BF16) for i in range(2)]
        d_a2a1o = [IN("a2a1_out%d" % i, [NCORE, 128, T], BF16) for i in range(2)]
        d_a2a2i = [IN("a2a2_in%d" % i, [NCORE, 128, T], BF16) for i in range(2)]
        d_a2a2o = [IN("a2a2_out%d" % i, [NCORE, 128, T], BF16) for i in range(2)]
        d_wotai = IN("wota_in", [KC, 128, 16], F32)
        d_wotao = SH("wota_out", [KC, 128, 16], F32)
        d_x1sp = IN("x1_sp", [KC, 128, T], F32)
        d_hssp = IN("hs_sp", [KC, 128, T], F32)

        for s in ("LD", "WD", "TT", "VV", "SC", "CC1a", "CC1b", "CC1c", "CC1d",
                  "CC2a", "CC2b",
                  "CC3a", "CC3b", "CC3c", "CC3d", "CC4a", "CC4b", "CC5",
                  "Q0", "Q1", "Q2", "Q3", "Q4", "Q5", "Q6", "Q7", "Q8", "Q9",
                  "R0", "R1", "R2", "R3", "R4", "R5", "R6", "R7", "WI",
                  "U0", "U1", "U2", "U3",
                  "YA", "YB", "SG1", "SG2", "SG3", "SG4", "SP", "SP2", "WT", "XB"):
            self.sems[s] = self.ctx.enter_context(nc.semaphore(s))

        csb = self.sb("csb", [128, 4], F32)
        csb16 = self.sb("csb16", [128, 4], BF16)
        crow16 = self.sb("crow16", [1, 256], BF16)
        crow32 = self.sb("crow32", [1, 256], F32)
        id16 = self.sb("id16", [128, 128], BF16)
        tril_sb = self.sb("tril_sb", [128, 128], F32)
        trilneg_sb = self.sb("trilneg_sb", [128, 128], F32)
        lnw_sb = self.sb("lnw_sb", [128, 2 * KC], F32)
        tmp32 = self.sb("tmp32", [128, 1024], F32)
        wota16 = self.sb("wota16", [128, KC * 16], BF16)
        a16 = self.sb("a16", [128, KC * 16], BF16)
        modhA = self.sb("modhA", [16, T], BF16)
        m1_32 = self.sb("m1_32", [16, T], F32)
        grow4 = self.sb("grow4", [1, 512], BF16)
        gt32_4 = self.sb("gt32_4", [64, 512], F32)
        cmp4 = self.sb("cmp4", [64, 512], BF16)
        msk_sb = self.sb("msk_sb", [128, 128], F32)
        rc_sb = self.sb("rc_sb", [128, 4], F32)
        otm16 = self.sb("otm16", [128, 512], BF16)

        scstack = ExitStack()
        strow = self.sb("strow", [1, 4 * T], F32, scstack, side="right")
        sc16a = self.sb("sc16a", [128, T], BF16, scstack, side="right")
        sc16b = self.sb("sc16b", [128, T], BF16, scstack, side="right")
        sq16a = self.sb("sq16a", [128, T], BF16, scstack, side="right")
        sq16b = self.sb("sq16b", [128, T], BF16, scstack, side="right")
        mwstack = ExitStack()
        mw16 = self.sb("mw16", [128, KC * 4 * 128], BF16, mwstack, side="right")
        xstack1 = ExitStack()
        xbuf = self.sb("xbuf", [128, KC * T], F32, xstack1, side="right")

        scU = self.ctx.enter_context(nc.psum_tensor("scU", [128, 1024], F32))
        scV = self.ctx.enter_context(nc.psum_tensor("scV", [128, 1024], F32))
        psE = self.ctx.enter_context(nc.psum_tensor("psE", [128, 512], F32))
        psF = self.ctx.enter_context(nc.psum_tensor("psF", [128, 512], F32))
        psG = self.ctx.enter_context(nc.psum_tensor("psG", [128, 512], F32))
        psT = self.ctx.enter_context(nc.psum_tensor("psT", [128, 1024], BF16))

        class PsView:
            def __init__(self, t, off):
                self.t, self.off = t, off

            def __getitem__(self, idx):
                p, c = idx
                return self.t[p, slice(c.start + self.off, c.stop + self.off)]

        psA = PsView(scU, 0)
        psB = PsView(scU, 512)
        psC = PsView(scV, 0)
        psD = PsView(scV, 512)
        PS = [psA, psB, psC, psD, psE, psF]

        P.dma("sync", lambda e: e.dma_start(out=csb[:, :], in_=d_consts))
        P.dma("sync", lambda e: e.dma_start(out=crow32[:, :], in_=d_crow))
        P.dma("sync", lambda e: e.dma_start(out=tril_sb[:, :], in_=d_tril))
        P.dma("sync", lambda e: e.dma_start(out=trilneg_sb[:, :], in_=d_trilneg))
        P.dma("sync", lambda e: e.dma_start(out=lnw_sb[:, :], in_=d_lnw))
        ld_c = ("LD", P.c("LD"))
        x_marks = {}
        for pc, eng in enumerate(("scalar", "sync", "gpsimd", "gpsimd")):
            P.dma(eng, lambda e, pc=pc: e.dma_start(
                out=xbuf[:, pc * 4 * T:(pc + 1) * 4 * T].rearrange(
                    "p (kc t) -> p kc t", kc=4),
                in_=d_xT[4 * pc:4 * pc + 4].rearrange("kc p t -> p kc t")),
                sem="R%d" % pc)
            for kc in range(4 * pc, 4 * pc + 4):
                x_marks[kc] = ("R%d" % pc, P.c("R%d" % pc))
        P.dma("gpsimd", lambda e: e.dma_start(out=csb16[:, :], in_=d_consts), sem="WD")
        P.dma("gpsimd", lambda e: e.dma_start(out=crow16[:, :], in_=d_crow), sem="WD")
        P.dma("gpsimd", lambda e: e.dma_start(out=id16[:, :], in_=d_ident), sem="WD")
        P.dma("gpsimd", lambda e: e.dma_start(
            out=a16[:, :].rearrange("p (dc r) -> p dc r", dc=KC),
            in_=d_afull.rearrange("dc p r -> p dc r")), sem="WD")
        ld0 = P.c("LD")

        wostack = ExitStack()
        wo2sb = self.sb("wo2sb", [128, 2 * KC * 128], BF16, wostack)
        a2sb = self.sb("a2sb", [128, 2 * 16], BF16, wostack)
        wotp = self.sb("wotp", [128, KC * 16], F32, wostack)
        P.dma("gpsimd", lambda e: e.dma_start(
            out=wo2sb[:, :].rearrange("p (e dc c) -> p e dc c", e=2, dc=KC),
            in_=d_wo2s.rearrange("e dc p c -> p e dc c")), sem="WD")
        P.dma("gpsimd", lambda e: e.dma_start(
            out=a2sb[:, :].rearrange("p (e r) -> p e r", e=2),
            in_=d_a2s.rearrange("e p r -> p e r")), sem="WD")
        P.dma("gpsimd", lambda e: e.dma_start(
            out=mw16[:, :].rearrange("p (kc m c) -> p kc m c", kc=KC, m=4),
            in_=d_mwqkv.rearrange("kc m p c -> p kc m c")), sem="WD")
        wd0 = P.c("WD")
        wd_mw = wd0
        wd_wo2 = wd0
        v_prev = 0
        for dc in range(KC):
            for e2 in range(2):
                P.add("tensor", lambda e, dc=dc, e2=e2: e.matmul(
                    psG[0:128, 0:16],
                    wo2sb[:, (e2 * KC + dc) * 128:(e2 * KC + dc + 1) * 128],
                    a2sb[:, e2 * 16:(e2 + 1) * 16],
                    start=(e2 == 0), stop=(e2 == 1)),
                    ([("WD", wd_wo2), ("VV", v_prev)] if e2 == 0 else []),
                    inc="TT" if e2 == 1 else None)
            tmark = P.c("TT")
            P.add("vector", lambda e, dc=dc: e.tensor_copy(
                wotp[:, dc * 16:(dc + 1) * 16], psG[:, 0:16]), [("TT", tmark)], inc="VV")
            v_prev = P.c("VV")
        P.dma("sync", lambda e: e.dma_start(
            out=d_wotai.rearrange("dc p r -> p dc r"),
            in_=wotp[:, :].rearrange("p (dc r) -> p dc r", dc=KC)),
            waits=[("VV", v_prev)], sem="WI")
        ld_wotai = P.c("WI")
        wostack.close()

        def emit_stats(src, *, with_mean, eps_col, prereq, x_marks=None):
            sc16 = (sc16a, sc16b)
            sq16 = (sq16a, sq16b)
            marks = []
            for kc in range(KC):
                w = list(prereq) if kc < 2 else [("TT", marks[kc - 2])]
                if x_marks is not None:
                    w = [x_marks[kc]] + w
                cm = P.add("vector", lambda e, kc=kc: e.tensor_copy(
                    sc16[kc % 2][:, :], src[:, kc * T:(kc + 1) * T]), w, inc="VV")
                sm = P.add("vector", lambda e, kc=kc: e.tensor_tensor(
                    sq16[kc % 2][:, :], sc16[kc % 2][:, :], sc16[kc % 2][:, :], ALU.mult), [], inc="VV")
                for h in range(2):
                    if with_mean:
                        P.add("tensor", lambda e, kc=kc, h=h: e.matmul(
                            psA[(32 * h):(32 * h + 1), 0:512], csb16[:, 0:1],
                            sc16[kc % 2][:, h * 512:(h + 1) * 512],
                            start=(kc == 0), stop=(kc == KC - 1)),
                            [("VV", cm)] if h == 0 else [])
                    P.add("tensor", lambda e, kc=kc, h=h: e.matmul(
                        psB[(32 * h):(32 * h + 1), 0:512], csb16[:, 0:1],
                        sq16[kc % 2][:, h * 512:(h + 1) * 512],
                        start=(kc == 0), stop=(kc == KC - 1)),
                        [("VV", sm)] if h == 0 else [],
                        inc="TT" if h == 1 else None)
                marks.append(P.c("TT"))
            t_sums = P.c("TT")
            if with_mean:
                P.add("vector", lambda e: e.tensor_copy(strow[0:1, 0:512], psA[0:1, 0:512]),
                      [("TT", t_sums)], inc="VV")
                P.add("vector", lambda e: e.tensor_copy(strow[0:1, 512:1024], psA[32:33, 0:512]), [], inc="VV")
            P.add("vector", lambda e: e.tensor_copy(strow[0:1, T:T + 512], psB[0:1, 0:512]),
                  [("TT", t_sums)], inc="VV")
            P.add("vector", lambda e: e.tensor_copy(strow[0:1, T + 512:2 * T], psB[32:33, 0:512]), [], inc="VV")
            if with_mean:
                P.add("vector", lambda e: e.tensor_tensor(
                    strow[0:1, 2 * T:3 * T], strow[0:1, 0:T], strow[0:1, 0:T], ALU.mult), [], inc="VV")
                P.add("vector", lambda e: e.tensor_tensor(
                    strow[0:1, T:2 * T], strow[0:1, T:2 * T], strow[0:1, 2 * T:3 * T], ALU.subtract), [], inc="VV")
            v_var = P.c("VV")
            P.add("scalar", lambda e: e.activation(
                strow[0:1, 2 * T:3 * T], strow[0:1, T:2 * T], AF.Ln,
                bias=csb[0:1, eps_col:eps_col + 1]),
                [("VV", v_var)], inc="SC")
            P.add("scalar", lambda e: e.activation(
                strow[0:1, 3 * T:4 * T], strow[0:1, 2 * T:3 * T], AF.Exp, scale=-0.5), [], inc="SC")
            s_rstd = P.c("SC")
            if with_mean:
                P.add("vector", lambda e: e.tensor_tensor(
                    strow[0:1, 2 * T:3 * T], strow[0:1, 0:T], strow[0:1, 3 * T:4 * T], ALU.mult),
                    [("SC", s_rstd)], inc="VV")
            return P.c("VV"), s_rstd

        def emit_bcasts(*, with_mean, s_rstd, v_a):
            P.add("tensor", lambda e: e.matmul(psE[:, 0:512], crow32[0:1, 0:128],
                                               strow[0:1, 3 * T:3 * T + 512], start=True, stop=True),
                  [("SC", s_rstd)])
            P.add("tensor", lambda e: e.matmul(psF[:, 0:512], crow32[0:1, 0:128],
                                               strow[0:1, 3 * T + 512:4 * T], start=True, stop=True),
                  [], inc="TT")
            if with_mean:
                P.add("tensor", lambda e: e.matmul(psC[:, 0:512], crow32[0:1, 0:128],
                                                   strow[0:1, 2 * T:2 * T + 512], start=True, stop=True),
                      [("VV", v_a)])
                P.add("tensor", lambda e: e.matmul(psD[:, 0:512], crow32[0:1, 0:128],
                                                   strow[0:1, 2 * T + 512:3 * T], start=True, stop=True),
                      [], inc="TT")
            return P.c("TT")

        v0, s0 = emit_stats(xbuf, with_mean=True, eps_col=2,
                            prereq=[ld_c, ("WD", wd0)], x_marks=x_marks)
        t_bc0 = emit_bcasts(with_mean=True, s_rstd=s0, v_a=v0)
        for h in range(2):
            rb = psE if h == 0 else psF
            ab = psC if h == 0 else psD
            for kc in range(KC):
                dst = (sc16a, sq16a)[kc % 2]
                qs = "R%d" % (4 * h + 2 * (kc // 8) + kc % 2)
                w = [("TT", t_bc0)]
                lin = h * KC + kc
                if lin >= 2:
                    hp, kcp = divmod(lin - 2, KC)
                    pq = "R%d" % (4 * hp + 2 * (kcp // 8) + kcp % 2)
                    w.append((pq, P.c(pq)))
                P.add("vector", lambda e, kc=kc, h=h, rb=rb: e.tensor_tensor(
                    tmp32[:, 0:512], xbuf[:, kc * T + h * 512: kc * T + (h + 1) * 512],
                    rb[:, 0:512], ALU.mult), w, inc="VV")
                P.add("vector", lambda e, dst=dst, ab=ab: e.tensor_tensor(
                    dst[:, 0:512], tmp32[:, 0:512], ab[:, 0:512], ALU.subtract),
                    [], inc="VV")
                vm = P.c("VV")
                P.dma("sync", lambda e, kc=kc, h=h, dst=dst: e.dma_start(
                    out=d_ag1i[2 * h + kc // 8][kc % 8], in_=dst[:, 0:512]),
                    waits=[("VV", vm)], sem=qs)
                if kc % 8 == 7:
                    q4 = 2 * h + kc // 8
                    ra, rb_ = "R%d" % (2 * q4), "R%d" % (2 * q4 + 1)
                    P.add("gpsimd", lambda e, q4=q4: e.collective_compute(
                        "AllGather", ALU.bypass, replica_groups=[list(range(NCORE))],
                        ins=[d_ag1i[q4].opt()], outs=[d_ag1o[q4].opt()]),
                        [(ra, P.c(ra)), (rb_, P.c(rb_))], inc="CC1" + "abcd"[q4])
        P.add("gpsimd", lambda e: e.collective_compute(
            "AllReduce", ALU.add, replica_groups=[list(range(NCORE))],
            ins=[d_wotai.opt()], outs=[d_wotao.opt()]), [("WI", ld_wotai)], inc="CC5")
        xstack1.close()

        P.barrier()
        mstackA = ExitStack()
        aQ0 = self.sb("aQ0", [128, S], BF16, mstackA)
        aQ1 = self.sb("aQ1", [128, S], BF16, mstackA)
        aK = self.sb("aK", [128, S], BF16, mstackA)
        aV = self.sb("aV", [128, NB * VSTRIDE + 64], BF16, mstackA)
        means16 = self.sb("means16", [128, 64], BF16, mstackA)
        m3stack = ExitStack()
        lnst = [self.sb("lnst%d" % i, [128, KC * 512], BF16, m3stack) for i in range(2)]
        cosb = self.sb("cosb", [64, S], BF16, m3stack)
        sinb = self.sb("sinb", [64, S], BF16, m3stack)
        rtmp = self.sb("rtmp", [64, 4 * 512], BF16, m3stack)
        rblo = [self.sb("rblo%d" % i, [64, 512], BF16, m3stack) for i in range(2)]
        rbhi = [self.sb("rbhi%d" % i, [64, 512], BF16, m3stack) for i in range(2)]
        vtmp = self.sb("vtmp", [128, 512], BF16, m3stack)
        P.add("vector", lambda e: e.memset(
            aV[:, :NB * VSTRIDE].rearrange("p (n c) -> p n c", c=VSTRIDE)[:, :, 128:129], 1.0),
            [], inc="VV")

        v_tr_prev = [0]
        t_tr_prev = [0]

        def emit_qkv_produce(nm, d_src, cc_sem, wsb, wd_w, nroles, on_evac, strbufs):
            ev = {}
            ld_marks = {}
            tt_roles = {}
            tc_list = list(range(0, 16, 2)) + list(range(1, 16, 2))
            for idx, tc in enumerate(tc_list):
                rk, hf = tc // 2, tc % 2
                slot = idx % 2
                for j in range(2):
                    qj = "Q%d" % (2 + 2 * j + slot)
                    if idx < 2:
                        w = [(cc_sem[2 * hf + j], 1)]
                    else:
                        w = ([("TT", tt_roles.get((tc_list[idx - 2], nroles - 1), 0))]
                             if j == 0 else [])
                        if idx in (8, 9):
                            w.append((cc_sem[2 + j], 1))
                    P.dma("sync", lambda e, rk=rk, hf=hf, slot=slot, j=j: e.dma_start(
                        out=strbufs[slot][:, j * 8 * 512:(j + 1) * 8 * 512]
                            .rearrange("p (kc t) -> p kc t", kc=KC // 2),
                        in_=d_src[2 * hf + j][rk].rearrange("kc p t -> p kc t")),
                        waits=w, sem=qj)
                    ld_marks[(tc, j)] = (qj, P.c(qj))
                for m in range(nroles):
                    si = idx * nroles + m
                    bank = PS[si % 6]
                    prev = si - 6
                    wv = []
                    if prev >= 0:
                        wv.append(ev[(tc_list[prev // nroles], prev % nroles)])
                    for kc in range(KC):
                        if kc == 0:
                            w_mm = [ld_marks[(tc, 0)], ("WD", wd_w)] + wv
                        elif kc == 8:
                            w_mm = [ld_marks[(tc, 1)]]
                        else:
                            w_mm = []
                        P.add("tensor", lambda e, m=m, kc=kc, slot=slot, bank=bank: e.matmul(
                            bank[:, 0:512],
                            wsb[:, (kc * nroles + m) * 128:(kc * nroles + m + 1) * 128],
                            strbufs[slot][:, kc * 512:(kc + 1) * 512],
                            start=(kc == 0), stop=(kc == KC - 1)),
                            w_mm, inc="TT" if kc == KC - 1 else None)
                    tt_roles[(tc, m)] = P.c("TT")
                    ev[(tc, m)] = on_evac(tc, m, bank, tt_roles[(tc, m)])
            return ev

        def mx_evac(tc, m, bank, tmark):
            cols = slice(tc * 512, (tc + 1) * 512)
            if m < 3:
                dst = (aQ0, aQ1, aK)[m]
                cs = slice(tc * 512, (tc + 1) * 512)
                rl, rh = rblo[m % 2], rbhi[m % 2]
                P.add("scalar", lambda e, bank=bank, rl=rl: e.activation(
                    rl[:, :], bank[0:64, 0:512], AF.Copy),
                    [("TT", tmark), ld_cs, ("VV", self.rb_free.get(m % 2, 0))], inc="SC")
                P.add("scalar", lambda e, bank=bank, rh=rh: e.activation(
                    rh[:, :], bank[64:128, 0:512], AF.Copy), [], inc="SC")
                scm = P.c("SC")
                P.add("vector", lambda e, rl=rl, cs=cs: e.tensor_tensor(
                    rtmp[:, 0:512], rl[:, :], cosb[:, cs], ALU.mult),
                    [("SC", scm)], inc="VV")
                P.add("vector", lambda e, rh=rh, cs=cs: e.tensor_tensor(
                    rtmp[:, 512:1024], rh[:, :], sinb[:, cs], ALU.mult), [], inc="VV")
                P.add("vector", lambda e, rl=rl, cs=cs: e.tensor_tensor(
                    rtmp[:, 1024:1536], rl[:, :], sinb[:, cs], ALU.mult), [], inc="VV")
                P.add("vector", lambda e, rh=rh, cs=cs: e.tensor_tensor(
                    rtmp[:, 1536:2048], rh[:, :], cosb[:, cs], ALU.mult), [], inc="VV")
                self.rb_free[m % 2] = P.c("VV")
                P.add("vector", lambda e, dst=dst, cols=cols: e.tensor_tensor(
                    dst[0:64, cols], rtmp[:, 0:512], rtmp[:, 512:1024], ALU.subtract), [], inc="VV")
                P.add("vector", lambda e, dst=dst, cols=cols: e.tensor_tensor(
                    dst[64:128, cols], rtmp[:, 1024:1536], rtmp[:, 1536:2048], ALU.add), [], inc="VV")
                return ("VV", P.c("VV"))
            P.add("vector", lambda e, bank=bank: e.tensor_copy(vtmp[:, :], bank[:, 0:512]),
                  [("TT", tmark), ("TT", t_tr_prev[0])], inc="VV")
            vm = P.c("VV")
            for j in range(4):
                P.add("tensor", lambda e, j=j: e.transpose(
                    psT[:, j * 128:(j + 1) * 128], vtmp[:, j * 128:(j + 1) * 128], id16[:, :]),
                    [("VV", vm), ("VV", v_tr_prev[0])] if j == 0 else [],
                    inc="TT" if j == 3 else None)
            ttm = P.c("TT")
            t_tr_prev[0] = ttm
            for j in range(4):
                bidx = 4 * tc + j
                P.add("vector", lambda e, j=j, bidx=bidx: e.tensor_copy(
                    aV[:, bidx * VSTRIDE:bidx * VSTRIDE + 128], psT[:, j * 128:(j + 1) * 128]),
                    [("TT", ttm)] if j == 0 else [], inc="VV" if j == 3 else None)
            v_tr_prev[0] = P.c("VV")
            return ("VV", v_tr_prev[0])

        P.dma("gpsimd", lambda e: e.dma_start(out=cosb[:, :], in_=d_cossin[0:64, :]), sem="WD")
        P.dma("gpsimd", lambda e: e.dma_start(out=sinb[:, :], in_=d_cossin[64:128, :]), sem="WD")
        ld_cs = ("WD", P.c("WD"))
        ev_mx = emit_qkv_produce("mx", d_ag1o, ("CC1a", "CC1b", "CC1c", "CC1d"), mw16, wd_mw, 4, mx_evac, lnst)

        def _means_op(e):
            with nc.allow_low_precision(reason="gating block sums feed comparisons only"):
                return e.reduce_sum(
                    means16[:, :].rearrange("p (n o) -> p n o", o=1),
                    aK[:, :].rearrange("p (n w) -> p n w", w=128), axis=AX.X)
        P.add("vector", _means_op, [ev_mx[(15, 2)]], inc="VV")
        P.tap("q0", aQ0[:, 0:1024], [128, 1024], BF16)
        P.tap("k", aK[:, 0:1024], [128, 1024], BF16)
        m3stack.close()
        mwstack.close()

        SCB2 = [scU, scV]

        def pvloc(i):
            return (psE if i < 2 else psF), (0 if i % 2 == 0 else 256)

        def emit_pen_precompute(Qs, means, penbuf):
            m_s3 = 0
            m_s4 = 0
            m_s6 = 0
            m_s8 = 0
            for tb in range(14):
                t0 = 8 + 4 * tb
                w1 = [("VV", m_s3), ("VV", m_s8)]
                w2 = [("VV", m_s4)]
                if tb == 0:
                    w1 += [("VV", self.g_mi)]
                    w2 += [("VV", self.pv_free[0]), ("VV", self.pv_free[1])]
                for i in range(4):
                    P.add("tensor", lambda e, t=t0 + i, i=i: e.matmul(
                        psG[0:1, i * 128:(i + 1) * 128], means[:, t:t + 1],
                        Qs[:, t * 128:(t + 1) * 128], start=True, stop=True),
                        w1 if i == 0 else [], inc=None)
                for i in range(4):
                    P.add("tensor", lambda e, t=t0 + i, i=i: e.matmul(
                        psE[0:64, i * 128:(i + 1) * 128], means[:, 0:64],
                        Qs[:, t * 128:(t + 1) * 128], start=True, stop=True),
                        w2 if i == 0 else [],
                        inc="TT" if i == 3 else None)
                m_s12 = P.c("TT")
                P.add("vector", lambda e: e.tensor_copy(grow4[0:1, :], psG[0:1, 0:512]),
                      [("TT", m_s12)], inc="VV")
                m_s3 = P.c("VV")
                P.add("vector", lambda e: e.tensor_copy(gt32_4[:, :], psE[0:64, 0:512]),
                      [], inc="VV")
                m_s4 = P.c("VV")
                w5 = [("VV", m_s4), ("VV", m_s6)]
                if tb == 0:
                    w5 += [("VV", self.pv_free[2]), ("VV", self.pv_free[3])]
                for i in range(4):
                    P.add("tensor", lambda e, i=i: e.matmul(
                        psF[0:64, i * 128:(i + 1) * 128], crow16[0:1, 0:64],
                        grow4[0:1, i * 128:(i + 1) * 128], start=True, stop=True),
                        w5 if i == 0 else [],
                        inc="TT" if i == 3 else None)
                m_s5 = P.c("TT")
                P.add("vector", lambda e: e.tensor_tensor(
                    cmp4[:, :], gt32_4[:, :], psF[0:64, 0:512], ALU.is_ge),
                    [("TT", m_s5)], inc="VV")
                m_s6 = P.c("VV")
                for i in range(4):
                    P.add("tensor", lambda e, t=t0 + i, i=i: e.matmul(
                        psG[0:1, i * 128:(i + 1) * 128], csb16[0:t + 1, 1:2],
                        cmp4[0:t + 1, i * 128:(i + 1) * 128], start=True, stop=True),
                        [("VV", m_s6)] if i == 0 else [],
                        inc="TT" if i == 3 else None)
                m_s7 = P.c("TT")
                P.add("vector", lambda e, t0=t0: e.tensor_scalar(
                    penbuf[0:1, (t0 - 8) * 128:(t0 - 4) * 128], psG[0:1, 0:512],
                    8.5, NEG_PEN, ALU.is_gt, ALU.mult), [("TT", m_s7)], inc="VV")
                m_s8 = P.c("VV")
            m_mi = 0
            for tb in range(14):
                for i in range(4):
                    t = 8 + 4 * tb + i
                    P.add("tensor", lambda e, t=t: e.matmul(
                        psG[:, 256:384], crow16[0:1, 0:128],
                        penbuf[0:1, (t - 8) * 128:(t - 7) * 128],
                        start=True, stop=True),
                        [("VV", m_s8 if (tb == 0 and i == 0) else 0), ("VV", m_mi)],
                        inc="TT")
                    m_bc = P.c("TT")
                    P.add("vector", lambda e, t=t: e.tensor_tensor(
                        mskbuf[:, (t - 8) * 128:(t - 7) * 128], tril_sb[:, :],
                        psG[:, 256:384], ALU.mult), [("TT", m_bc)], inc="VV")
                    m_mi = P.c("VV")
            self.pv_free[0] = max(self.pv_free[0], m_s4)
            self.pv_free[1] = max(self.pv_free[1], m_s4)
            self.pv_free[2] = max(self.pv_free[2], m_s6)
            self.pv_free[3] = max(self.pv_free[3], m_s6)
            self.g_mi = m_mi
            return P.c("VV")

        def emit_attention(Qs, Ks, Vtm, outb, ptb, *, mixer, means, penbuf=None):
            if mixer:
                v_pen = emit_pen_precompute(Qs, means, penbuf)
            pending = []
            norm_q = []

            def emit_norm_step():
                if not norm_q:
                    return
                i, t, vm = norm_q.pop(0)
                P.add("tensor", lambda e, i=i: e.transpose(
                    psT[:, i * 128:(i + 1) * 128], otm16[:, i * 128:(i + 1) * 128], id16[:, :]),
                    [("VV", vm), ("VV", v_tr_prev[0])], inc="TT")
                tm2 = P.c("TT")
                P.add("vector", lambda e, i=i, t=t: e.tensor_copy(
                    outb[:, t * 128:(t + 1) * 128], psT[:, i * 128:(i + 1) * 128]),
                    [("TT", tm2)], inc="VV")
                v_tr_prev[0] = P.c("VV")

            def emit_pv(ctx):
                slot, grp, rr, smark = ctx
                for j, (i, b, pj) in enumerate(grp):
                    t = 4 * rr + i
                    last = (b == t) if not (mixer and rr < 2) else (b == 7)
                    pvb, off = pvloc(i)
                    wv = [("SC", smark)] if j == 0 else []
                    if b == 0:
                        wv.append(("VV", self.pv_free[i]))
                        if i % 2 == 0:
                            wv.append(("VV", self.pv_free[i + 1]))
                    P.add("tensor", lambda e, b=b, pj=pj, slot=slot, last=last,
                          i=i, pvb=pvb, off=off: e.matmul(
                        pvb[:, off:off + 129],
                        ptb[:, slot * 1024 + pj * 128: slot * 1024 + (pj + 1) * 128],
                        Vtm[:, b * VSTRIDE:b * VSTRIDE + 129],
                        start=(b == 0 and i % 2 == 0), stop=last),
                        wv, inc="TT" if j == len(grp) - 1 else None)
                self.slot_pv[slot] = P.c("TT")

            def push_pv(ctx):
                pending.append(ctx)
                if len(pending) > 2:
                    emit_pv(pending.pop(0))

            def flush():
                while pending:
                    emit_pv(pending.pop(0))

            for r in range(16):
                n_common = 8 if (mixer and r < 2) else 4 * r
                qcols = slice(512 * r, 512 * r + 512)
                for pb in range(n_common // 2):
                    b0 = 2 * pb
                    g = self.gidx; self.gidx += 1
                    sc = SCB2[g % 2]
                    slot = g % 4
                    P.add("tensor", lambda e, b0=b0, sc=sc, qcols=qcols: e.matmul(
                        sc[:, 0:512], Ks[:, b0 * 128:(b0 + 1) * 128], Qs[:, qcols],
                        start=True, stop=True),
                        [("SC", self.bank_exp[g % 2])], inc=None)
                    P.add("tensor", lambda e, b0=b0, sc=sc, qcols=qcols: e.matmul(
                        sc[:, 512:1024], Ks[:, (b0 + 1) * 128:(b0 + 2) * 128], Qs[:, qcols],
                        start=True, stop=True), [], inc="TT")
                    tmark = P.c("TT")
                    emit_norm_step()
                    P.add("scalar", lambda e, sc=sc, slot=slot: e.activation(
                        ptb[:, slot * 1024:(slot + 1) * 1024], sc[:, 0:1024], AF.Exp, scale=ISD),
                        [("TT", tmark), ("TT", self.slot_pv[slot])], inc="SC")
                    self.bank_exp[g % 2] = P.c("SC")
                    push_pv((slot,
                             [(i, b0, i) for i in (0, 2, 1, 3)]
                             + [(i, b0 + 1, 4 + i) for i in (0, 2, 1, 3)],
                             r, P.c("SC")))
                if not (mixer and r < 2):
                    pieces = [(i, 4 * r + b) for i in range(4) for b in range(i + 1)]
                    groups = [pieces[0:8], pieces[8:10]]
                    rag_slots = []
                    for grp in groups:
                        g = self.gidx; self.gidx += 1
                        sc = SCB2[g % 2]
                        slot = g % 4
                        rag_slots.append(slot)
                        for j, (i, b) in enumerate(grp):
                            t = 4 * r + i
                            wv = [("SC", self.bank_exp[g % 2])] if j == 0 else []
                            P.add("tensor", lambda e, b=b, j=j, t=t, sc=sc: e.matmul(
                                sc[:, j * 128:(j + 1) * 128],
                                Ks[:, b * 128:(b + 1) * 128], Qs[:, t * 128:(t + 1) * 128],
                                start=True, stop=True),
                                wv, inc="TT" if j == len(grp) - 1 else None)
                        gmark = P.c("TT")
                        vmask = 0
                        for j, (i, b) in enumerate(grp):
                            t = 4 * r + i
                            if b != t:
                                continue
                            if not mixer:
                                P.add("vector", lambda e, j=j, sc=sc: e.tensor_tensor(
                                    sc[:, j * 128:(j + 1) * 128], sc[:, j * 128:(j + 1) * 128],
                                    trilneg_sb[:, :], ALU.add), [("TT", gmark)], inc="VV")
                                vmask = P.c("VV")
                            else:
                                P.add("vector", lambda e, j=j, t=t, sc=sc: e.tensor_tensor(
                                    sc[:, j * 128:(j + 1) * 128], sc[:, j * 128:(j + 1) * 128],
                                    mskbuf[:, (t - 8) * 128:(t - 7) * 128], ALU.add),
                                    [("TT", gmark), ("VV", v_pen)], inc="VV")
                                vmask = P.c("VV")
                        wexp = [("TT", gmark), ("TT", self.slot_pv[slot])]
                        if vmask:
                            wexp.append(("VV", vmask))
                        P.add("scalar", lambda e, sc=sc, slot=slot, n=len(grp): e.activation(
                            ptb[:, slot * 1024: slot * 1024 + n * 128], sc[:, 0:n * 128],
                            AF.Exp, scale=ISD), wexp, inc="SC")
                        self.bank_exp[g % 2] = P.c("SC")
                        push_pv((slot, [(i, b, j) for j, (i, b) in enumerate(grp)],
                                 r, P.c("SC")))
                flush()
                while norm_q:
                    emit_norm_step()
                if not (mixer and r < 2):
                    tA = self.slot_pv[rag_slots[0]]
                    tB = self.slot_pv[rag_slots[1]]
                else:
                    tA = tB = P.c("TT")
                for lo, hi, tm in ((0, 2, tA), (2, 4, tB)):
                    rms = []
                    for i in range(lo, hi):
                        pvb, off = pvloc(i)
                        P.add("vector", lambda e, i=i, pvb=pvb, off=off: e.reciprocal(
                            rc_sb[:, i:i + 1], pvb[:, off + 128:off + 129]),
                            [("TT", tm)], inc="VV")
                        rms.append(P.c("VV"))
                    for i in range(lo, hi):
                        t = 4 * r + i
                        pvb, off = pvloc(i)
                        P.add("vector", lambda e, i=i, pvb=pvb, off=off: e.tensor_scalar(
                            otm16[:, i * 128:(i + 1) * 128], pvb[:, off:off + 128],
                            rc_sb[:, i:i + 1], None, ALU.mult),
                            [("VV", rms[1])], inc="VV")
                        self.pv_free[i] = P.c("VV")
                        norm_q.append((i, t, P.c("VV")))
            while norm_q:
                emit_norm_step()

        P.barrier()
        mstackB = ExitStack()
        pt_sb = self.sb("pt_sb", [128, 4 * 1024], BF16, mstackB)
        outh0 = self.sb("outh0", [128, S], BF16, mstackB)
        outh1 = self.sb("outh1", [128, S], BF16, mstackB)
        penbuf = self.sb("penbuf", [1, 56 * 128], BF16, mstackB)
        mskbuf = self.sb("mskbuf", [128, 56 * 128], F32, mstackB)
        def stage_a2a(outb, d_in, d_out, cc, sg):
            v_att = P.c("VV")
            for r in range(NCORE):
                P.dma("sync", lambda e, r=r, outb=outb, d_in=d_in: e.dma_start(
                    out=d_in[r], in_=outb[:, r * T:(r + 1) * T]),
                    waits=[("VV", v_att)] if r == 0 else [], sem=sg)
            ldm = P.c(sg)

            def _op(e, d_in=d_in, d_out=d_out):
                with allow_ici_all_to_all():
                    return e.collective_compute(
                        "AllToAll", ALU.bypass, replica_groups=[list(range(NCORE))],
                        ins=[d_in.opt()], outs=[d_out.opt()])
            P.add("gpsimd", _op, [(sg, ldm)], inc=cc)

        emit_attention(aQ0, aK, aV, outh0, pt_sb, mixer=True, means=means16, penbuf=penbuf)
        stage_a2a(outh0, d_a2a1i[0], d_a2a1o[0], "CC2a", "SG1")
        emit_attention(aQ1, aK, aV, outh1, pt_sb, mixer=True, means=means16, penbuf=penbuf)
        stage_a2a(outh1, d_a2a1i[1], d_a2a1o[1], "CC2b", "SG2")
        mstackB.close()
        mstackA.close()

        P.barrier()
        wstack = ExitStack()
        att1 = self.sb("att1", [128, KC * T], BF16, wstack)
        wosb = [self.sb("wosb%d" % i, [128, KC * 128], BF16, wstack) for i in range(4)]
        xstack2 = ExitStack()
        xb2 = self.sb("xb2", [128, KC * T], F32, xstack2, side="right")
        P.dma("sync", lambda e: e.dma_start(
            out=xb2[:, :].rearrange("p (kc t) -> p kc t", kc=KC),
            in_=d_xT.rearrange("kc p t -> p kc t")), sem="XB")
        xb_ld = P.c("XB")
        for r in range(NCORE):
            P.dma("sync", lambda e, r=r: e.dma_start(
                out=att1[:, (2 * r) * T:(2 * r + 1) * T], in_=d_a2a1o[0][r]),
                waits=[("CC2a", 1)] if r == 0 else [], sem="YA")
        ld_att1A = ("YA", P.c("YA"))
        for r in range(NCORE):
            P.dma("sync", lambda e, r=r: e.dma_start(
                out=att1[:, (2 * r + 1) * T:(2 * r + 2) * T], in_=d_a2a1o[1][r]),
                waits=[("CC2b", 1)] if r == 0 else [], sem="YB")
        ld_att1B = ("YB", P.c("YB"))
        wo_tm = {}
        wo_ev = {}
        cast_mk = {}
        sq_mk = {}
        stat_tm = {}
        wo_seq = [(h, ec) for h in range(2) for ec in range(KC)]

        def emit_stat_mms(s2):
            h2, ec2 = wo_seq[s2]
            cb = (sc16a, sc16b)[(s2 % 4) // 2]
            qb = (sq16a, sq16b)[(s2 % 4) // 2]
            co = (s2 % 2) * 512
            P.add("tensor", lambda e, h2=h2, ec2=ec2, cb=cb, co=co: e.matmul(
                psG[64 * h2:64 * h2 + 1, 0:512], csb16[:, 0:1], cb[:, co:co + 512],
                start=(ec2 == 0), stop=(ec2 == KC - 1)),
                [("VV", cast_mk[s2])])
            if h2 == 0:
                sq_out, sq_st = psG[32:33, 0:512], (ec2 == 0)
            else:
                sq_out, sq_st = psE[0:1, 0:512], (ec2 == 0)
            P.add("tensor", lambda e, ec2=ec2, qb=qb, co=co, sq_out=sq_out, sq_st=sq_st: e.matmul(
                sq_out, csb16[:, 0:1], qb[:, co:co + 512],
                start=sq_st, stop=(ec2 == KC - 1)),
                [("VV", sq_mk[s2])], inc="TT")
            stat_tm[s2] = P.c("TT")

        for s, (h, ec) in enumerate(wo_seq):
            us = "U%d" % (s % 4)
            P.dma("gpsimd", lambda e, ec=ec, s=s: e.dma_start(
                out=wosb[s % 4][:, :].rearrange("p (kc c) -> p kc c", kc=KC),
                in_=d_wot[ec].rearrange("kc p c -> p kc c")),
                waits=[("TT", wo_tm.get(s - 4, 0))], sem=us)
            wdm = (us, P.c(us))
            kco = list(range(0, KC, 2)) + list(range(1, KC, 2))
            bank = (psA, psB, psC, psD)[s % 4]
            for ki, kc in enumerate(kco):
                wv = []
                if ki == 0:
                    wv = [wdm, ld_att1A, ("VV", wo_ev.get(s - 4, 0))]
                elif ki == 8:
                    wv = [ld_att1B]
                P.add("tensor", lambda e, ec=ec, h=h, kc=kc, ki=ki, s=s, bank=bank: e.matmul(
                    bank[:, 0:512], wosb[s % 4][:, kc * 128:(kc + 1) * 128],
                    att1[:, kc * T + h * 512: kc * T + (h + 1) * 512],
                    start=(ki == 0), stop=(ki == KC - 1)),
                    wv, inc="TT" if ki == KC - 1 else None)
            tmark = P.c("TT")
            wo_tm[s] = tmark
            P.add("vector", lambda e, ec=ec, h=h, bank=bank: e.tensor_tensor(
                xb2[:, ec * T + h * 512: ec * T + (h + 1) * 512], bank[:, 0:512],
                xb2[:, ec * T + h * 512: ec * T + (h + 1) * 512], ALU.add),
                [("TT", tmark), ("XB", xb_ld)], inc="VV")
            wo_ev[s] = P.c("VV")
            cb = (sc16a, sc16b)[(s % 4) // 2]
            qb = (sq16a, sq16b)[(s % 4) // 2]
            co = (s % 2) * 512
            P.add("vector", lambda e, ec=ec, h=h, cb=cb, co=co: e.tensor_copy(
                cb[:, co:co + 512], xb2[:, ec * T + h * 512: ec * T + (h + 1) * 512]),
                [("TT", stat_tm.get(s - 4, 0))], inc="VV")
            cast_mk[s] = P.c("VV")
            P.add("vector", lambda e, cb=cb, qb=qb, co=co: e.tensor_tensor(
                qb[:, co:co + 512], cb[:, co:co + 512], cb[:, co:co + 512], ALU.mult),
                [], inc="VV")
            sq_mk[s] = P.c("VV")
            if s >= 1:
                emit_stat_mms(s - 1)
        emit_stat_mms(2 * KC - 1)
        v_wo_all = P.c("VV")
        P.add("vector", lambda e: e.tensor_copy(
            strow[0:1, T + 512:2 * T], psE[0:1, 0:512]),
            [("TT", stat_tm[2 * KC - 1])], inc="VV")
        v_ex2h1 = P.c("VV")
        wstack.close()

        h16stack = ExitStack()
        h16 = self.sb("h16", [128, KC * T], BF16, h16stack)
        sc_last = {}
        for h in range(2):
            hs_ = h * 512
            rbb = psE if h == 0 else psC
            abb = psF if h == 0 else psD
            sp_sem = "SP" if h == 0 else "SP2"
            w0 = [("TT", stat_tm[2 * KC - 1])]
            P.add("vector", lambda e, hs_=hs_, h=h: e.tensor_copy(
                strow[0:1, hs_:hs_ + 512], psG[64 * h:64 * h + 1, 0:512]), w0, inc="VV")
            if h == 0:
                P.add("vector", lambda e: e.tensor_copy(
                    strow[0:1, T:T + 512], psG[32:33, 0:512]), [], inc="VV")
            P.add("vector", lambda e, hs_=hs_: e.tensor_tensor(
                strow[0:1, 2 * T + hs_:2 * T + hs_ + 512], strow[0:1, hs_:hs_ + 512],
                strow[0:1, hs_:hs_ + 512], ALU.mult), [], inc="VV")
            P.add("vector", lambda e, hs_=hs_: e.tensor_tensor(
                strow[0:1, T + hs_:T + hs_ + 512], strow[0:1, T + hs_:T + hs_ + 512],
                strow[0:1, 2 * T + hs_:2 * T + hs_ + 512], ALU.subtract), [], inc="VV")
            v_var = P.c("VV")
            P.tap("strow%d" % h, strow[0:1, :], [1, 4 * T], F32)
            P.add("scalar", lambda e, hs_=hs_: e.activation(
                strow[0:1, 2 * T + hs_:2 * T + hs_ + 512], strow[0:1, T + hs_:T + hs_ + 512],
                AF.Ln, bias=csb[0:1, 2:3]), [("VV", v_var)], inc="SC")
            P.add("scalar", lambda e, hs_=hs_: e.activation(
                strow[0:1, 3 * T + hs_:3 * T + hs_ + 512],
                strow[0:1, 2 * T + hs_:2 * T + hs_ + 512], AF.Exp, scale=-0.5), [], inc="SC")
            s_r = P.c("SC")
            P.add("vector", lambda e, hs_=hs_: e.tensor_tensor(
                strow[0:1, 2 * T + hs_:2 * T + hs_ + 512], strow[0:1, hs_:hs_ + 512],
                strow[0:1, 3 * T + hs_:3 * T + hs_ + 512], ALU.mult), [("SC", s_r)], inc="VV")
            v_a = P.c("VV")
            wb1 = [("SC", s_r)] + ([("VV", v_wo_all)] if h == 1 else [("VV", v_ex2h1)])
            P.add("tensor", lambda e, hs_=hs_, rbb=rbb: e.matmul(
                rbb[:, 0:512], crow32[0:1, 0:128],
                strow[0:1, 3 * T + hs_:3 * T + hs_ + 512], start=True, stop=True), wb1)
            wb2 = [("VV", v_a)] + ([("VV", v_wo_all)] if h == 1 else [])
            P.add("tensor", lambda e, hs_=hs_, abb=abb: e.matmul(
                abb[:, 0:512], crow32[0:1, 0:128],
                strow[0:1, 2 * T + hs_:2 * T + hs_ + 512], start=True, stop=True),
                wb2, inc="TT")
            t_bch = P.c("TT")
            for kc in range(KC):
                P.dma("sync", lambda e, kc=kc, hs_=hs_: e.dma_start(
                    out=d_x1sp[kc][:, hs_:hs_ + 512],
                    in_=xb2[:, kc * T + hs_:kc * T + hs_ + 512]),
                    waits=[("VV", wo_ev[h * KC + kc])], sem=sp_sem)
            sp_half = P.c(sp_sem)
            ap_mk = {}
            for kc in range(KC):
                w = []
                if kc == 0:
                    w = [("TT", t_bch), (sp_sem, sp_half)]
                    if h == 1:
                        w.append(("SC", sc_last[0]))
                P.add("vector", lambda e, kc=kc, hs_=hs_, rbb=rbb: e.tensor_tensor(
                    tmp32[:, 0:512], xb2[:, kc * T + hs_:kc * T + hs_ + 512],
                    rbb[:, 0:512], ALU.mult), w, inc="VV")
                P.add("vector", lambda e, kc=kc, hs_=hs_, abb=abb: e.tensor_tensor(
                    xb2[:, kc * T + hs_:kc * T + hs_ + 512], tmp32[:, 0:512],
                    abb[:, 0:512], ALU.subtract), [], inc="VV")
                ap_mk[kc] = P.c("VV")
                P.dma("sync", lambda e, kc=kc, hs_=hs_: e.dma_start(
                    out=d_hssp[kc][:, hs_:hs_ + 512],
                    in_=xb2[:, kc * T + hs_:kc * T + hs_ + 512]),
                    waits=[("VV", ap_mk[kc])], sem=sp_sem)
            P.add("vector", lambda e, hs_=hs_: e.tensor_scalar(
                strow[0:1, 2 * T + hs_:2 * T + hs_ + 512],
                strow[0:1, T + hs_:T + hs_ + 512], 1e-5, None, ALU.add), [], inc="VV")
            P.add("vector", lambda e, hs_=hs_: e.reciprocal(
                strow[0:1, hs_:hs_ + 512], strow[0:1, 2 * T + hs_:2 * T + hs_ + 512]),
                [], inc="VV")
            P.add("vector", lambda e, hs_=hs_: e.tensor_tensor(
                strow[0:1, 2 * T + hs_:2 * T + hs_ + 512], strow[0:1, T + hs_:T + hs_ + 512],
                strow[0:1, hs_:hs_ + 512], ALU.mult), [], inc="VV")
            v_rr = P.c("VV")
            P.add("scalar", lambda e, hs_=hs_: e.activation(
                strow[0:1, hs_:hs_ + 512], strow[0:1, 2 * T + hs_:2 * T + hs_ + 512],
                AF.Ln, bias=csb[0:1, 3:4]), [("VV", v_rr)], inc="SC")
            P.add("scalar", lambda e, hs_=hs_: e.activation(
                strow[0:1, 3 * T + hs_:3 * T + hs_ + 512], strow[0:1, hs_:hs_ + 512],
                AF.Exp, scale=-0.5), [], inc="SC")
            s_r2 = P.c("SC")
            P.add("tensor", lambda e, hs_=hs_, rbb=rbb: e.matmul(
                rbb[:, 0:512], crow32[0:1, 0:128],
                strow[0:1, 3 * T + hs_:3 * T + hs_ + 512], start=True, stop=True),
                [("SC", s_r2), ("VV", ap_mk[KC - 1])], inc="TT")
            t_b2 = P.c("TT")
            sc_mk = {}
            for kc in range(KC):
                wv = [("TT", t_b2)] if kc == 0 else []
                if kc >= 2:
                    wv.append(("SC", sc_mk[kc - 2]))
                P.add("vector", lambda e, kc=kc, hs_=hs_, rbb=rbb: e.tensor_tensor(
                    tmp32[:, (kc % 2) * 512:(kc % 2) * 512 + 512],
                    xb2[:, kc * T + hs_:kc * T + hs_ + 512], rbb[:, 0:512], ALU.mult),
                    wv, inc="VV")
                vmk = P.c("VV")
                stg = (sc16a, sq16a)[kc % 2]
                qs = "R%d" % (4 * h + 2 * (kc // 8) + kc % 2)
                ws = [("VV", vmk)]
                if h == 0 and kc < 2:
                    ws.append(("TT", stat_tm[2 * KC - 1]))
                lin = h * KC + kc
                if lin >= 2:
                    hp, kcp = divmod(lin - 2, KC)
                    pq = "R%d" % (4 * hp + 2 * (kcp // 8) + kcp % 2)
                    ws.append((pq, P.c(pq)))
                P.add("scalar", lambda e, kc=kc, stg=stg: e.activation(
                    stg[:, 0:512], tmp32[:, (kc % 2) * 512:(kc % 2) * 512 + 512],
                    AF.Copy, scale=lnw_sb[:, kc:kc + 1]), ws, inc="SC")
                P.add("scalar", lambda e, kc=kc, hs_=hs_: e.activation(
                    h16[:, kc * T + hs_:kc * T + hs_ + 512],
                    tmp32[:, (kc % 2) * 512:(kc % 2) * 512 + 512],
                    AF.Copy, scale=lnw_sb[:, KC + kc:KC + kc + 1]), [], inc="SC")
                sc_mk[kc] = P.c("SC")
                P.dma("sync", lambda e, kc=kc, h=h, stg=stg: e.dma_start(
                    out=d_ag2i[2 * h + kc // 8][kc % 8], in_=stg[:, 0:512]),
                    waits=[("SC", sc_mk[kc])], sem=qs)
                if kc % 8 == 7:
                    q4 = 2 * h + kc // 8
                    ra, rb_ = "R%d" % (2 * q4), "R%d" % (2 * q4 + 1)
                    P.add("gpsimd", lambda e, q4=q4: e.collective_compute(
                        "AllGather", ALU.bypass, replica_groups=[list(range(NCORE))],
                        ins=[d_ag2i[q4].opt()], outs=[d_ag2o[q4].opt()]),
                        [(ra, P.c(ra)), (rb_, P.c(rb_))], inc="CC3" + "abcd"[q4])
            sc_last[h] = sc_mk[KC - 1]
        P.tap("x1", d_x1sp, [KC, 128, T], F32)
        P.tap("hs", d_hssp, [KC, 128, T], F32)
        xstack2.close()
        scstack.close()

        P.barrier()
        a2stack = ExitStack()
        q2 = [self.sb("q2_%d" % i, [128, S], BF16, a2stack) for i in range(2)]
        k2 = [self.sb("k2_%d" % i, [128, S], BF16, a2stack) for i in range(2)]
        v2 = [self.sb("v2_%d" % i, [128, NB * VSTRIDE + 64], BF16, a2stack) for i in range(2)]
        w8stack = ExitStack()
        w216 = self.sb("w216", [128, KC * 6 * 128], BF16, w8stack)
        xst = [self.sb("xst%d" % i, [128, KC * 512], BF16, w8stack) for i in range(2)]
        vtmp2 = self.sb("vtmp2", [128, 512], BF16, w8stack)
        P.dma("gpsimd", lambda e: e.dma_start(
            out=w216[:, :].rearrange("p (kc m c) -> p kc m c", kc=KC, m=6),
            in_=d_w2qkv.rearrange("kc m p c -> p kc m c")), sem="WD")
        wd_w2 = P.c("WD")
        for hh in range(2):
            P.add("vector", lambda e, hh=hh: e.memset(
                v2[hh][:, :NB * VSTRIDE].rearrange("p (n c) -> p n c", c=VSTRIDE)[:, :, 128:129], 1.0),
                [], inc="VV")

        def a2_evac(tc, m, bank, tmark):
            hh, role = m // 3, m % 3
            cols = slice(tc * 512, (tc + 1) * 512)
            if role == 0:
                P.add("scalar", lambda e, hh=hh, cols=cols, bank=bank: e.activation(
                    q2[hh][:, cols], bank[:, 0:512], AF.Copy), [("TT", tmark)], inc="SC")
                return ("SC", P.c("SC"))
            if role == 1:
                P.add("scalar", lambda e, hh=hh, cols=cols, bank=bank: e.activation(
                    k2[hh][:, cols], bank[:, 0:512], AF.Copy), [("TT", tmark)], inc="SC")
                return ("SC", P.c("SC"))
            P.add("vector", lambda e, bank=bank: e.tensor_copy(vtmp2[:, :], bank[:, 0:512]),
                  [("TT", tmark), ("TT", t_tr_prev[0])], inc="VV")
            vm = P.c("VV")
            for j in range(4):
                P.add("tensor", lambda e, j=j: e.transpose(
                    psT[:, j * 128:(j + 1) * 128], vtmp2[:, j * 128:(j + 1) * 128], id16[:, :]),
                    [("VV", vm), ("VV", v_tr_prev[0])] if j == 0 else [],
                    inc="TT" if j == 3 else None)
            ttm = P.c("TT")
            t_tr_prev[0] = ttm
            for j in range(4):
                bidx = 4 * tc + j
                P.add("vector", lambda e, hh=hh, j=j, bidx=bidx: e.tensor_copy(
                    v2[hh][:, bidx * VSTRIDE:bidx * VSTRIDE + 128], psT[:, j * 128:(j + 1) * 128]),
                    [("TT", ttm)] if j == 0 else [], inc="VV" if j == 3 else None)
            v_tr_prev[0] = P.c("VV")
            return ("VV", v_tr_prev[0])

        emit_qkv_produce("a2", d_ag2o, ("CC3a", "CC3b", "CC3c", "CC3d"), w216, wd_w2, 6, a2_evac, xst)
        w8stack.close()

        P.barrier()
        pt2 = self.sb("pt2", [128, 4 * 1024], BF16, a2stack)
        out20 = self.sb("out20", [128, S], BF16, a2stack)
        out21 = self.sb("out21", [128, S], BF16, a2stack)
        emit_attention(q2[0], k2[0], v2[0], out20, pt2, mixer=False, means=None)
        stage_a2a(out20, d_a2a2i[0], d_a2a2o[0], "CC4a", "SG3")
        emit_attention(q2[1], k2[1], v2[1], out21, pt2, mixer=False, means=None)
        stage_a2a(out21, d_a2a2i[1], d_a2a2o[1], "CC4b", "SG4")
        a2stack.close()

        P.barrier()
        ph10stack = ExitStack()
        att2 = self.sb("att2", [128, KC * T], BF16, ph10stack)
        P.dma("gpsimd", lambda e: e.dma_start(
            out=wota16[:, :].rearrange("p (dc r) -> p dc r", dc=KC),
            in_=d_wotao.rearrange("dc p r -> p dc r")), waits=[("CC5", 1)], sem="WT")
        wt_ld = P.c("WT")
        for r in range(NCORE):
            P.dma("sync", lambda e, r=r: e.dma_start(
                out=att2[:, (2 * r) * T:(2 * r + 1) * T], in_=d_a2a2o[0][r]),
                waits=[("CC4a", 1)] if r == 0 else [], sem="YA")
        ld_att2A = ("YA", P.c("YA"))
        for r in range(NCORE):
            P.dma("sync", lambda e, r=r: e.dma_start(
                out=att2[:, (2 * r + 1) * T:(2 * r + 2) * T], in_=d_a2a2o[1][r]),
                waits=[("CC4b", 1)] if r == 0 else [], sem="YB")
        ld_att2B = ("YB", P.c("YB"))
        dco = list(range(0, KC, 2)) + list(range(1, KC, 2))
        for h in range(2):
            bank = (psA, psB)[h]
            for di, dc in enumerate(dco):
                wv = []
                if di == 0:
                    wv = [ld_att2A, ("WT", wt_ld)]
                elif di == 8:
                    wv = [ld_att2B]
                P.add("tensor", lambda e, h=h, dc=dc, di=di, bank=bank: e.matmul(
                    bank[0:16, 0:512], wota16[:, dc * 16:(dc + 1) * 16],
                    att2[:, dc * T + h * 512: dc * T + (h + 1) * 512],
                    start=(di == 0), stop=(di == KC - 1)),
                    wv, inc="TT" if di == KC - 1 else None)
            tmark = P.c("TT")
            P.add("scalar", lambda e, h=h, bank=bank: e.activation(
                m1_32[0:16, h * 512:(h + 1) * 512], bank[0:16, 0:512], AF.Tanh),
                [("TT", tmark)], inc="SC")
        s_tanh = P.c("SC")
        for h in range(2):
            bank = (psE, psF)[h]
            for dc in range(KC):
                P.add("tensor", lambda e, h=h, dc=dc, bank=bank: e.matmul(
                    bank[0:16, 0:512], a16[:, dc * 16:(dc + 1) * 16],
                    h16[:, dc * T + h * 512: dc * T + (h + 1) * 512],
                    start=(dc == 0), stop=(dc == KC - 1)), [],
                    inc="TT" if dc == KC - 1 else None)
            tmark = P.c("TT")
            P.add("vector", lambda e, h=h, bank=bank: e.scalar_tensor_tensor(
                modhA[0:16, h * 512:(h + 1) * 512], bank[0:16, 0:512], 0.1,
                m1_32[0:16, h * 512:(h + 1) * 512], ALU.mult, ALU.mult),
                [("TT", tmark), ("SC", s_tanh)], inc="VV")
        v_modhA = P.c("VV")
        P.tap("modhA", modhA[0:16, :], [16, T], BF16)
        ph10stack.close()

        P.barrier()
        gustack = ExitStack()
        gu = self.sb("gu", [128, FC * T], BF16, gustack, side="right")
        ph11stack = ExitStack()
        winb = [self.sb("winb%d" % i, [128, KC * 128], BF16, ph11stack) for i in range(2)]
        wupb = [self.sb("wupb%d" % i, [128, KC * 128], BF16, ph11stack) for i in range(2)]
        sil32 = [self.sb("sil32_%d" % i, [128, 512], F32, ph11stack) for i in range(2)]
        b16s = [self.sb("b16s%d" % i, [16, 16 * 128], BF16, ph11stack) for i in range(2)]
        b_marks = {}
        f_tm = {}
        f_sil = {}
        f_guv = {}

        def load_bstrip(g4):
            us = "U%d" % (2 + g4 % 2)
            P.dma("gpsimd", lambda e, g4=g4: e.dma_start(
                out=b16s[g4 % 2][0:16, :].rearrange("p (fc c) -> p fc c", fc=16),
                in_=d_bmat[g4 * 16:(g4 + 1) * 16].rearrange("fc p c -> p fc c")),
                waits=[("TT", f_tm.get((g4 - 1) * 16 - 1, 0))], sem=us)
            b_marks[g4] = (us, P.c(us))

        load_bstrip(0)
        load_bstrip(1)
        for fc in range(FC):
            if fc >= 16 and fc % 16 == 0 and (fc // 16 + 1) < 4:
                load_bstrip(fc // 16 + 1)
            us = "U%d" % (fc % 2)
            P.dma("gpsimd", lambda e, fc=fc: e.dma_start(
                out=winb[fc % 2][:, :].rearrange("p (kc c) -> p kc c", kc=KC),
                in_=d_win[fc].rearrange("kc p c -> p kc c")),
                waits=[("TT", f_tm.get(fc - 2, 0))], sem=us)
            P.dma("gpsimd", lambda e, fc=fc: e.dma_start(
                out=wupb[fc % 2][:, :].rearrange("p (kc c) -> p kc c", kc=KC),
                in_=d_wup[fc].rearrange("kc p c -> p kc c")), sem=us)
            wdm = (us, P.c(us))
            g4 = fc // 16
            for h in range(2):
                gbank = (psA, psB)[h]
                ubank = (psC, psD)[h]
                for kc in range(KC):
                    wv = [wdm, b_marks[g4],
                          ("SC", f_sil.get((fc - 1, h), 0)), ("VV", v_modhA)] if kc == 0 else []
                    P.add("tensor", lambda e, fc=fc, h=h, kc=kc, gbank=gbank: e.matmul(
                        gbank[:, 0:512], winb[fc % 2][:, kc * 128:(kc + 1) * 128],
                        h16[:, kc * T + h * 512: kc * T + (h + 1) * 512],
                        start=(kc == 0), stop=False), wv)
                P.add("tensor", lambda e, fc=fc, h=h, gbank=gbank, g4=g4: e.matmul(
                    gbank[:, 0:512], b16s[g4 % 2][0:16, (fc % 16) * 128:(fc % 16 + 1) * 128],
                    modhA[0:16, h * 512:(h + 1) * 512],
                    start=False, stop=True), [], inc="TT")
                tg = P.c("TT")
                for kc in range(KC):
                    wv = [("VV", f_guv.get((fc - 1, h), 0))] if kc == 0 else []
                    P.add("tensor", lambda e, fc=fc, h=h, kc=kc, ubank=ubank: e.matmul(
                        ubank[:, 0:512], wupb[fc % 2][:, kc * 128:(kc + 1) * 128],
                        h16[:, kc * T + h * 512: kc * T + (h + 1) * 512],
                        start=(kc == 0), stop=(kc == KC - 1)), wv,
                        inc="TT" if kc == KC - 1 else None)
                tu = P.c("TT")
                if h == 1:
                    f_tm[fc] = tu
                P.add("scalar", lambda e, h=h, gbank=gbank: e.activation(
                    sil32[h][:, :], gbank[:, 0:512], AF.Silu), [("TT", tg)], inc="SC")
                f_sil[(fc, h)] = P.c("SC")
                P.add("vector", lambda e, fc=fc, h=h, ubank=ubank: e.tensor_tensor(
                    gu[:, fc * T + h * 512: fc * T + (h + 1) * 512], ubank[:, 0:512],
                    sil32[h][:, :], ALU.mult), [("SC", f_sil[(fc, h)]), ("TT", tu)], inc="VV")
                f_guv[(fc, h)] = P.c("VV")
        ph11stack.close()
        h16stack.close()

        P.barrier()
        ph12stack = ExitStack()
        wdnb = [self.sb("wdnb%d" % i, [128, FC * 128], BF16, ph12stack) for i in range(2)]
        x1t = [self.sb("x1t%d" % i, [128, T], F32, ph12stack) for i in range(2)]
        hst = [self.sb("hst%d" % i, [128, T], F32, ph12stack) for i in range(2)]
        outst = [self.sb("outst%d" % i, [128, T], F32, ph12stack) for i in range(2)]
        d_tm = {}
        d_ev = {}
        st_marks = {}
        for ec in range(KC):
            us = "U%d" % (ec % 2)
            qs = "Q%d" % (ec % 2)
            qo = "Q%d" % (2 + ec % 2)
            P.dma("gpsimd", lambda e, ec=ec: e.dma_start(
                out=wdnb[ec % 2][:, :].rearrange("p (fc c) -> p fc c", fc=FC),
                in_=d_wdn[ec].rearrange("fc p c -> p fc c")),
                waits=[("TT", d_tm.get(ec - 2, 0))], sem=us)
            wdm = (us, P.c(us))
            P.dma("sync", lambda e, ec=ec: e.dma_start(out=x1t[ec % 2][:, :], in_=d_x1sp[ec]),
                  waits=[("VV", d_ev.get(ec - 2, 0))], sem=qs)
            P.dma("sync", lambda e, ec=ec: e.dma_start(out=hst[ec % 2][:, :], in_=d_hssp[ec]),
                  sem=qs)
            ldm = (qs, P.c(qs))
            for h in range(2):
                bank = (psA, psB)[h]
                for fc in range(FC):
                    wv = [wdm, ("VV", d_ev.get((ec - 1, h), 0))] if fc == 0 else []
                    P.add("tensor", lambda e, ec=ec, h=h, fc=fc, bank=bank: e.matmul(
                        bank[:, 0:512], wdnb[ec % 2][:, fc * 128:(fc + 1) * 128],
                        gu[:, fc * T + h * 512: fc * T + (h + 1) * 512],
                        start=(fc == 0), stop=(fc == FC - 1)), wv,
                        inc="TT" if fc == FC - 1 else None)
                tmark = P.c("TT")
                if h == 1:
                    d_tm[ec] = tmark
                P.add("vector", lambda e, ec=ec, h=h, bank=bank: e.tensor_tensor(
                    tmp32[:, h * 512:(h + 1) * 512], bank[:, 0:512],
                    x1t[ec % 2][:, h * 512:(h + 1) * 512], ALU.add),
                    [("TT", tmark), ldm, (qo, st_marks.get(ec - 2, 0))], inc="VV")
                P.add("vector", lambda e, ec=ec, h=h: e.tensor_tensor(
                    outst[ec % 2][:, h * 512:(h + 1) * 512], tmp32[:, h * 512:(h + 1) * 512],
                    hst[ec % 2][:, h * 512:(h + 1) * 512], ALU.add), [], inc="VV")
                d_ev[(ec, h)] = P.c("VV")
            d_ev[ec] = P.c("VV")
            P.dma("sync", lambda e, ec=ec: e.dma_start(out=d_out[ec], in_=outst[ec % 2][:, :]),
                  waits=[("VV", d_ev[ec])], sem=qo)
            st_marks[ec] = P.c(qo)
        P.add("sync", lambda e: e.nop(),
              waits=[("Q2", P.c("Q2")), ("Q3", P.c("Q3"))])
        P.add("gpsimd", lambda e: e.nop(),
              waits=[("Q2", P.c("Q2")), ("Q3", P.c("Q3"))])

        self.emit_and_finalize()
        ph12stack.close()
        gustack.close()
        self.ctx.close()
        return nc


def build_program(debug_taps=()):
    b = Builder(debug_taps)
    b.build()
    return b.nc


def _host_stage(inputs):
    x = np.asarray(inputs["L0_attn_x"])
    wq = np.asarray(inputs["L0_attn_wq"])
    wk = np.asarray(inputs["L0_attn_wk"])
    wv = np.asarray(inputs["L0_attn_wv"])
    wo = np.asarray(inputs["L0_attn_wo"])
    ln1w = np.asarray(inputs["L0_ffn_ln1_w"])
    ln2w = np.asarray(inputs["L0_ffn_ln2_w"])
    Wq2 = np.asarray(inputs["L0_ffn_Wq"])
    Wk2 = np.asarray(inputs["L0_ffn_Wk"])
    Wv2 = np.asarray(inputs["L0_ffn_Wv"])
    Wo2 = np.asarray(inputs["L0_ffn_Wo"])
    A = np.asarray(inputs["L0_ffn_A"])
    B = np.asarray(inputs["L0_ffn_B"])
    Win = np.asarray(inputs["L0_ffn_W_in_base"])
    Wup = np.asarray(inputs["L0_ffn_W_up_base"])
    Wdn = np.asarray(inputs["L0_ffn_W_down"])

    perm = np.concatenate([np.arange(0, 128, 2), np.arange(1, 128, 2)])
    inv_freq = 1.0 / (10000.0 ** (np.arange(0, 128, 2, dtype=np.float32) / 128.0))
    ang = np.arange(S, dtype=np.float32)[:, None] * inv_freq[None, :]
    cossin = np.empty((128, S), np.float32)
    cossin[0:64] = np.cos(ang).T
    cossin[64:128] = np.sin(ang).T

    wot = np.ascontiguousarray(wo.reshape(KC, 128, KC, 128).transpose(2, 0, 1, 3)).astype(np.float32)
    winT = np.ascontiguousarray(Win.reshape(FC, 128, KC, 128).transpose(0, 2, 3, 1)).astype(np.float32)
    wupT = np.ascontiguousarray(Wup.reshape(FC, 128, KC, 128).transpose(0, 2, 3, 1)).astype(np.float32)
    wdnT = np.ascontiguousarray(Wdn.reshape(KC, 128, FC, 128).transpose(0, 2, 3, 1)).astype(np.float32)
    afull = np.ascontiguousarray(A.reshape(KC, 128, 16)).astype(np.float32)
    bmat = np.ascontiguousarray(B.reshape(16, FC, 128).transpose(1, 0, 2)).astype(np.float32)
    lnw_arr = np.empty((128, 2 * KC), np.float32)
    lnw_arr[:, :KC] = ln1w.reshape(KC, 128).T
    lnw_arr[:, KC:] = ln2w.reshape(KC, 128).T
    consts = np.zeros((128, 4), np.float32)
    consts[:, 0] = 1.0 / 2048.0
    consts[:, 1] = 1.0
    consts[:, 2] = 1e-5
    consts[:, 3] = 1e-6
    crow = np.ones((1, 256), np.float32)
    ident = np.eye(128, dtype=np.float32)
    ii = np.arange(128)
    tril01 = (ii[:, None] > ii[None, :]).astype(np.float32)
    trilneg = (tril01 * NEG_CAUSAL).astype(np.float32)

    in_maps = []
    for c in range(NCORE):
        sl = slice(c * T, (c + 1) * T)
        xTc = np.ascontiguousarray(x[0, sl, :].T).reshape(KC, 128, T).astype(np.float32)
        q0 = wq[:, (2 * c) * 128:(2 * c + 1) * 128][:, perm]
        q1 = wq[:, (2 * c + 1) * 128:(2 * c + 2) * 128][:, perm]
        kk = wk[:, c * 128:(c + 1) * 128][:, perm]
        vv = wv[:, c * 128:(c + 1) * 128]
        mw = np.stack([q0, q1, kk, vv], axis=1)
        mwqkv = np.ascontiguousarray(
            mw.reshape(KC, 128, 4, 128).transpose(0, 2, 1, 3)).astype(np.float32)
        h0 = slice((2 * c) * 128, (2 * c + 1) * 128)
        h1 = slice((2 * c + 1) * 128, (2 * c + 2) * 128)
        w2 = np.stack([Wq2[h0].T, Wk2[h0].T, Wv2[h0].T,
                       Wq2[h1].T, Wk2[h1].T, Wv2[h1].T], axis=1)
        w2qkv = np.ascontiguousarray(
            w2.reshape(KC, 128, 6, 128).transpose(0, 2, 1, 3)).astype(np.float32)
        esl = slice(c * 256, (c + 1) * 256)
        wo2s = np.ascontiguousarray(Wo2[esl].reshape(2, 128, KC, 128).transpose(0, 2, 1, 3)).astype(np.float32)
        a2s = np.ascontiguousarray(A[esl].reshape(2, 128, 16)).astype(np.float32)
        in_maps.append({
            "xT": xTc, "cossin": cossin,
            "mwqkv": mwqkv, "wot": wot, "w2qkv": w2qkv,
            "wo2s": wo2s, "a2s": a2s, "afull": afull, "bmat": bmat,
            "lnw": lnw_arr, "winT": winT, "wupT": wupT, "wdnT": wdnT,
            "consts": consts, "crow": crow, "ident": ident,
            "tril01": tril01, "trilneg": trilneg,
        })
    return in_maps


def kernel(**inputs):
    in_maps = _host_stage(inputs)
    nc = build_program()
    res = run_bass_kernel_spmd(nc, in_maps, core_ids=list(range(NCORE)))
    full = np.empty((1, S, D), np.float32)
    for c in range(NCORE):
        o = np.asarray(res.results[c]["out"], np.float32)
        full[0, c * T:(c + 1) * T, :] = o.reshape(D, T).T
    return full
